# version 04
# baseline (speedup 1.0000x reference)
import numpy as np
import ml_dtypes
from contextlib import ExitStack

import concourse.bass as bass
import concourse.mybir as mybir
from concourse.bass_utils import run_bass_kernel_spmd

F32 = mybir.dt.float32
BF16 = mybir.dt.bfloat16
I32 = mybir.dt.int32
AF = mybir.ActivationFunctionType
ALU = mybir.AluOpType
BF = ml_dtypes.bfloat16

D = 2048
TC = 2048
ST = 1024
NDD = D // 128
NE = 16
IH = 1408
NIC = IH // 128
SHID = 5632
NJC = SHID // 128
QKVF = 3 * D
NEC = QKVF // 128
S = 8192
HD = 128
SCALE = 1.0 / float(np.sqrt(HD))
BIGNEG = -1.0e9
EPS = 1e-5

ENGS = ("sy", "gp", "te", "ve", "sc")


class Sem:
    __slots__ = ("h", "n", "name")

    def __init__(self, es, nc, name):
        self.h = es.enter_context(nc.semaphore(name))
        self.n = 0
        self.name = name


class Eng:
    def __init__(self, name):
        self.name = name
        self.wm = {}
        self.e = None

    def wait(self, tok):
        if tok is None:
            return
        sem, n = tok
        if n <= 0:
            return
        if self.wm.get(sem.name, 0) < n:
            self.e.wait_ge(sem.h, n)
            self.wm[sem.name] = n


class Slot:
    __slots__ = ("ap", "free", "ready", "sem")

    def __init__(self, ap, sem=None):
        self.ap = ap
        self.free = []
        self.ready = None
        self.sem = sem


class Bank(Slot):
    pass


class Pool:
    def __init__(self, banks):
        self.banks = banks
        self.i = 0

    def acquire(self, ctx):
        b = self.banks[self.i % len(self.banks)]
        self.i += 1
        ctx.waits("te", b.free)
        b.free = []
        return b


class Ctx:
    def __init__(self, B, mode):
        self.B = B
        self.mode = mode
        self.entry = None

    def wait(self, eng, tok):
        if self.mode == eng:
            self.B.E[eng].wait(tok)

    def waits(self, eng, toks):
        for t in toks:
            self.wait(eng, t)

    def dma(self, q, thunk, sem=None):
        if sem is None:
            sem = self.B.SY if q == "sy" else self.B.SW
        if self.mode == q:
            kw = thunk()
            self.B.E[q].e.dma_start(**kw).then_inc(sem.h, 16)
        sem.n += 16
        return (sem, sem.n)

    def op(self, eng, fn, sem=None):
        instr = None
        if self.mode == eng:
            instr = fn(self.B.E[eng].e)
        if sem is not None:
            if instr is not None:
                instr.then_inc(sem.h, 1)
            sem.n += 1
            return (sem, sem.n)
        return None

    def run(self, eng, fn):
        if self.mode == eng:
            return fn(self.B.E[eng].e)
        return None

    def vop(self, fn):
        tok = self.op("ve", fn, self.B.VS)
        self.wait("ve", tok)
        return tok

    def barrier(self, toks):
        for t in toks:
            self.B.E[self.mode].wait(t)


class Builder:
    def __init__(self, nc, es):
        self.nc = nc
        self.es = es
        self.SY = Sem(es, nc, "SY")
        self.SW = Sem(es, nc, "SW")
        self.CC = Sem(es, nc, "CC")
        self.TS = Sem(es, nc, "TS")
        self.VS = Sem(es, nc, "VS")
        self.SS = Sem(es, nc, "SS")
        self.sems = [self.SY, self.SW, self.CC, self.TS, self.VS, self.SS]
        self.E = {n: Eng(n) for n in ENGS}

    def new_sem(self, name):
        s = Sem(self.es, self.nc, name)
        self.sems.append(s)
        return s

    def toks(self):
        return [(s, s.n) for s in self.sems]

    def phase(self, blk, body):
        entry = self.toks()
        snap = [s.n for s in self.sems]
        final = None
        for mode in ENGS:
            for s, n0 in zip(self.sems, snap):
                s.n = n0
            ctx = Ctx(self, mode)
            ctx.entry = entry

            def run_body(e, mode=mode, ctx=ctx):
                self.E[mode].e = e
                body(ctx)

            getattr_map = {"sy": blk.sync, "gp": blk.gpsimd, "te": blk.tensor,
                           "ve": blk.vector, "sc": blk.scalar}
            getattr_map[mode](run_body)
            end = [s.n for s in self.sems]
            if final is None:
                final = end
            else:
                assert final == end, ("phase replay divergence", final, end)
        for s, n in zip(self.sems, final):
            s.n = n


def build_program(dumps=()):
    nc = bass.Bass("TRN2", target_bir_lowering=False, debug=False)

    xT = nc.dram_tensor("xT", [D, TC], F32, kind="ExternalInput").ap()
    wattnT = nc.dram_tensor("wattnT", [NEC, 128, D], F32, kind="ExternalInput").ap()
    battnC = nc.dram_tensor("battnC", [128, NEC], F32, kind="ExternalInput").ap()
    wprojT = nc.dram_tensor("wprojT", [NDD, 128, D], F32, kind="ExternalInput").ap()
    bprojC = nc.dram_tensor("bprojC", [128, NDD], F32, kind="ExternalInput").ap()
    gate_sg = nc.dram_tensor("gate_sg", [D, 17], F32, kind="ExternalInput").ap()
    w1 = nc.dram_tensor("w1", [NE, NIC, 128, D], F32, kind="ExternalInput").ap()
    w3 = nc.dram_tensor("w3", [NE, NIC, 128, D], F32, kind="ExternalInput").ap()
    w2 = nc.dram_tensor("w2", [NE, NDD, 128, IH], F32, kind="ExternalInput").ap()
    sw1 = nc.dram_tensor("sw1", [NJC, 128, D], F32, kind="ExternalInput").ap()
    sw3 = nc.dram_tensor("sw3", [NJC, 128, D], F32, kind="ExternalInput").ap()
    sw2 = nc.dram_tensor("sw2", [4, NDD, 128, IH], F32, kind="ExternalInput").ap()
    identbf = nc.dram_tensor("identbf", [128, 128], BF16, kind="ExternalInput").ap()
    identf = nc.dram_tensor("identf", [128, 128], F32, kind="ExternalInput").ap()
    onescol = nc.dram_tensor("onescol", [128, 1], F32, kind="ExternalInput").ap()
    onesrow = nc.dram_tensor("onesrow", [1, 128], F32, kind="ExternalInput").ap()
    cmaskD = nc.dram_tensor("cmaskD", [128, 128], F32, kind="ExternalInput").ap()
    meta = nc.dram_tensor("meta", [1, 16], I32, kind="ExternalInput").ap()

    out = nc.dram_tensor("out", [TC, D], F32, kind="ExternalOutput").ap()

    y1H = nc.dram_tensor("y1H", [D, TC], F32, kind="Internal").ap()
    y2H = nc.dram_tensor("y2H", [D, TC], F32, kind="Internal").ap()
    ginA = nc.dram_tensor("ginA", [QKVF, TC], BF16, kind="Internal").ap()
    goutA = nc.dram_tensor("goutA", [4 * QKVF, TC], BF16, kind="Internal").ap()
    ginB = nc.dram_tensor("ginB", [4 * HD, S], BF16, kind="Internal").ap()
    goutB = nc.dram_tensor("goutB", [D, S], BF16, kind="Internal").ap()
    kvqH = nc.dram_tensor("kvqH", [4, 384, TC], BF16, kind="Internal").ap()
    ctxH = nc.dram_tensor("ctxH", [D, TC], BF16, kind="Internal").ap()

    dump_specs = []
    srcs = {"y1H": y1H, "y2H": y2H, "ginA": ginA, "goutA": goutA,
            "ginB": ginB, "goutB": goutB, "kvqH": kvqH, "ctxH": ctxH}
    for name in dumps:
        sap = srcs[name]
        dap = nc.dram_tensor("dump_" + name, list(sap.tensor.shape),
                             sap.tensor.dtype, kind="ExternalOutput").ap()
        dump_specs.append((dap, sap))

    es = ExitStack()
    B = Builder(nc, es)
    SY, SW, CC, TS, VS, SS = B.SY, B.SW, B.CC, B.TS, B.VS, B.SS

    def sbuf(name, shape, dtype):
        return es.enter_context(nc.sbuf_tensor(name, shape, dtype)).ap()

    identbf_sb = sbuf("c_identbf", [128, 128], BF16)
    identf_sb = sbuf("c_identf", [128, 128], F32)
    onescol_sb = sbuf("c_onescol", [128, 1], F32)
    onesrow_sb = sbuf("c_onesrow", [1, 128], F32)
    cmask_sb = sbuf("c_cmask", [128, 128], F32)
    battn_sb = sbuf("c_battn", [128, NEC], F32)
    bproj_sb = sbuf("c_bproj", [128, NDD], F32)
    gsg_sb = sbuf("c_gsg", [128, NDD * 17], F32)
    meta_sb = sbuf("c_meta", [1, 16], I32)
    eps_sb = sbuf("c_eps", [1, 1], F32)

    FBaps = [es.enter_context(nc.psum_tensor("psf%d" % i, [128, 512], F32)).ap()
             for i in range(6)]
    GBaps = [es.enter_context(nc.psum_tensor("psg%d" % i, [128, 1024], BF16)).ap()
             for i in range(2)]

    def emit_ln(ctx, st, fpool, ysrc, tok0, ntok, zt_ap, zt_stride, zt_off, gate):
        y_sl = st["y_sl"]
        sc2 = st["sc2"]
        z32 = st["z32"]
        ztmp = st["ztmp"]
        row_m, row_r, row_t = st["row_m"], st["row_r"], st["row_t"]
        mb_sb, rb_sb = st["mb"], st["rb"]
        rf = st.setdefault("row_free", {"m": [], "r": [], "t": []})
        zt_free = st.get("zt_free", [])
        zt_done = None
        ntt = ntok // 512
        for tt in range(ntt):
            tabs = tok0 + tt * 512
            tloc = zt_off + tt * 512
            b_sum = fpool.acquire(ctx)
            b_sq = fpool.acquire(ctx)
            sum_tok = sumsq_tok = None
            for dd in range(NDD):
                sl = y_sl[dd % 4]
                ctx.waits("sy", sl.free)
                sl.free = []
                sl.ready = ctx.dma("sy", lambda dd=dd, tabs=tabs, sl=sl: dict(
                    out=sl.ap[:], in_=ysrc[dd * 128:(dd + 1) * 128, tabs:tabs + 512]),
                    sem=sl.sem)
                sq = sc2[dd % 2]
                ctx.wait("sc", sl.ready)
                ctx.waits("sc", sq.free)
                sq.free = []
                sq.ready = ctx.op("sc", lambda e, sq=sq, sl=sl: e.square(sq.ap[:], sl.ap[:]), SS)
                ctx.wait("te", sl.ready)
                t1 = ctx.op("te", lambda e, sl=sl, dd=dd, b=b_sum: e.matmul(
                    b.ap[0:1, :], onescol_sb, sl.ap[:],
                    start=(dd == 0), stop=(dd == NDD - 1)), TS)
                ctx.wait("te", sq.ready)
                t2 = ctx.op("te", lambda e, sq=sq, dd=dd, b=b_sq: e.matmul(
                    b.ap[0:1, :], onescol_sb, sq.ap[:],
                    start=(dd == 0), stop=(dd == NDD - 1)), TS)
                sl.free = [t1, sq.ready]
                sq.free = [t2]
                sum_tok, sumsq_tok = t1, t2
            ctx.wait("sc", sum_tok)
            ctx.waits("sc", rf["m"])
            rf["m"] = []
            mtok = ctx.op("sc", lambda e: e.mul(row_m[0:1, :], b_sum.ap[0:1, :], 1.0 / D), SS)
            ctx.wait("sc", sumsq_tok)
            ctx.waits("sc", rf["r"])
            rf["r"] = []
            qtok = ctx.op("sc", lambda e: e.mul(row_r[0:1, :], b_sq.ap[0:1, :], 1.0 / D), SS)
            b_sum.free = [mtok]
            b_sq.free = [qtok]
            ctx.wait("ve", qtok)
            ctx.waits("ve", rf["t"])
            rf["t"] = []
            ctx.vop(lambda e: e.tensor_mul(row_t[0:1, :], row_m[0:1, :], row_m[0:1, :]))
            vtok = ctx.vop(lambda e: e.tensor_sub(row_r[0:1, :], row_r[0:1, :],
                                                  row_t[0:1, :]))
            ctx.wait("sc", vtok)
            lntok = ctx.op("sc", lambda e: e.activation(row_t[0:1, :], row_r[0:1, :], AF.Ln,
                                                        bias=eps_sb[0:1, 0:1]), SS)
            ctx.wait("sc", lntok)
            rtok = ctx.op("sc", lambda e: e.activation(row_r[0:1, :], row_t[0:1, :], AF.Exp,
                                                       scale=-0.5), SS)
            b_mb = fpool.acquire(ctx)
            b_rb = fpool.acquire(ctx)
            ctx.wait("te", mtok)
            tmb = ctx.op("te", lambda e: e.matmul(b_mb.ap[:, :], onesrow_sb, row_m[0:1, :],
                                                  start=True, stop=True), TS)
            ctx.wait("te", rtok)
            trb = ctx.op("te", lambda e: e.matmul(b_rb.ap[:, :], onesrow_sb, row_r[0:1, :],
                                                  start=True, stop=True), TS)
            ctx.wait("ve", tmb)
            ctx.op("ve", lambda e: e.tensor_copy(mb_sb, b_mb.ap[:, :]))
            ctx.wait("ve", trb)
            mrtok = ctx.op("ve", lambda e: e.tensor_copy(rb_sb, b_rb.ap[:, :]), VS)
            b_mb.free = [mrtok]
            b_rb.free = [mrtok]
            rf["m"] = [vtok, tmb]
            rf["r"] = [trb]
            rf["t"] = [rtok]
            if gate:
                gbank = [fpool.acquire(ctx) for _ in range(4)]
            gtok = None
            for dd in range(NDD):
                sl = y_sl[dd % 4]
                ctx.waits("sy", sl.free)
                sl.free = []
                sl.ready = ctx.dma("sy", lambda dd=dd, tabs=tabs, sl=sl: dict(
                    out=sl.ap[:], in_=ysrc[dd * 128:(dd + 1) * 128, tabs:tabs + 512]),
                    sem=sl.sem)
                z = z32[dd % 2]
                ctx.wait("ve", sl.ready)
                ctx.wait("ve", mrtok)
                ctx.waits("ve", z.free)
                z.free = []
                ctx.waits("ve", zt_free)
                ctx.op("ve", lambda e, sl=sl: e.tensor_sub(ztmp, sl.ap[:], mb_sb))
                ctx.op("ve", lambda e, z=z: e.tensor_mul(z.ap[:], ztmp, rb_sb))
                ztok = ctx.op("ve", lambda e, z=z, dd=dd, tloc=tloc: e.tensor_copy(
                    zt_ap[:, dd * zt_stride + tloc:dd * zt_stride + tloc + 512],
                    z.ap[:]), VS)
                sl.free = [ztok]
                z.free = [ztok]
                zt_done = ztok
                if gate:
                    ctx.wait("te", ztok)
                    for g in range(4):
                        gtok = ctx.op("te", lambda e, g=g, dd=dd, z=z: e.matmul(
                            gbank[g].ap[:, 0:17], z.ap[:, g * 128:(g + 1) * 128],
                            gsg_sb[:, dd * 17:(dd + 1) * 17],
                            start=(dd == 0), stop=(dd == NDD - 1)),
                            TS if g == 3 else None)
                    z.free.append(gtok)
            if gate:
                emit_gate_finalize(ctx, st, fpool, gbank, gtok, tt * 512)
        st["zt_free"] = []
        return zt_done

    def emit_gate_finalize(ctx, st, fpool, gbank, gdone, tloc):
        gl, pp, pn, cwv, msk = st["gl"], st["pp"], st["pn"], st["cw"], st["msk"]
        col, col2 = st["col"], st["col2"]
        sg_cols, cwT = st["sg"], st["cwT"]
        vt3 = None
        for g in range(4):
            gb = gbank[g]
            ctx.wait("ve", gdone)
            ctx.vop(lambda e, gb=gb: e.tensor_copy(gl, gb.ap[:, 0:17]))
            ctx.vop(lambda e, g=g: e.tensor_copy(
                sg_cols[:, (tloc // 128) + g:(tloc // 128) + g + 1], gl[:, 16:17]))
            vt = ctx.vop(lambda e: e.reduce_max(col, gl[:, 0:16],
                                                axis=mybir.AxisListType.X))
            gb.free = [vt]
            ctx.wait("sc", vt)
            ntok = ctx.op("sc", lambda e: e.mul(col2, col, -1.0), SS)
            ctx.wait("sc", ntok)
            stok = ctx.op("sc", lambda e: e.activation(pp, gl[:, 0:16], AF.Exp,
                                                       bias=col2, accum_out=col), SS)
            ctx.wait("ve", stok)
            ctx.vop(lambda e: e.reciprocal(col2, col))
            ctx.vop(lambda e: e.tensor_scalar(pn, pp, col2, None, ALU.mult))
            cur = pn
            for rnd in range(3):
                ctx.vop(lambda e, cur=cur: e.reduce_max(col, cur,
                                                        axis=mybir.AxisListType.X))
                ctx.vop(lambda e, cur=cur: e.tensor_scalar(
                    msk, cur, col, None, ALU.is_ge))
                dst = pp if rnd % 2 == 0 else cwv
                ctx.vop(lambda e, cur=cur, dst=dst: e.scalar_tensor_tensor(
                    dst, msk, BIGNEG, cur, ALU.mult, ALU.add))
                cur = dst
            ctx.vop(lambda e, cur=cur: e.reduce_max(col, cur,
                                                    axis=mybir.AxisListType.X))
            ctx.vop(lambda e: e.tensor_scalar(msk, pn, col, None, ALU.is_ge))
            vt2 = ctx.vop(lambda e: e.tensor_mul(cwv, msk, pn))
            bT = fpool.acquire(ctx)
            ctx.wait("te", vt2)
            ttk = ctx.op("te", lambda e, bT=bT: e.transpose(bT.ap[0:16, 0:128], cwv,
                                                            identf_sb), TS)
            ctx.wait("ve", ttk)
            vt3 = ctx.op("ve", lambda e, g=g, tloc=tloc, bT=bT: e.tensor_copy(
                cwT[0:16, tloc + g * 128:tloc + (g + 1) * 128],
                bT.ap[0:16, 0:128]), VS)
            bT.free = [vt3]
        st["cwT_done"] = vt3

    def emit_sig(ctx, st, fpool):
        sg_cols, sig, sigrow, sigb = st["sg"], st["sig"], st["sigrow"], st["sigb"]
        ctx.wait("sc", st["cwT_done"])
        s1 = ctx.op("sc", lambda e: e.activation(sig, sg_cols, AF.Sigmoid), SS)
        rowtoks = []
        for h in range(2):
            bT = fpool.acquire(ctx)
            ctx.wait("te", s1)
            tts = None
            for g in range(4):
                gg = h * 4 + g
                tts = ctx.op("te", lambda e, bT=bT, g=g, gg=gg: e.transpose(
                    bT.ap[0:1, g * 128:(g + 1) * 128], sig[:, gg:gg + 1], identf_sb),
                    TS if g == 3 else None)
            ctx.wait("ve", tts)
            vt = ctx.op("ve", lambda e, bT=bT, h=h: e.tensor_copy(
                sigrow[0:1, h * 512:(h + 1) * 512], bT.ap[0:1, 0:512]), VS)
            bT.free = [vt]
            rowtoks.append(vt)
        sigb_tok = None
        for h in range(2):
            bb = fpool.acquire(ctx)
            ctx.wait("te", rowtoks[h])
            tb = ctx.op("te", lambda e, bb=bb, h=h: e.matmul(
                bb.ap[:, :], onesrow_sb, sigrow[0:1, h * 512:(h + 1) * 512],
                start=True, stop=True), TS)
            ctx.wait("ve", tb)
            sigb_tok = ctx.op("ve", lambda e, bb=bb, h=h: e.tensor_copy(
                sigb[:, h * 512:(h + 1) * 512], bb.ap[:, :]), VS)
            bb.free = [sigb_tok]
        st["sigb_done"] = sigb_tok

    def emit_unit(ctx, st, fpool, zt_ap, zt_done, w1src, w3src, w2src,
                  scale_ap, scale_tok, acc_init, he_state):
        W1, W3, W2S = st["W1"], st["W3"], st["W2S"]
        silu = st["sc2"]
        prod = st["ztmp"]
        he_ap = st["he"]
        acc_ap = st["acc"]
        ctx.waits("ve", he_state["free"])
        v_tok = None
        t_h3 = None
        for jc in range(NIC):
            s1 = W1[jc % 2]
            s3 = W3[jc % 2]
            ctx.waits("gp", s1.free)
            s1.free = []
            s1.ready = ctx.dma("gp", lambda s1=s1, jc=jc: dict(
                out=s1.ap, in_=w1src(jc)), sem=s1.sem)
            ctx.waits("gp", s3.free)
            s3.free = []
            s3.ready = ctx.dma("gp", lambda s3=s3, jc=jc: dict(
                out=s3.ap, in_=w3src(jc)), sem=s3.sem)
            bh1 = [fpool.acquire(ctx), fpool.acquire(ctx)]
            bh3 = [fpool.acquire(ctx), fpool.acquire(ctx)]
            ctx.wait("te", s1.ready)
            ctx.wait("te", s3.ready)
            ctx.wait("te", zt_done)
            t_h1 = None
            for dd in range(NDD):
                for th in range(2):
                    t_h1 = ctx.op("te", lambda e, dd=dd, th=th, s1=s1: e.matmul(
                        bh1[th].ap[:, :], s1.ap[:, dd * 128:(dd + 1) * 128],
                        zt_ap[:, dd * 1024 + th * 512:dd * 1024 + (th + 1) * 512],
                        start=(dd == 0), stop=(dd == NDD - 1)),
                        TS if (dd == NDD - 1 and th == 1) else None)
                for th in range(2):
                    t_h3 = ctx.op("te", lambda e, dd=dd, th=th, s3=s3: e.matmul(
                        bh3[th].ap[:, :], s3.ap[:, dd * 128:(dd + 1) * 128],
                        zt_ap[:, dd * 1024 + th * 512:dd * 1024 + (th + 1) * 512],
                        start=(dd == 0), stop=(dd == NDD - 1)),
                        TS if (dd == NDD - 1 and th == 1) else None)
            s1.free = [t_h1]
            s3.free = [t_h3]
            for th in range(2):
                sl = silu[th]
                ctx.wait("sc", t_h1)
                ctx.waits("sc", sl.free)
                sl.free = []
                s_tok = ctx.op("sc", lambda e, sl=sl, th=th: e.activation(
                    sl.ap[:], bh1[th].ap[:, :], AF.Silu), SS)
                bh1[th].free = [s_tok]
                ctx.wait("ve", s_tok)
                ctx.wait("ve", t_h3)
                ctx.wait("ve", scale_tok)
                ctx.op("ve", lambda e, sl=sl, th=th: e.tensor_mul(
                    prod, sl.ap[:], bh3[th].ap[:, :]))
                v_tok = ctx.op("ve", lambda e, jc=jc, th=th: e.tensor_mul(
                    he_ap[:, jc * 1024 + th * 512:jc * 1024 + (th + 1) * 512],
                    prod, scale_ap[:, th * 512:(th + 1) * 512]), VS)
                sl.free = [v_tok]
                bh3[th].free = [v_tok]
        he_ready = v_tok
        st["last_hidden"] = t_h3
        t_w2 = None
        a_tok = None
        for dd in range(NDD):
            s2 = W2S[dd % 2]
            ctx.waits("gp", s2.free)
            s2.free = []
            s2.ready = ctx.dma("gp", lambda s2=s2, dd=dd: dict(
                out=s2.ap, in_=w2src(dd)), sem=s2.sem)
            b0 = fpool.acquire(ctx)
            b1 = fpool.acquire(ctx)
            ctx.wait("te", s2.ready)
            ctx.wait("te", he_ready)
            for ic in range(NIC):
                for th, bb in ((0, b0), (1, b1)):
                    t_w2 = ctx.op("te", lambda e, ic=ic, th=th, bb=bb, s2=s2: e.matmul(
                        bb.ap[:, :], s2.ap[:, ic * 128:(ic + 1) * 128],
                        he_ap[:, ic * 1024 + th * 512:ic * 1024 + (th + 1) * 512],
                        start=(ic == 0), stop=(ic == NIC - 1)),
                        TS if (ic == NIC - 1 and th == 1) else None)
            s2.free = [t_w2]
            ctx.wait("ve", t_w2)
            for th, bb in ((0, b0), (1, b1)):
                dst = acc_ap[:, dd * 1024 + th * 512:dd * 1024 + (th + 1) * 512]
                if acc_init:
                    a_tok = ctx.op("ve", lambda e, dst=dst, bb=bb: e.tensor_copy(
                        dst, bb.ap[:, :]), VS)
                else:
                    a_tok = ctx.op("ve", lambda e, dst=dst, bb=bb: e.tensor_add(
                        dst, dst, bb.ap[:, :]), VS)
                bb.free = [a_tok]
        he_state["free"] = [t_w2]
        return a_tok

    def emit_cw_bcast(ctx, st, fpool, e_idx, slot):
        cwT = st["cwT"]
        cwrow = st["cwrow"]
        ctx.waits("sy", cwrow.free)
        cwrow.free = []
        ctx.wait("sy", st["cwT_done"])
        rtok = ctx.dma("sy", lambda e_idx=e_idx: dict(
            out=cwrow.ap[0:1, :], in_=cwT[e_idx:e_idx + 1, :]), sem=cwrow.sem)
        toks = []
        for h in range(2):
            bb = fpool.acquire(ctx)
            ctx.wait("te", rtok)
            tb = ctx.op("te", lambda e, bb=bb, h=h: e.matmul(
                bb.ap[:, :], onesrow_sb, cwrow.ap[0:1, h * 512:(h + 1) * 512],
                start=True, stop=True), TS)
            ctx.waits("ve", slot.free)
            slot.free = []
            ctx.wait("ve", tb)
            vt = ctx.op("ve", lambda e, bb=bb, h=h, slot=slot: e.tensor_copy(
                slot.ap[:, h * 512:(h + 1) * 512], bb.ap[:, :]), VS)
            bb.free = [vt]
            toks.append(vt)
        cwrow.free = [toks[1]]
        slot.ready = toks[1]

    def ffn_phase(blk, ysrc, ydst, final_mode):
        pes = ExitStack()

        def psb(name, shape, dtype):
            return pes.enter_context(nc.sbuf_tensor(name, shape, dtype)).ap()

        tag = "f1" if final_mode == "store" else "f2"
        zt_ap = psb(tag + "_zt", [128, NDD * 1024], BF16)
        acc_ap = psb(tag + "_acc", [128, NDD * 1024], F32)
        he_ap = psb(tag + "_he", [128, NIC * 1024], BF16)
        w1aps = [psb(tag + "_w1%d" % i, [128, NDD * 128], BF16) for i in range(2)]
        w3aps = [psb(tag + "_w3%d" % i, [128, NDD * 128], BF16) for i in range(2)]
        w2aps = [psb(tag + "_w2%d" % i, [128, NIC * 128], BF16) for i in range(2)]
        yaps = [psb(tag + "_y%d" % i, [128, 512], F32) for i in range(4)]
        sc2aps = [psb(tag + "_sc2%d" % i, [128, 512], F32) for i in range(2)]
        z32aps = [psb(tag + "_z32%d" % i, [128, 512], F32) for i in range(2)]
        ztmp = psb(tag + "_ztmp", [128, 512], F32)
        row_m = psb(tag + "_rowm", [1, 512], F32)
        row_r = psb(tag + "_rowr", [1, 512], F32)
        row_t = psb(tag + "_rowt", [1, 512], F32)
        mb = psb(tag + "_mb", [128, 512], F32)
        rb = psb(tag + "_rb", [128, 512], F32)
        gl = psb(tag + "_gl", [128, 17], F32)
        pp = psb(tag + "_pp", [128, 16], F32)
        pn = psb(tag + "_pn", [128, 16], F32)
        cwv = psb(tag + "_cwv", [128, 16], F32)
        msk = psb(tag + "_msk", [128, 16], F32)
        col = psb(tag + "_col", [128, 1], F32)
        col2 = psb(tag + "_col2", [128, 1], F32)
        sg = psb(tag + "_sg", [128, 8], F32)
        sig = psb(tag + "_sig", [128, 8], F32)
        sigrow = psb(tag + "_sigrow", [1, 1024], F32)
        sigb = psb(tag + "_sigb", [128, 1024], F32)
        cwT = psb(tag + "_cwT", [16, 1024], F32)
        cwrow_ap = psb(tag + "_cwrow", [1, 1024], F32)
        cwbaps = [psb(tag + "_cwb%d" % i, [128, 1024], F32) for i in range(2)]
        yo_aps = [psb(tag + "_yo%d" % i, [128, 512], F32) for i in range(2)]
        pc_aps = [psb(tag + "_pc%d" % i, [128, 128], F32) for i in range(2)]
        ysems = [B.new_sem(tag + "_sy%d" % i) for i in range(4)]
        w1sems = [B.new_sem(tag + "_sw1%d" % i) for i in range(2)]
        w3sems = [B.new_sem(tag + "_sw3%d" % i) for i in range(2)]
        w2sems = [B.new_sem(tag + "_sw2%d" % i) for i in range(2)]
        cwrsem = B.new_sem(tag + "_scwr")
        yosems = [B.new_sem(tag + "_syo%d" % i) for i in range(2)]
        pcsems = [B.new_sem(tag + "_spc%d" % i) for i in range(2)]

        def body(ctx):
            ctx.barrier(ctx.entry)
            fpool = Pool([Bank(a) for a in FBaps])
            st = {
                "y_sl": [Slot(a, sm) for a, sm in zip(yaps, ysems)],
                "sc2": [Slot(a) for a in sc2aps],
                "z32": [Slot(a) for a in z32aps],
                "ztmp": ztmp, "row_m": row_m, "row_r": row_r, "row_t": row_t,
                "mb": mb, "rb": rb, "gl": gl, "pp": pp, "pn": pn, "cw": cwv,
                "msk": msk, "col": col, "col2": col2, "sg": sg, "sig": sig,
                "sigrow": sigrow, "sigb": sigb, "cwT": cwT,
                "cwrow": Slot(cwrow_ap, cwrsem),
                "W1": [Slot(a, sm) for a, sm in zip(w1aps, w1sems)],
                "W3": [Slot(a, sm) for a, sm in zip(w3aps, w3sems)],
                "W2S": [Slot(a, sm) for a, sm in zip(w2aps, w2sems)],
                "he": he_ap, "acc": acc_ap,
            }
            cwb = [Slot(a) for a in cwbaps]
            yo = [Slot(a, sm) for a, sm in zip(yo_aps, yosems)]
            pc = [Slot(a, sm) for a, sm in zip(pc_aps, pcsems)]
            he_state = {"free": []}
            for s in range(2):
                if s > 0:
                    ctx.barrier([(x, x.n) for x in B.sems])
                t0 = s * ST
                zt_done = emit_ln(ctx, st, fpool, ysrc, t0, ST, zt_ap, 1024, 0,
                                  gate=True)
                emit_sig(ctx, st, fpool)
                a_tok = None
                for q in range(4):
                    a_tok = emit_unit(
                        ctx, st, fpool, zt_ap, zt_done,
                        w1src=lambda jc, q=q: sw1[q * NIC + jc],
                        w3src=lambda jc, q=q: sw3[q * NIC + jc],
                        w2src=lambda dd, q=q: sw2[q, dd],
                        scale_ap=sigb, scale_tok=st["sigb_done"],
                        acc_init=(q == 0), he_state=he_state)
                for ei in range(NE):
                    slot = cwb[ei % 2]
                    emit_cw_bcast(ctx, st, fpool, ei, slot)
                    a_tok = emit_unit(
                        ctx, st, fpool, zt_ap, zt_done,
                        w1src=lambda jc, ei=ei: w1[ei, jc],
                        w3src=lambda jc, ei=ei: w3[ei, jc],
                        w2src=lambda dd, ei=ei: w2[ei, dd],
                        scale_ap=slot.ap, scale_tok=slot.ready,
                        acc_init=False, he_state=he_state)
                    slot.free = [a_tok]
                acc_done = a_tok
                st["zt_free"] = [st["last_hidden"]]
                for dd in range(NDD):
                    for th in range(2):
                        tabs = t0 + th * 512
                        sl = st["y_sl"][(dd * 2 + th) % 4]
                        ctx.waits("sy", sl.free)
                        sl.free = []
                        sl.ready = ctx.dma("sy", lambda dd=dd, tabs=tabs, sl=sl: dict(
                            out=sl.ap[:], in_=ysrc[dd * 128:(dd + 1) * 128, tabs:tabs + 512]),
                            sem=sl.sem)
                        o = yo[(dd * 2 + th) % 2]
                        ctx.wait("ve", sl.ready)
                        ctx.wait("ve", acc_done)
                        ctx.waits("ve", o.free)
                        o.free = []
                        votok = ctx.op("ve", lambda e, o=o, dd=dd, th=th, sl=sl:
                                       e.scalar_tensor_tensor(
                                           o.ap[:],
                                           acc_ap[:, dd * 1024 + th * 512:dd * 1024 + (th + 1) * 512],
                                           0.5, sl.ap[:], ALU.mult, ALU.add), VS)
                        sl.free = [votok]
                        if final_mode == "store":
                            ctx.wait("sy", votok)
                            stok = ctx.dma("sy", lambda o=o, dd=dd, tabs=tabs: dict(
                                out=ydst[dd * 128:(dd + 1) * 128, tabs:tabs + 512],
                                in_=o.ap[:]), sem=o.sem)
                            o.free = [stok]
                        else:
                            lastt = votok
                            for sub in range(4):
                                bT = fpool.acquire(ctx)
                                ctx.wait("te", votok)
                                ttok = ctx.op("te", lambda e, bT=bT, o=o, sub=sub: e.transpose(
                                    bT.ap[:, 0:128], o.ap[:, sub * 128:(sub + 1) * 128],
                                    identf_sb), TS)
                                p = pc[(dd * 8 + th * 4 + sub) % 2]
                                ctx.waits("ve", p.free)
                                p.free = []
                                ctx.wait("ve", ttok)
                                pvt = ctx.op("ve", lambda e, p=p, bT=bT: e.tensor_copy(
                                    p.ap[:], bT.ap[:, 0:128]), VS)
                                bT.free = [pvt]
                                ctx.wait("sy", pvt)
                                pstok = ctx.dma("sy", lambda p=p, tabs=tabs, sub=sub, dd=dd: dict(
                                    out=out[tabs + sub * 128:tabs + (sub + 1) * 128,
                                            dd * 128:(dd + 1) * 128],
                                    in_=p.ap[:]), sem=p.sem)
                                p.free = [pstok]
                                lastt = ttok
                            o.free = [lastt]

        B.phase(blk, body)
        pes.close()

    def qkv_phase(blk):
        pes = ExitStack()

        def psb(name, shape, dtype):
            return pes.enter_context(nc.sbuf_tensor(name, shape, dtype)).ap()

        zt_ap = psb("qk_zt", [128, NDD * 2048], BF16)
        waps = [psb("qk_w%d" % i, [128, NDD * 128], BF16) for i in range(2)]
        qaps = [psb("qk_q%d" % i, [128, 2048], BF16) for i in range(2)]
        yaps = [psb("qk_y%d" % i, [128, 512], F32) for i in range(4)]
        sc2aps = [psb("qk_sc2%d" % i, [128, 512], F32) for i in range(2)]
        z32aps = [psb("qk_z32%d" % i, [128, 512], F32) for i in range(2)]
        ztmp = psb("qk_ztmp", [128, 512], F32)
        row_m = psb("qk_rowm", [1, 512], F32)
        row_r = psb("qk_rowr", [1, 512], F32)
        row_t = psb("qk_rowt", [1, 512], F32)
        mb = psb("qk_mb", [128, 512], F32)
        rb = psb("qk_rb", [128, 512], F32)
        qysems = [B.new_sem("qk_sy%d" % i) for i in range(4)]
        qwsems = [B.new_sem("qk_sw%d" % i) for i in range(2)]
        qssems = [B.new_sem("qk_sq%d" % i) for i in range(2)]

        def body(ctx):
            ctx.barrier(ctx.entry)
            fpool = Pool([Bank(a) for a in FBaps])
            st = {
                "y_sl": [Slot(a, sm) for a, sm in zip(yaps, qysems)],
                "sc2": [Slot(a) for a in sc2aps],
                "z32": [Slot(a) for a in z32aps],
                "ztmp": ztmp, "row_m": row_m, "row_r": row_r, "row_t": row_t,
                "mb": mb, "rb": rb,
            }
            zt_done = None
            for half in range(2):
                zt_done = emit_ln(ctx, st, fpool, y1H, half * 1024, 1024, zt_ap,
                                  2048, half * 1024, gate=False)
            W = [Slot(a, sm) for a, sm in zip(waps, qwsems)]
            Q = [Slot(a, sm) for a, sm in zip(qaps, qssems)]
            last_stores = {}
            for ec in range(NEC):
                s = W[ec % 2]
                ctx.waits("gp", s.free)
                s.free = []
                s.ready = ctx.dma("gp", lambda s=s, ec=ec: dict(
                    out=s.ap, in_=wattnT[ec]), sem=s.sem)
                banks = [fpool.acquire(ctx) for _ in range(4)]
                ctx.wait("te", s.ready)
                ctx.wait("te", zt_done)
                tq = None
                for dd in range(NDD):
                    for ti in range(4):
                        tq = ctx.op("te", lambda e, dd=dd, ti=ti, s=s: e.matmul(
                            banks[ti].ap[:, :], s.ap[:, dd * 128:(dd + 1) * 128],
                            zt_ap[:, dd * 2048 + ti * 512:dd * 2048 + (ti + 1) * 512],
                            start=(dd == 0), stop=(dd == NDD - 1)),
                            TS if (dd == NDD - 1 and ti == 3) else None)
                s.free = [tq]
                qs = Q[ec % 2]
                ctx.waits("sc", qs.free)
                qs.free = []
                ctx.wait("sc", tq)
                sq = None
                for ti in range(4):
                    sq = ctx.op("sc", lambda e, ti=ti, ec=ec, qs=qs: e.activation(
                        qs.ap[:, ti * 512:(ti + 1) * 512], banks[ti].ap[:, :],
                        AF.Identity, bias=battn_sb[:, ec:ec + 1]),
                        SS if ti == 3 else None)
                for ti in range(4):
                    banks[ti].free = [sq]
                ctx.wait("sy", sq)
                stok = ctx.dma("sy", lambda qs=qs, ec=ec: dict(
                    out=ginA[ec * 128:(ec + 1) * 128, :], in_=qs.ap[:]), sem=qs.sem)
                qs.free = [stok]
                last_stores[ec % 2] = stok
            for t in last_stores.values():
                ctx.wait("gp", t)
            ctx.op("gp", lambda e: e.collective_compute(
                "AllGather", ALU.bypass,
                replica_groups=[[0, 1, 2, 3], [4, 5, 6, 7]],
                ins=[ginA.opt()], outs=[goutA.opt()]), CC)

        B.phase(blk, body)
        pes.close()

    def attn_phase(blk):
        pes = ExitStack()

        def psb(name, shape, dtype):
            return pes.enter_context(nc.sbuf_tensor(name, shape, dtype)).ap()

        K_ap = psb("at_K", [128, S], BF16)
        Q_ap = psb("at_Q", [128, S], BF16)
        Vtm_ap = psb("at_V", [128, S], BF16)
        vf_aps = [psb("at_vf%d" % i, [128, 1024], BF16) for i in range(2)]
        scores_ap = psb("at_sco", [128, S], F32)
        probs_ap = psb("at_pr", [128, 2 * S], BF16)
        pT_aps = [psb("at_pT%d" % i, [128, 256], BF16) for i in range(2)]
        cx_aps = [psb("at_cx%d" % i, [128, 256], BF16) for i in range(2)]
        lrow = psb("at_lrow", [1, 256], F32)
        lcol = psb("at_lcol", [128, 2], F32)
        linvb = psb("at_linvb", [128, 256], F32)
        mcol = psb("at_mcol", [128, 1], F32)
        ncol = psb("at_ncol", [128, 1], F32)
        scol = psb("at_scol", [128, 1], F32)
        kvqsem = B.new_sem("at_skvq")
        Ksem = B.new_sem("at_sK")
        Qsem = B.new_sem("at_sQ")
        vfsems = [B.new_sem("at_svf%d" % i) for i in range(2)]
        cxsems = [B.new_sem("at_scx%d" % i) for i in range(2)]

        g3 = goutA.rearrange("(r f) t -> r f t", r=4)
        g4 = goutA.rearrange("(r f) t -> f r t", r=4)

        def body(ctx):
            ctx.barrier(ctx.entry)
            cc_tok = (CC, CC.n)
            ctx.wait("sy", cc_tok)
            fpool = Pool([Bank(a) for a in FBaps])
            gpool = Pool([Bank(a) for a in GBaps])
            regs = {}

            def setup_regs(e):
                for idx in range(4):
                    regs[idx] = e.value_load(meta_sb[0:1, idx:idx + 1])
            ctx.run("sy", setup_regs)
            vf = [Slot(a, sm) for a, sm in zip(vf_aps, vfsems)]
            pT = [Slot(a) for a in pT_aps]
            cx = [Slot(a, sm) for a, sm in zip(cx_aps, cxsems)]
            K = Slot(K_ap, Ksem)
            Q = Slot(Q_ap, Qsem)
            Vtm = Slot(Vtm_ap)
            probs_free = [[], []]
            lrow_free = []
            lcol_free = []
            last_cs = None

            for i in range(4):
                if i > 0:
                    ctx.barrier([(x, x.n) for x in B.sems])
                ctx.wait("sy", K.ready)
                ctx.wait("sy", Q.ready)
                for sl in vf:
                    ctx.wait("sy", sl.ready)
                hq_tok = ctx.dma("sy", lambda i=i: dict(
                    out=kvqH,
                    in_=g3[:, bass.ds(regs[i], 384), :]), sem=kvqsem)
                ctx.wait("sy", hq_tok)
                K.ready = ctx.dma("sy", lambda i=i: dict(
                    out=K_ap.rearrange("p (r t) -> p r t", r=4),
                    in_=kvqH.rearrange("r f t -> f r t")[128:256, :, :]), sem=K.sem)
                Q.ready = ctx.dma("sy", lambda i=i: dict(
                    out=Q_ap.rearrange("p (r t) -> p r t", r=4),
                    in_=kvqH.rearrange("r f t -> f r t")[0:128, :, :]), sem=Q.sem)
                vt_tok = None
                for c in range(8):
                    sl = vf[c % 2]
                    ctx.waits("sy", sl.free)
                    sl.free = []
                    sl.ready = ctx.dma("sy", lambda c=c, i=i, sl=sl: dict(
                        out=sl.ap[:],
                        in_=kvqH[c // 2, 256:384,
                                 (c % 2) * 1024:(c % 2 + 1) * 1024]), sem=sl.sem)
                    gb = gpool.acquire(ctx)
                    ctx.wait("te", sl.ready)
                    tv = None
                    for j in range(8):
                        tv = ctx.op("te", lambda e, gb=gb, sl=sl, j=j: e.transpose(
                            gb.ap[:, j * 128:(j + 1) * 128],
                            sl.ap[:, j * 128:(j + 1) * 128], identbf_sb),
                            TS if j == 7 else None)
                    sl.free = [tv]
                    ctx.wait("ve", tv)
                    vt_tok = ctx.op("ve", lambda e, gb=gb, c=c: e.tensor_copy(
                        Vtm_ap[:, c * 1024:(c + 1) * 1024], gb.ap[:, :]), VS)
                    gb.free = [vt_tok]
                Vtm.ready = vt_tok

                for qb in range(32):
                    exp_toks = []
                    vL = None
                    for a in range(2):
                        qa = qb * 256 + a * 128
                        nk = qa + 128
                        ev_tok = None
                        nkb = (nk + 511) // 512
                        for kb in range(nkb):
                            n = min(512, nk - kb * 512)
                            bsc = fpool.acquire(ctx)
                            ctx.wait("te", K.ready)
                            ctx.wait("te", Q.ready)
                            tsc = ctx.op("te", lambda e, bsc=bsc, qa=qa, kb=kb, n=n: e.matmul(
                                bsc.ap[:, 0:n], Q_ap[:, qa:qa + 128],
                                K_ap[:, kb * 512:kb * 512 + n],
                                start=True, stop=True), TS)
                            ctx.wait("sc", tsc)
                            ev_tok = ctx.op("sc", lambda e, bsc=bsc, kb=kb, n=n: e.copy(
                                scores_ap[0:128, kb * 512:kb * 512 + n],
                                bsc.ap[:, 0:n]), SS)
                            bsc.free = [ev_tok]
                        ctx.wait("ve", ev_tok)
                        ctx.vop(lambda e, qa=qa: e.tensor_add(
                            scores_ap[0:128, qa:qa + 128],
                            scores_ap[0:128, qa:qa + 128], cmask_sb))
                        mx_tok = ctx.vop(lambda e, nk=nk: e.reduce_max(
                            mcol, scores_ap[0:128, 0:nk],
                            axis=mybir.AxisListType.X))
                        ctx.wait("sc", mx_tok)
                        ctx.waits("sc", probs_free[a])
                        probs_free[a] = []
                        nstok = ctx.op("sc", lambda e: e.mul(ncol, mcol, -SCALE), SS)
                        ctx.wait("sc", nstok)
                        ex_tok = ctx.op("sc", lambda e, a=a, nk=nk: e.activation(
                            probs_ap[:, a * S:a * S + nk], scores_ap[0:128, 0:nk],
                            AF.Exp, bias=ncol, scale=SCALE, accum_out=scol), SS)
                        exp_toks.append(ex_tok)
                        ctx.wait("ve", ex_tok)
                        ctx.waits("ve", lcol_free)
                        lcol_free = []
                        rc_tok = ctx.op("ve", lambda e, a=a: e.reciprocal(
                            lcol[:, a:a + 1], scol), VS)
                        bL = fpool.acquire(ctx)
                        ctx.wait("te", rc_tok)
                        tL = ctx.op("te", lambda e, bL=bL, a=a: e.transpose(
                            bL.ap[0:1, a * 128:(a + 1) * 128], lcol[:, a:a + 1],
                            identf_sb), TS)
                        lcol_free.append(tL)
                        ctx.waits("ve", lrow_free)
                        lrow_free = []
                        ctx.wait("ve", tL)
                        vL = ctx.op("ve", lambda e, bL=bL, a=a: e.tensor_copy(
                            lrow[0:1, a * 128:(a + 1) * 128],
                            bL.ap[0:1, a * 128:(a + 1) * 128]), VS)
                        bL.free = [vL]
                    bB = fpool.acquire(ctx)
                    ctx.wait("te", vL)
                    tB = ctx.op("te", lambda e, bB=bB: e.matmul(
                        bB.ap[:, 0:256], onesrow_sb, lrow[0:1, 0:256],
                        start=True, stop=True), TS)
                    lrow_free = [tB]
                    ctx.wait("ve", tB)
                    vB = ctx.op("ve", lambda e, bB=bB: e.tensor_copy(
                        linvb, bB.ap[:, 0:256]), VS)
                    bB.free = [vB]
                    bctx = fpool.acquire(ctx)
                    nkt = 2 * qb + 2
                    nkA = qb * 256 + 128
                    pv_tok = None
                    for kt in range(nkt):
                        bothv = kt * 128 < nkA
                        gb = gpool.acquire(ctx)
                        ctx.wait("te", exp_toks[1])
                        if bothv:
                            ctx.wait("te", exp_toks[0])
                            ctx.op("te", lambda e, gb=gb, kt=kt: e.transpose(
                                gb.ap[:, 0:128], probs_ap[:, kt * 128:(kt + 1) * 128],
                                identbf_sb))
                        tT = ctx.op("te", lambda e, gb=gb, kt=kt: e.transpose(
                            gb.ap[:, 128:256],
                            probs_ap[:, S + kt * 128:S + (kt + 1) * 128],
                            identbf_sb), TS)
                        p = pT[kt % 2]
                        ctx.waits("ve", p.free)
                        p.free = []
                        ctx.wait("ve", tT)
                        lo = 0 if bothv else 128
                        vT = ctx.op("ve", lambda e, p=p, gb=gb, lo=lo: e.tensor_copy(
                            p.ap[:, lo:256], gb.ap[:, lo:256]), VS)
                        gb.free = [vT]
                        ctx.wait("te", vT)
                        ctx.wait("te", Vtm.ready)
                        pv_tok = ctx.op("te", lambda e, p=p, kt=kt, lo=lo, bctx=bctx,
                                        nkt=nkt: e.matmul(
                            bctx.ap[:, lo:256], Vtm_ap[:, kt * 128:(kt + 1) * 128],
                            p.ap[:, lo:256],
                            start=(kt == 0), stop=(kt == nkt - 1)), TS)
                        p.free = [pv_tok]
                    probs_free[0] = [pv_tok]
                    probs_free[1] = [pv_tok]
                    c = cx[qb % 2]
                    ctx.waits("ve", c.free)
                    c.free = []
                    ctx.wait("ve", pv_tok)
                    ctx.wait("ve", vB)
                    vC = ctx.op("ve", lambda e, c=c, bctx=bctx: e.tensor_mul(
                        c.ap[:], bctx.ap[:, 0:256], linvb), VS)
                    bctx.free = [vC]
                    ctx.wait("sy", vC)
                    cs_tok = ctx.dma("sy", lambda c=c, i=i, qb=qb: dict(
                        out=ginB[i * 128:(i + 1) * 128, qb * 256:(qb + 1) * 256],
                        in_=c.ap[:]), sem=c.sem)
                    c.free = [cs_tok]
                    last_cs = {**(last_cs or {}), qb % 2: cs_tok}
            for t in last_cs.values():
                ctx.wait("gp", t)
            ctx.op("gp", lambda e: e.collective_compute(
                "AllGather", ALU.bypass,
                replica_groups=[[0, 1, 2, 3], [4, 5, 6, 7]],
                ins=[ginB.opt()], outs=[goutB.opt()]), CC)

        B.phase(blk, body)
        pes.close()

    def proj_phase(blk):
        pes = ExitStack()

        def psb(name, shape, dtype):
            return pes.enter_context(nc.sbuf_tensor(name, shape, dtype)).ap()

        ctxf_ap = psb("pj_ctx", [128, NDD * 2048], BF16)
        waps = [psb("pj_w%d" % i, [128, NDD * 128], BF16) for i in range(2)]
        po_aps = [psb("pj_po%d" % i, [128, 512], F32) for i in range(2)]
        yaps = [psb("pj_y%d" % i, [128, 512], F32) for i in range(4)]
        y2aps = [psb("pj_y2%d" % i, [128, 512], F32) for i in range(2)]
        pctxsem = B.new_sem("pj_sctxH")
        pcfsem = B.new_sem("pj_sctxf")
        pwsems = [B.new_sem("pj_sw%d" % i) for i in range(2)]
        pysems = [B.new_sem("pj_sy%d" % i) for i in range(4)]
        py2sems = [B.new_sem("pj_sy2%d" % i) for i in range(2)]

        def body(ctx):
            ctx.barrier(ctx.entry)
            cc_tok = (CC, CC.n)
            ctx.wait("sy", cc_tok)
            fpool = Pool([Bank(a) for a in FBaps])
            regs = {}

            def setup_regs(e):
                regs[0] = e.value_load(meta_sb[0:1, 12:13])
            ctx.run("sy", setup_regs)
            cpy_tok = ctx.dma("sy", lambda: dict(
                out=ctxH, in_=goutB[:, bass.ds(regs[0], 2048)]), sem=pctxsem)
            ctx.wait("sy", cpy_tok)
            ctxf_tok = None
            for hc in range(NDD):
                ctxf_tok = ctx.dma("sy", lambda hc=hc: dict(
                    out=ctxf_ap[:, hc * 2048:(hc + 1) * 2048],
                    in_=ctxH[hc * 128:(hc + 1) * 128, :]), sem=pcfsem)
            W = [Slot(a, sm) for a, sm in zip(waps, pwsems)]
            po = [Slot(a) for a in po_aps]
            ysl = [Slot(a, sm) for a, sm in zip(yaps, pysems)]
            y2 = [Slot(a, sm) for a, sm in zip(y2aps, py2sems)]
            for dd in range(NDD):
                s = W[dd % 2]
                ctx.waits("gp", s.free)
                s.free = []
                s.ready = ctx.dma("gp", lambda s=s, dd=dd: dict(
                    out=s.ap, in_=wprojT[dd]), sem=s.sem)
                banks = [fpool.acquire(ctx) for _ in range(4)]
                ctx.wait("te", s.ready)
                ctx.wait("te", ctxf_tok)
                tp = None
                for hc in range(NDD):
                    for ti in range(4):
                        tp = ctx.op("te", lambda e, hc=hc, ti=ti, s=s: e.matmul(
                            banks[ti].ap[:, :], s.ap[:, hc * 128:(hc + 1) * 128],
                            ctxf_ap[:, hc * 2048 + ti * 512:hc * 2048 + (ti + 1) * 512],
                            start=(hc == 0), stop=(hc == NDD - 1)),
                            TS if (hc == NDD - 1 and ti == 3) else None)
                s.free = [tp]
                for ti in range(4):
                    o = po[ti % 2]
                    ctx.waits("sc", o.free)
                    o.free = []
                    ctx.wait("sc", tp)
                    sp = ctx.op("sc", lambda e, o=o, ti=ti, dd=dd: e.activation(
                        o.ap[:], banks[ti].ap[:, :], AF.Identity,
                        bias=bproj_sb[:, dd:dd + 1]), SS)
                    banks[ti].free = [sp]
                    sl = ysl[ti]
                    ctx.waits("sy", sl.free)
                    sl.free = []
                    sl.ready = ctx.dma("sy", lambda sl=sl, dd=dd, ti=ti: dict(
                        out=sl.ap[:],
                        in_=y1H[dd * 128:(dd + 1) * 128, ti * 512:(ti + 1) * 512]),
                        sem=sl.sem)
                    o2 = y2[ti % 2]
                    ctx.waits("ve", o2.free)
                    o2.free = []
                    ctx.wait("ve", sp)
                    ctx.wait("ve", sl.ready)
                    v2 = ctx.op("ve", lambda e, o2=o2, o=o, sl=sl: e.tensor_add(
                        o2.ap[:], o.ap[:], sl.ap[:]), VS)
                    o.free = [v2]
                    sl.free = [v2]
                    ctx.wait("sy", v2)
                    s2 = ctx.dma("sy", lambda o2=o2, dd=dd, ti=ti: dict(
                        out=y2H[dd * 128:(dd + 1) * 128, ti * 512:(ti + 1) * 512],
                        in_=o2.ap[:]), sem=o2.sem)
                    o2.free = [s2]

        B.phase(blk, body)
        pes.close()

    with nc.Block() as blk:
        def init_body(ctx):
            for dst, src in [
                (identbf_sb, identbf), (identf_sb, identf),
                (onescol_sb, onescol), (onesrow_sb, onesrow),
                (cmask_sb, cmaskD), (battn_sb, battnC),
                (bproj_sb, bprojC), (meta_sb, meta),
            ]:
                ctx.dma("sy", lambda dst=dst, src=src: dict(out=dst, in_=src))
            ctx.dma("sy", lambda: dict(
                out=gsg_sb.rearrange("p (dd c) -> p dd c", dd=NDD),
                in_=gate_sg.rearrange("(dd p) c -> p dd c", p=128)))
            ctx.op("ve", lambda e: e.memset(eps_sb, EPS), VS)

        B.phase(blk, init_body)
        ffn_phase(blk, xT, y1H, "store")
        qkv_phase(blk)
        attn_phase(blk)
        proj_phase(blk)
        ffn_phase(blk, y2H, None, "out")

        def fin_body(ctx):
            ctx.barrier(ctx.entry)
            for dap, sap in dump_specs:
                ctx.dma("sy", lambda dap=dap, sap=sap: dict(out=dap, in_=sap))
            ctx.wait("sy", (B.SY, B.SY.n))

        B.phase(blk, fin_body)

    es.close()
    return nc


def _host_inputs(L0_attn_x, L0_attn_w_attn, L0_attn_b_attn, L0_attn_w_proj,
                 L0_attn_b_proj, L0_ffn_gate_w, L0_ffn_w1, L0_ffn_w3, L0_ffn_w2,
                 L0_ffn_s_w1, L0_ffn_s_w3, L0_ffn_s_w2, L0_ffn_sg_w):
    f = np.asarray
    x = f(L0_attn_x, dtype=np.float32)
    def slab_kd(w):
        K, M = w.shape
        return np.ascontiguousarray(
            w.reshape(K // 128, 128, M // 128, 128).transpose(2, 1, 0, 3)
            .reshape(M // 128, 128, K))

    w1f = f(L0_ffn_w1, dtype=np.float32)
    w3f = f(L0_ffn_w3, dtype=np.float32)
    w2f = f(L0_ffn_w2, dtype=np.float32)
    consts = dict(
        wattnT=slab_kd(f(L0_attn_w_attn, dtype=np.float32).T),
        battnC=np.ascontiguousarray(f(L0_attn_b_attn, dtype=np.float32).reshape(NEC, 128).T),
        wprojT=slab_kd(f(L0_attn_w_proj, dtype=np.float32).T),
        bprojC=np.ascontiguousarray(f(L0_attn_b_proj, dtype=np.float32).reshape(NDD, 128).T),
        gate_sg=np.ascontiguousarray(np.concatenate(
            [f(L0_ffn_gate_w, dtype=np.float32), f(L0_ffn_sg_w, dtype=np.float32)],
            axis=1)),
        w1=np.stack([slab_kd(w1f[e]) for e in range(NE)]),
        w3=np.stack([slab_kd(w3f[e]) for e in range(NE)]),
        w2=np.stack([slab_kd(w2f[e]) for e in range(NE)]),
        sw1=slab_kd(f(L0_ffn_s_w1, dtype=np.float32)),
        sw3=slab_kd(f(L0_ffn_s_w3, dtype=np.float32)),
        sw2=np.stack([slab_kd(f(L0_ffn_s_w2, dtype=np.float32)[q * IH:(q + 1) * IH, :])
                      for q in range(4)]),
        identbf=np.eye(128, dtype=BF),
        identf=np.eye(128, dtype=np.float32),
        onescol=np.ones((128, 1), np.float32),
        onesrow=np.ones((1, 128), np.float32),
        cmaskD=np.where(np.arange(128)[:, None] >= np.arange(128)[None, :],
                        np.float32(0.0), np.float32(BIGNEG)).astype(np.float32),
    )
    in_maps = []
    for core in range(8):
        b, q = core // 4, core % 4
        m = np.zeros((1, 16), np.int32)
        hb = q * 4 * 384
        for i in range(4):
            m[0, 0 + i] = hb + i * 384
        m[0, 12] = q * TC
        d = dict(consts)
        d["xT"] = np.ascontiguousarray(x[b, q * TC:(q + 1) * TC, :].T)
        d["meta"] = m
        in_maps.append(d)
    return in_maps


def kernel(**inputs):
    in_maps = _host_inputs(**inputs)
    nc = build_program()
    res = run_bass_kernel_spmd(nc, in_maps, core_ids=list(range(8)))
    y = np.empty((2, S, D), np.float32)
    for core in range(8):
        b, q = core // 4, core % 4
        y[b, q * TC:(q + 1) * TC, :] = res.results[core]["out"]
    return y


# version 05
# speedup vs baseline: 1.0013x; 1.0013x over previous
import numpy as np
import ml_dtypes
from contextlib import ExitStack

import concourse.bass as bass
import concourse.mybir as mybir
from concourse.bass_utils import run_bass_kernel_spmd

F32 = mybir.dt.float32
BF16 = mybir.dt.bfloat16
I32 = mybir.dt.int32
AF = mybir.ActivationFunctionType
ALU = mybir.AluOpType
BF = ml_dtypes.bfloat16

D = 2048
TC = 2048
ST = 1024
NDD = D // 128
NE = 16
IH = 1408
NIC = IH // 128
SHID = 5632
NJC = SHID // 128
QKVF = 3 * D
NEC = QKVF // 128
S = 8192
HD = 128
SCALE = 1.0 / float(np.sqrt(HD))
BIGNEG = -1.0e9
EPS = 1e-5

ENGS = ("sy", "gp", "te", "ve", "sc")


class Sem:
    __slots__ = ("h", "n", "name")

    def __init__(self, es, nc, name):
        self.h = es.enter_context(nc.semaphore(name))
        self.n = 0
        self.name = name


class Eng:
    def __init__(self, name):
        self.name = name
        self.wm = {}
        self.e = None

    def wait(self, tok):
        if tok is None:
            return
        sem, n = tok
        if n <= 0:
            return
        if self.wm.get(sem.name, 0) < n:
            self.e.wait_ge(sem.h, n)
            self.wm[sem.name] = n


class Slot:
    __slots__ = ("ap", "free", "ready", "sem")

    def __init__(self, ap, sem=None):
        self.ap = ap
        self.free = []
        self.ready = None
        self.sem = sem


class Bank(Slot):
    pass


class Pool:
    def __init__(self, banks):
        self.banks = banks
        self.i = 0

    def acquire(self, ctx):
        b = self.banks[self.i % len(self.banks)]
        self.i += 1
        ctx.waits("te", b.free)
        b.free = []
        return b


class Ctx:
    def __init__(self, B, mode):
        self.B = B
        self.mode = mode
        self.entry = None

    def wait(self, eng, tok):
        if self.mode == eng:
            self.B.E[eng].wait(tok)

    def waits(self, eng, toks):
        for t in toks:
            self.wait(eng, t)

    def dma(self, q, thunk, sem=None):
        if sem is None:
            sem = self.B.SY if q == "sy" else self.B.SW
        if self.mode == q:
            kw = thunk()
            self.B.E[q].e.dma_start(**kw).then_inc(sem.h, 16)
        sem.n += 16
        return (sem, sem.n)

    def op(self, eng, fn, sem=None):
        instr = None
        if self.mode == eng:
            instr = fn(self.B.E[eng].e)
        if sem is not None:
            if instr is not None:
                instr.then_inc(sem.h, 1)
            sem.n += 1
            return (sem, sem.n)
        return None

    def run(self, eng, fn):
        if self.mode == eng:
            return fn(self.B.E[eng].e)
        return None

    def vop(self, fn):
        tok = self.op("ve", fn, self.B.VS)
        self.wait("ve", tok)
        return tok

    def barrier(self, toks):
        for t in toks:
            self.B.E[self.mode].wait(t)


class Builder:
    def __init__(self, nc, es):
        self.nc = nc
        self.es = es
        self.SY = Sem(es, nc, "SY")
        self.SW = Sem(es, nc, "SW")
        self.CC = Sem(es, nc, "CC")
        self.TS = Sem(es, nc, "TS")
        self.VS = Sem(es, nc, "VS")
        self.SS = Sem(es, nc, "SS")
        self.sems = [self.SY, self.SW, self.CC, self.TS, self.VS, self.SS]
        self.E = {n: Eng(n) for n in ENGS}

    def new_sem(self, name):
        s = Sem(self.es, self.nc, name)
        self.sems.append(s)
        return s

    def toks(self):
        return [(s, s.n) for s in self.sems]

    def phase(self, blk, body):
        entry = self.toks()
        snap = [s.n for s in self.sems]
        final = None
        for mode in ENGS:
            for s, n0 in zip(self.sems, snap):
                s.n = n0
            ctx = Ctx(self, mode)
            ctx.entry = entry

            def run_body(e, mode=mode, ctx=ctx):
                self.E[mode].e = e
                body(ctx)

            getattr_map = {"sy": blk.sync, "gp": blk.gpsimd, "te": blk.tensor,
                           "ve": blk.vector, "sc": blk.scalar}
            getattr_map[mode](run_body)
            end = [s.n for s in self.sems]
            if final is None:
                final = end
            else:
                assert final == end, ("phase replay divergence", final, end)
        for s, n in zip(self.sems, final):
            s.n = n


def build_program(dumps=()):
    nc = bass.Bass("TRN2", target_bir_lowering=False, debug=False)

    xT = nc.dram_tensor("xT", [D, TC], F32, kind="ExternalInput").ap()
    wattnT = nc.dram_tensor("wattnT", [NEC, 128, D], F32, kind="ExternalInput").ap()
    battnC = nc.dram_tensor("battnC", [128, NEC], F32, kind="ExternalInput").ap()
    wprojT = nc.dram_tensor("wprojT", [NDD, 128, D], F32, kind="ExternalInput").ap()
    bprojC = nc.dram_tensor("bprojC", [128, NDD], F32, kind="ExternalInput").ap()
    gate_sg = nc.dram_tensor("gate_sg", [D, 17], F32, kind="ExternalInput").ap()
    w1 = nc.dram_tensor("w1", [NE, NIC, 128, D], F32, kind="ExternalInput").ap()
    w3 = nc.dram_tensor("w3", [NE, NIC, 128, D], F32, kind="ExternalInput").ap()
    w2 = nc.dram_tensor("w2", [NE, NDD, 128, IH], F32, kind="ExternalInput").ap()
    sw1 = nc.dram_tensor("sw1", [NJC, 128, D], F32, kind="ExternalInput").ap()
    sw3 = nc.dram_tensor("sw3", [NJC, 128, D], F32, kind="ExternalInput").ap()
    sw2 = nc.dram_tensor("sw2", [4, NDD, 128, IH], F32, kind="ExternalInput").ap()
    identbf = nc.dram_tensor("identbf", [128, 128], BF16, kind="ExternalInput").ap()
    identf = nc.dram_tensor("identf", [128, 128], F32, kind="ExternalInput").ap()
    onescol = nc.dram_tensor("onescol", [128, 1], F32, kind="ExternalInput").ap()
    onesrow = nc.dram_tensor("onesrow", [1, 128], F32, kind="ExternalInput").ap()
    cmaskD = nc.dram_tensor("cmaskD", [128, 128], F32, kind="ExternalInput").ap()
    meta = nc.dram_tensor("meta", [1, 16], I32, kind="ExternalInput").ap()

    out = nc.dram_tensor("out", [TC, D], F32, kind="ExternalOutput").ap()

    y1H = nc.dram_tensor("y1H", [D, TC], F32, kind="Internal").ap()
    y2H = nc.dram_tensor("y2H", [D, TC], F32, kind="Internal").ap()
    ginA = nc.dram_tensor("ginA", [QKVF, TC], BF16, kind="Internal").ap()
    goutA = nc.dram_tensor("goutA", [4 * QKVF, TC], BF16, kind="Internal").ap()
    ginB = nc.dram_tensor("ginB", [4 * HD, S], BF16, kind="Internal").ap()
    goutB = nc.dram_tensor("goutB", [D, S], BF16, kind="Internal").ap()
    kvqH = nc.dram_tensor("kvqH", [4, 384, TC], BF16, kind="Internal").ap()
    ctxH = nc.dram_tensor("ctxH", [D, TC], BF16, kind="Internal").ap()

    dump_specs = []
    srcs = {"y1H": y1H, "y2H": y2H, "ginA": ginA, "goutA": goutA,
            "ginB": ginB, "goutB": goutB, "kvqH": kvqH, "ctxH": ctxH}
    for name in dumps:
        sap = srcs[name]
        dap = nc.dram_tensor("dump_" + name, list(sap.tensor.shape),
                             sap.tensor.dtype, kind="ExternalOutput").ap()
        dump_specs.append((dap, sap))

    es = ExitStack()
    B = Builder(nc, es)
    SY, SW, CC, TS, VS, SS = B.SY, B.SW, B.CC, B.TS, B.VS, B.SS

    def sbuf(name, shape, dtype):
        return es.enter_context(nc.sbuf_tensor(name, shape, dtype)).ap()

    identbf_sb = sbuf("c_identbf", [128, 128], BF16)
    identf_sb = sbuf("c_identf", [128, 128], F32)
    onescol_sb = sbuf("c_onescol", [128, 1], F32)
    onesrow_sb = sbuf("c_onesrow", [1, 128], F32)
    cmask_sb = sbuf("c_cmask", [128, 128], F32)
    battn_sb = sbuf("c_battn", [128, NEC], F32)
    bproj_sb = sbuf("c_bproj", [128, NDD], F32)
    gsg_sb = sbuf("c_gsg", [128, NDD * 17], F32)
    meta_sb = sbuf("c_meta", [1, 16], I32)
    eps_sb = sbuf("c_eps", [1, 1], F32)

    FBaps = [es.enter_context(nc.psum_tensor("psf%d" % i, [128, 512], F32)).ap()
             for i in range(6)]
    GBaps = [es.enter_context(nc.psum_tensor("psg%d" % i, [128, 1024], BF16)).ap()
             for i in range(2)]

    def emit_ln(ctx, st, fpool, ysrc, tok0, ntok, zt_ap, zt_stride, zt_off, gate):
        y_sl = st["y_sl"]
        sc2 = st["sc2"]
        z32 = st["z32"]
        ztmp = st["ztmp"]
        row_m, row_r, row_t = st["row_m"], st["row_r"], st["row_t"]
        mb_sb, rb_sb = st["mb"], st["rb"]
        rf = st.setdefault("row_free", {"m": [], "r": [], "t": []})
        zt_free = st.get("zt_free", [])
        zt_done = None
        ntt = ntok // 512
        for tt in range(ntt):
            tabs = tok0 + tt * 512
            tloc = zt_off + tt * 512
            b_sum = fpool.acquire(ctx)
            b_sq = fpool.acquire(ctx)
            sum_tok = sumsq_tok = None
            for dd in range(NDD):
                sl = y_sl[dd % 4]
                ctx.waits("sy", sl.free)
                sl.free = []
                sl.ready = ctx.dma("sy", lambda dd=dd, tabs=tabs, sl=sl: dict(
                    out=sl.ap[:], in_=ysrc[dd * 128:(dd + 1) * 128, tabs:tabs + 512]),
                    sem=sl.sem)
                sq = sc2[dd % 2]
                ctx.wait("sc", sl.ready)
                ctx.waits("sc", sq.free)
                sq.free = []
                sq.ready = ctx.op("sc", lambda e, sq=sq, sl=sl: e.square(sq.ap[:], sl.ap[:]), SS)
                ctx.wait("te", sl.ready)
                t1 = ctx.op("te", lambda e, sl=sl, dd=dd, b=b_sum: e.matmul(
                    b.ap[0:1, :], onescol_sb, sl.ap[:],
                    start=(dd == 0), stop=(dd == NDD - 1)), TS)
                ctx.wait("te", sq.ready)
                t2 = ctx.op("te", lambda e, sq=sq, dd=dd, b=b_sq: e.matmul(
                    b.ap[0:1, :], onescol_sb, sq.ap[:],
                    start=(dd == 0), stop=(dd == NDD - 1)), TS)
                sl.free = [t1, sq.ready]
                sq.free = [t2]
                sum_tok, sumsq_tok = t1, t2
            ctx.wait("sc", sum_tok)
            ctx.waits("sc", rf["m"])
            rf["m"] = []
            mtok = ctx.op("sc", lambda e: e.mul(row_m[0:1, :], b_sum.ap[0:1, :], 1.0 / D), SS)
            ctx.wait("sc", sumsq_tok)
            ctx.waits("sc", rf["r"])
            rf["r"] = []
            qtok = ctx.op("sc", lambda e: e.mul(row_r[0:1, :], b_sq.ap[0:1, :], 1.0 / D), SS)
            b_sum.free = [mtok]
            b_sq.free = [qtok]
            ctx.wait("ve", qtok)
            ctx.waits("ve", rf["t"])
            rf["t"] = []
            ctx.vop(lambda e: e.tensor_mul(row_t[0:1, :], row_m[0:1, :], row_m[0:1, :]))
            vtok = ctx.vop(lambda e: e.tensor_sub(row_r[0:1, :], row_r[0:1, :],
                                                  row_t[0:1, :]))
            ctx.wait("sc", vtok)
            lntok = ctx.op("sc", lambda e: e.activation(row_t[0:1, :], row_r[0:1, :], AF.Ln,
                                                        bias=eps_sb[0:1, 0:1]), SS)
            ctx.wait("sc", lntok)
            rtok = ctx.op("sc", lambda e: e.activation(row_r[0:1, :], row_t[0:1, :], AF.Exp,
                                                       scale=-0.5), SS)
            b_mb = fpool.acquire(ctx)
            b_rb = fpool.acquire(ctx)
            ctx.wait("te", mtok)
            tmb = ctx.op("te", lambda e: e.matmul(b_mb.ap[:, :], onesrow_sb, row_m[0:1, :],
                                                  start=True, stop=True), TS)
            ctx.wait("te", rtok)
            trb = ctx.op("te", lambda e: e.matmul(b_rb.ap[:, :], onesrow_sb, row_r[0:1, :],
                                                  start=True, stop=True), TS)
            ctx.wait("ve", tmb)
            ctx.op("ve", lambda e: e.tensor_copy(mb_sb, b_mb.ap[:, :]))
            ctx.wait("ve", trb)
            mrtok = ctx.op("ve", lambda e: e.tensor_copy(rb_sb, b_rb.ap[:, :]), VS)
            b_mb.free = [mrtok]
            b_rb.free = [mrtok]
            rf["m"] = [vtok, tmb]
            rf["r"] = [trb]
            rf["t"] = [rtok]
            if gate:
                gbank = [fpool.acquire(ctx) for _ in range(4)]
            gtok = None
            for dd in range(NDD):
                sl = y_sl[dd % 4]
                ctx.waits("sy", sl.free)
                sl.free = []
                sl.ready = ctx.dma("sy", lambda dd=dd, tabs=tabs, sl=sl: dict(
                    out=sl.ap[:], in_=ysrc[dd * 128:(dd + 1) * 128, tabs:tabs + 512]),
                    sem=sl.sem)
                z = z32[dd % 2]
                ctx.wait("ve", sl.ready)
                ctx.wait("ve", mrtok)
                ctx.waits("ve", z.free)
                z.free = []
                ctx.waits("ve", zt_free)
                ctx.op("ve", lambda e, sl=sl: e.tensor_sub(ztmp, sl.ap[:], mb_sb))
                ctx.op("ve", lambda e, z=z: e.tensor_mul(z.ap[:], ztmp, rb_sb))
                ztok = ctx.op("ve", lambda e, z=z, dd=dd, tloc=tloc: e.tensor_copy(
                    zt_ap[:, dd * zt_stride + tloc:dd * zt_stride + tloc + 512],
                    z.ap[:]), VS)
                sl.free = [ztok]
                z.free = [ztok]
                zt_done = ztok
                if gate:
                    ctx.wait("te", ztok)
                    for g in range(4):
                        gtok = ctx.op("te", lambda e, g=g, dd=dd, z=z: e.matmul(
                            gbank[g].ap[:, 0:17], z.ap[:, g * 128:(g + 1) * 128],
                            gsg_sb[:, dd * 17:(dd + 1) * 17],
                            start=(dd == 0), stop=(dd == NDD - 1)),
                            TS if g == 3 else None)
                    z.free.append(gtok)
            if gate:
                emit_gate_finalize(ctx, st, fpool, gbank, gtok, tt * 512)
        st["zt_free"] = []
        return zt_done

    def emit_gate_finalize(ctx, st, fpool, gbank, gdone, tloc):
        gl, pp, pn, cwv, msk = st["gl"], st["pp"], st["pn"], st["cw"], st["msk"]
        col, col2 = st["col"], st["col2"]
        sg_cols, cwT = st["sg"], st["cwT"]
        vt3 = None
        for g in range(4):
            gb = gbank[g]
            ctx.wait("ve", gdone)
            ctx.vop(lambda e, gb=gb: e.tensor_copy(gl, gb.ap[:, 0:17]))
            ctx.vop(lambda e, g=g: e.tensor_copy(
                sg_cols[:, (tloc // 128) + g:(tloc // 128) + g + 1], gl[:, 16:17]))
            vt = ctx.vop(lambda e: e.reduce_max(col, gl[:, 0:16],
                                                axis=mybir.AxisListType.X))
            gb.free = [vt]
            ctx.wait("sc", vt)
            ntok = ctx.op("sc", lambda e: e.mul(col2, col, -1.0), SS)
            ctx.wait("sc", ntok)
            stok = ctx.op("sc", lambda e: e.activation(pp, gl[:, 0:16], AF.Exp,
                                                       bias=col2, accum_out=col), SS)
            ctx.wait("ve", stok)
            ctx.vop(lambda e: e.reciprocal(col2, col))
            ctx.vop(lambda e: e.tensor_scalar(pn, pp, col2, None, ALU.mult))
            cur = pn
            for rnd in range(3):
                ctx.vop(lambda e, cur=cur: e.reduce_max(col, cur,
                                                        axis=mybir.AxisListType.X))
                ctx.vop(lambda e, cur=cur: e.tensor_scalar(
                    msk, cur, col, None, ALU.is_ge))
                dst = pp if rnd % 2 == 0 else cwv
                ctx.vop(lambda e, cur=cur, dst=dst: e.scalar_tensor_tensor(
                    dst, msk, BIGNEG, cur, ALU.mult, ALU.add))
                cur = dst
            ctx.vop(lambda e, cur=cur: e.reduce_max(col, cur,
                                                    axis=mybir.AxisListType.X))
            ctx.vop(lambda e: e.tensor_scalar(msk, pn, col, None, ALU.is_ge))
            vt2 = ctx.vop(lambda e: e.tensor_mul(cwv, msk, pn))
            bT = fpool.acquire(ctx)
            ctx.wait("te", vt2)
            ttk = ctx.op("te", lambda e, bT=bT: e.transpose(bT.ap[0:16, 0:128], cwv,
                                                            identf_sb), TS)
            ctx.wait("ve", ttk)
            vt3 = ctx.op("ve", lambda e, g=g, tloc=tloc, bT=bT: e.tensor_copy(
                cwT[0:16, tloc + g * 128:tloc + (g + 1) * 128],
                bT.ap[0:16, 0:128]), VS)
            bT.free = [vt3]
        st["cwT_done"] = vt3

    def emit_sig(ctx, st, fpool):
        sg_cols, sig, sigrow, sigb = st["sg"], st["sig"], st["sigrow"], st["sigb"]
        ctx.wait("sc", st["cwT_done"])
        s1 = ctx.op("sc", lambda e: e.activation(sig, sg_cols, AF.Sigmoid), SS)
        rowtoks = []
        for h in range(2):
            bT = fpool.acquire(ctx)
            ctx.wait("te", s1)
            tts = None
            for g in range(4):
                gg = h * 4 + g
                tts = ctx.op("te", lambda e, bT=bT, g=g, gg=gg: e.transpose(
                    bT.ap[0:1, g * 128:(g + 1) * 128], sig[:, gg:gg + 1], identf_sb),
                    TS if g == 3 else None)
            ctx.wait("ve", tts)
            vt = ctx.op("ve", lambda e, bT=bT, h=h: e.tensor_copy(
                sigrow[0:1, h * 512:(h + 1) * 512], bT.ap[0:1, 0:512]), VS)
            bT.free = [vt]
            rowtoks.append(vt)
        sigb_tok = None
        for h in range(2):
            bb = fpool.acquire(ctx)
            ctx.wait("te", rowtoks[h])
            tb = ctx.op("te", lambda e, bb=bb, h=h: e.matmul(
                bb.ap[:, :], onesrow_sb, sigrow[0:1, h * 512:(h + 1) * 512],
                start=True, stop=True), TS)
            ctx.wait("ve", tb)
            sigb_tok = ctx.op("ve", lambda e, bb=bb, h=h: e.tensor_copy(
                sigb[:, h * 512:(h + 1) * 512], bb.ap[:, :]), VS)
            bb.free = [sigb_tok]
        st["sigb_done"] = sigb_tok

    def emit_unit(ctx, st, fpool, zt_ap, zt_done, w1src, w3src, w2src,
                  scale_ap, scale_tok, acc_init, he_state):
        W1, W3, W2S = st["W1"], st["W3"], st["W2S"]
        silu = st["sc2"]
        prod = st["ztmp"]
        he_ap = st["he"]
        acc_ap = st["acc"]
        ctx.waits("ve", he_state["free"])
        v_tok = None
        t_h3 = None
        for jc in range(NIC):
            s1 = W1[jc % 2]
            s3 = W3[jc % 2]
            ctx.waits("gp", s1.free)
            s1.free = []
            s1.ready = ctx.dma("gp", lambda s1=s1, jc=jc: dict(
                out=s1.ap, in_=w1src(jc)), sem=s1.sem)
            ctx.waits("gp", s3.free)
            s3.free = []
            s3.ready = ctx.dma("gp", lambda s3=s3, jc=jc: dict(
                out=s3.ap, in_=w3src(jc)), sem=s3.sem)
            bh1 = [fpool.acquire(ctx), fpool.acquire(ctx)]
            bh3 = [fpool.acquire(ctx), fpool.acquire(ctx)]
            ctx.wait("te", s1.ready)
            ctx.wait("te", s3.ready)
            ctx.wait("te", zt_done)
            t_h1 = None
            for dd in range(NDD):
                for th in range(2):
                    t_h1 = ctx.op("te", lambda e, dd=dd, th=th, s1=s1: e.matmul(
                        bh1[th].ap[:, :], s1.ap[:, dd * 128:(dd + 1) * 128],
                        zt_ap[:, dd * 1024 + th * 512:dd * 1024 + (th + 1) * 512],
                        start=(dd == 0), stop=(dd == NDD - 1)),
                        TS if (dd == NDD - 1 and th == 1) else None)
                for th in range(2):
                    t_h3 = ctx.op("te", lambda e, dd=dd, th=th, s3=s3: e.matmul(
                        bh3[th].ap[:, :], s3.ap[:, dd * 128:(dd + 1) * 128],
                        zt_ap[:, dd * 1024 + th * 512:dd * 1024 + (th + 1) * 512],
                        start=(dd == 0), stop=(dd == NDD - 1)),
                        TS if (dd == NDD - 1 and th == 1) else None)
            s1.free = [t_h1]
            s3.free = [t_h3]
            for th in range(2):
                sl = silu[th]
                ctx.wait("sc", t_h1)
                ctx.waits("sc", sl.free)
                sl.free = []
                s_tok = ctx.op("sc", lambda e, sl=sl, th=th: e.activation(
                    sl.ap[:], bh1[th].ap[:, :], AF.Silu), SS)
                bh1[th].free = [s_tok]
                ctx.wait("ve", s_tok)
                ctx.wait("ve", t_h3)
                ctx.wait("ve", scale_tok)
                ctx.op("ve", lambda e, sl=sl, th=th: e.tensor_mul(
                    prod, sl.ap[:], bh3[th].ap[:, :]))
                v_tok = ctx.op("ve", lambda e, jc=jc, th=th: e.tensor_mul(
                    he_ap[:, jc * 1024 + th * 512:jc * 1024 + (th + 1) * 512],
                    prod, scale_ap[:, th * 512:(th + 1) * 512]), VS)
                sl.free = [v_tok]
                bh3[th].free = [v_tok]
        he_ready = v_tok
        st["last_hidden"] = t_h3
        t_w2 = None
        a_tok = None
        for dd in range(NDD):
            s2 = W2S[dd % 2]
            ctx.waits("gp", s2.free)
            s2.free = []
            s2.ready = ctx.dma("gp", lambda s2=s2, dd=dd: dict(
                out=s2.ap, in_=w2src(dd)), sem=s2.sem)
            b0 = fpool.acquire(ctx)
            b1 = fpool.acquire(ctx)
            ctx.wait("te", s2.ready)
            ctx.wait("te", he_ready)
            for ic in range(NIC):
                for th, bb in ((0, b0), (1, b1)):
                    t_w2 = ctx.op("te", lambda e, ic=ic, th=th, bb=bb, s2=s2: e.matmul(
                        bb.ap[:, :], s2.ap[:, ic * 128:(ic + 1) * 128],
                        he_ap[:, ic * 1024 + th * 512:ic * 1024 + (th + 1) * 512],
                        start=(ic == 0), stop=(ic == NIC - 1)),
                        TS if (ic == NIC - 1 and th == 1) else None)
            s2.free = [t_w2]
            ctx.wait("ve", t_w2)
            for th, bb in ((0, b0), (1, b1)):
                dst = acc_ap[:, dd * 1024 + th * 512:dd * 1024 + (th + 1) * 512]
                if acc_init:
                    a_tok = ctx.op("ve", lambda e, dst=dst, bb=bb: e.tensor_copy(
                        dst, bb.ap[:, :]), VS)
                else:
                    a_tok = ctx.op("ve", lambda e, dst=dst, bb=bb: e.tensor_add(
                        dst, dst, bb.ap[:, :]), VS)
                bb.free = [a_tok]
        he_state["free"] = [t_w2]
        return a_tok

    def emit_cw_bcast(ctx, st, fpool, e_idx, slot):
        cwT = st["cwT"]
        cwrow = st["cwrow"]
        ctx.waits("sy", cwrow.free)
        cwrow.free = []
        ctx.wait("sy", st["cwT_done"])
        rtok = ctx.dma("sy", lambda e_idx=e_idx: dict(
            out=cwrow.ap[0:1, :], in_=cwT[e_idx:e_idx + 1, :]), sem=cwrow.sem)
        toks = []
        for h in range(2):
            bb = fpool.acquire(ctx)
            ctx.wait("te", rtok)
            tb = ctx.op("te", lambda e, bb=bb, h=h: e.matmul(
                bb.ap[:, :], onesrow_sb, cwrow.ap[0:1, h * 512:(h + 1) * 512],
                start=True, stop=True), TS)
            ctx.waits("ve", slot.free)
            slot.free = []
            ctx.wait("ve", tb)
            vt = ctx.op("ve", lambda e, bb=bb, h=h, slot=slot: e.tensor_copy(
                slot.ap[:, h * 512:(h + 1) * 512], bb.ap[:, :]), VS)
            bb.free = [vt]
            toks.append(vt)
        cwrow.free = [toks[1]]
        slot.ready = toks[1]

    def ffn_phase(blk, ysrc, ydst, final_mode):
        pes = ExitStack()

        def psb(name, shape, dtype):
            return pes.enter_context(nc.sbuf_tensor(name, shape, dtype)).ap()

        tag = "f1" if final_mode == "store" else "f2"
        zt_ap = psb(tag + "_zt", [128, NDD * 1024], BF16)
        acc_ap = psb(tag + "_acc", [128, NDD * 1024], F32)
        he_ap = psb(tag + "_he", [128, NIC * 1024], BF16)
        w1aps = [psb(tag + "_w1%d" % i, [128, NDD * 128], BF16) for i in range(2)]
        w3aps = [psb(tag + "_w3%d" % i, [128, NDD * 128], BF16) for i in range(2)]
        w2aps = [psb(tag + "_w2%d" % i, [128, NIC * 128], BF16) for i in range(2)]
        yaps = [psb(tag + "_y%d" % i, [128, 512], F32) for i in range(4)]
        sc2aps = [psb(tag + "_sc2%d" % i, [128, 512], F32) for i in range(2)]
        z32aps = [psb(tag + "_z32%d" % i, [128, 512], F32) for i in range(2)]
        ztmp = psb(tag + "_ztmp", [128, 512], F32)
        row_m = psb(tag + "_rowm", [1, 512], F32)
        row_r = psb(tag + "_rowr", [1, 512], F32)
        row_t = psb(tag + "_rowt", [1, 512], F32)
        mb = psb(tag + "_mb", [128, 512], F32)
        rb = psb(tag + "_rb", [128, 512], F32)
        gl = psb(tag + "_gl", [128, 17], F32)
        pp = psb(tag + "_pp", [128, 16], F32)
        pn = psb(tag + "_pn", [128, 16], F32)
        cwv = psb(tag + "_cwv", [128, 16], F32)
        msk = psb(tag + "_msk", [128, 16], F32)
        col = psb(tag + "_col", [128, 1], F32)
        col2 = psb(tag + "_col2", [128, 1], F32)
        sg = psb(tag + "_sg", [128, 8], F32)
        sig = psb(tag + "_sig", [128, 8], F32)
        sigrow = psb(tag + "_sigrow", [1, 1024], F32)
        sigb = psb(tag + "_sigb", [128, 1024], F32)
        cwT = psb(tag + "_cwT", [16, 1024], F32)
        cwrow_ap = psb(tag + "_cwrow", [1, 1024], F32)
        cwbaps = [psb(tag + "_cwb%d" % i, [128, 1024], F32) for i in range(2)]
        yo_aps = [psb(tag + "_yo%d" % i, [128, 512], F32) for i in range(2)]
        pc_aps = [psb(tag + "_pc%d" % i, [128, 128], F32) for i in range(2)]
        ysems = [B.new_sem(tag + "_sy%d" % i) for i in range(4)]
        w1sems = [B.new_sem(tag + "_sw1%d" % i) for i in range(2)]
        w3sems = [B.new_sem(tag + "_sw3%d" % i) for i in range(2)]
        w2sems = [B.new_sem(tag + "_sw2%d" % i) for i in range(2)]
        cwrsem = B.new_sem(tag + "_scwr")
        yosems = [B.new_sem(tag + "_syo%d" % i) for i in range(2)]
        pcsems = [B.new_sem(tag + "_spc%d" % i) for i in range(2)]

        def body(ctx):
            ctx.barrier(ctx.entry)
            fpool = Pool([Bank(a) for a in FBaps])
            st = {
                "y_sl": [Slot(a, sm) for a, sm in zip(yaps, ysems)],
                "sc2": [Slot(a) for a in sc2aps],
                "z32": [Slot(a) for a in z32aps],
                "ztmp": ztmp, "row_m": row_m, "row_r": row_r, "row_t": row_t,
                "mb": mb, "rb": rb, "gl": gl, "pp": pp, "pn": pn, "cw": cwv,
                "msk": msk, "col": col, "col2": col2, "sg": sg, "sig": sig,
                "sigrow": sigrow, "sigb": sigb, "cwT": cwT,
                "cwrow": Slot(cwrow_ap, cwrsem),
                "W1": [Slot(a, sm) for a, sm in zip(w1aps, w1sems)],
                "W3": [Slot(a, sm) for a, sm in zip(w3aps, w3sems)],
                "W2S": [Slot(a, sm) for a, sm in zip(w2aps, w2sems)],
                "he": he_ap, "acc": acc_ap,
            }
            cwb = [Slot(a) for a in cwbaps]
            yo = [Slot(a, sm) for a, sm in zip(yo_aps, yosems)]
            pc = [Slot(a, sm) for a, sm in zip(pc_aps, pcsems)]
            he_state = {"free": []}
            for s in range(2):
                if s > 0:
                    ctx.barrier([(x, x.n) for x in B.sems])
                t0 = s * ST
                zt_done = emit_ln(ctx, st, fpool, ysrc, t0, ST, zt_ap, 1024, 0,
                                  gate=True)
                emit_sig(ctx, st, fpool)
                a_tok = None
                for q in range(4):
                    a_tok = emit_unit(
                        ctx, st, fpool, zt_ap, zt_done,
                        w1src=lambda jc, q=q: sw1[q * NIC + jc],
                        w3src=lambda jc, q=q: sw3[q * NIC + jc],
                        w2src=lambda dd, q=q: sw2[q, dd],
                        scale_ap=sigb, scale_tok=st["sigb_done"],
                        acc_init=(q == 0), he_state=he_state)
                for ei in range(NE):
                    slot = cwb[ei % 2]
                    emit_cw_bcast(ctx, st, fpool, ei, slot)
                    a_tok = emit_unit(
                        ctx, st, fpool, zt_ap, zt_done,
                        w1src=lambda jc, ei=ei: w1[ei, jc],
                        w3src=lambda jc, ei=ei: w3[ei, jc],
                        w2src=lambda dd, ei=ei: w2[ei, dd],
                        scale_ap=slot.ap, scale_tok=slot.ready,
                        acc_init=False, he_state=he_state)
                    slot.free = [a_tok]
                acc_done = a_tok
                st["zt_free"] = [st["last_hidden"]]
                for dd in range(NDD):
                    for th in range(2):
                        tabs = t0 + th * 512
                        sl = st["y_sl"][(dd * 2 + th) % 4]
                        ctx.waits("sy", sl.free)
                        sl.free = []
                        sl.ready = ctx.dma("sy", lambda dd=dd, tabs=tabs, sl=sl: dict(
                            out=sl.ap[:], in_=ysrc[dd * 128:(dd + 1) * 128, tabs:tabs + 512]),
                            sem=sl.sem)
                        o = yo[(dd * 2 + th) % 2]
                        ctx.wait("ve", sl.ready)
                        ctx.wait("ve", acc_done)
                        ctx.waits("ve", o.free)
                        o.free = []
                        votok = ctx.op("ve", lambda e, o=o, dd=dd, th=th, sl=sl:
                                       e.scalar_tensor_tensor(
                                           o.ap[:],
                                           acc_ap[:, dd * 1024 + th * 512:dd * 1024 + (th + 1) * 512],
                                           0.5, sl.ap[:], ALU.mult, ALU.add), VS)
                        sl.free = [votok]
                        if final_mode == "store":
                            ctx.wait("sy", votok)
                            stok = ctx.dma("sy", lambda o=o, dd=dd, tabs=tabs: dict(
                                out=ydst[dd * 128:(dd + 1) * 128, tabs:tabs + 512],
                                in_=o.ap[:]), sem=o.sem)
                            o.free = [stok]
                        else:
                            lastt = votok
                            for sub in range(4):
                                bT = fpool.acquire(ctx)
                                ctx.wait("te", votok)
                                ttok = ctx.op("te", lambda e, bT=bT, o=o, sub=sub: e.transpose(
                                    bT.ap[:, 0:128], o.ap[:, sub * 128:(sub + 1) * 128],
                                    identf_sb), TS)
                                p = pc[(dd * 8 + th * 4 + sub) % 2]
                                ctx.waits("ve", p.free)
                                p.free = []
                                ctx.wait("ve", ttok)
                                pvt = ctx.op("ve", lambda e, p=p, bT=bT: e.tensor_copy(
                                    p.ap[:], bT.ap[:, 0:128]), VS)
                                bT.free = [pvt]
                                ctx.wait("sy", pvt)
                                pstok = ctx.dma("sy", lambda p=p, tabs=tabs, sub=sub, dd=dd: dict(
                                    out=out[tabs + sub * 128:tabs + (sub + 1) * 128,
                                            dd * 128:(dd + 1) * 128],
                                    in_=p.ap[:]), sem=p.sem)
                                p.free = [pstok]
                                lastt = ttok
                            o.free = [lastt]

        B.phase(blk, body)
        pes.close()

    def qkv_phase(blk):
        pes = ExitStack()

        def psb(name, shape, dtype):
            return pes.enter_context(nc.sbuf_tensor(name, shape, dtype)).ap()

        zt_ap = psb("qk_zt", [128, NDD * 2048], BF16)
        waps = [psb("qk_w%d" % i, [128, NDD * 128], BF16) for i in range(2)]
        qaps = [psb("qk_q%d" % i, [128, 2048], BF16) for i in range(2)]
        yaps = [psb("qk_y%d" % i, [128, 512], F32) for i in range(4)]
        sc2aps = [psb("qk_sc2%d" % i, [128, 512], F32) for i in range(2)]
        z32aps = [psb("qk_z32%d" % i, [128, 512], F32) for i in range(2)]
        ztmp = psb("qk_ztmp", [128, 512], F32)
        row_m = psb("qk_rowm", [1, 512], F32)
        row_r = psb("qk_rowr", [1, 512], F32)
        row_t = psb("qk_rowt", [1, 512], F32)
        mb = psb("qk_mb", [128, 512], F32)
        rb = psb("qk_rb", [128, 512], F32)
        qysems = [B.new_sem("qk_sy%d" % i) for i in range(4)]
        qwsems = [B.new_sem("qk_sw%d" % i) for i in range(2)]
        qssems = [B.new_sem("qk_sq%d" % i) for i in range(2)]

        def body(ctx):
            ctx.barrier(ctx.entry)
            fpool = Pool([Bank(a) for a in FBaps])
            st = {
                "y_sl": [Slot(a, sm) for a, sm in zip(yaps, qysems)],
                "sc2": [Slot(a) for a in sc2aps],
                "z32": [Slot(a) for a in z32aps],
                "ztmp": ztmp, "row_m": row_m, "row_r": row_r, "row_t": row_t,
                "mb": mb, "rb": rb,
            }
            zt_done = None
            for half in range(2):
                zt_done = emit_ln(ctx, st, fpool, y1H, half * 1024, 1024, zt_ap,
                                  2048, half * 1024, gate=False)
            W = [Slot(a, sm) for a, sm in zip(waps, qwsems)]
            Q = [Slot(a, sm) for a, sm in zip(qaps, qssems)]
            last_stores = {}
            for ec in range(NEC):
                s = W[ec % 2]
                ctx.waits("gp", s.free)
                s.free = []
                s.ready = ctx.dma("gp", lambda s=s, ec=ec: dict(
                    out=s.ap, in_=wattnT[ec]), sem=s.sem)
                banks = [fpool.acquire(ctx) for _ in range(4)]
                ctx.wait("te", s.ready)
                ctx.wait("te", zt_done)
                tq = None
                for dd in range(NDD):
                    for ti in range(4):
                        tq = ctx.op("te", lambda e, dd=dd, ti=ti, s=s: e.matmul(
                            banks[ti].ap[:, :], s.ap[:, dd * 128:(dd + 1) * 128],
                            zt_ap[:, dd * 2048 + ti * 512:dd * 2048 + (ti + 1) * 512],
                            start=(dd == 0), stop=(dd == NDD - 1)),
                            TS if (dd == NDD - 1 and ti == 3) else None)
                s.free = [tq]
                qs = Q[ec % 2]
                ctx.waits("sc", qs.free)
                qs.free = []
                ctx.wait("sc", tq)
                sq = None
                for ti in range(4):
                    sq = ctx.op("sc", lambda e, ti=ti, ec=ec, qs=qs: e.activation(
                        qs.ap[:, ti * 512:(ti + 1) * 512], banks[ti].ap[:, :],
                        AF.Identity, bias=battn_sb[:, ec:ec + 1]),
                        SS if ti == 3 else None)
                for ti in range(4):
                    banks[ti].free = [sq]
                ctx.wait("sy", sq)
                stok = ctx.dma("sy", lambda qs=qs, ec=ec: dict(
                    out=ginA[ec * 128:(ec + 1) * 128, :], in_=qs.ap[:]), sem=qs.sem)
                qs.free = [stok]
                last_stores[ec % 2] = stok
            for t in last_stores.values():
                ctx.wait("gp", t)
            ctx.op("gp", lambda e: e.collective_compute(
                "AllGather", ALU.bypass,
                replica_groups=[[0, 1, 2, 3], [4, 5, 6, 7]],
                ins=[ginA.opt()], outs=[goutA.opt()]), CC)

        B.phase(blk, body)
        pes.close()

    def attn_phase(blk):
        pes = ExitStack()

        def psb(name, shape, dtype):
            return pes.enter_context(nc.sbuf_tensor(name, shape, dtype)).ap()

        K_ap = psb("at_K", [128, S], BF16)
        Q_ap = psb("at_Q", [128, S], BF16)
        Vtm_ap = psb("at_V", [128, S], BF16)
        vf_aps = [psb("at_vf%d" % i, [128, 1024], BF16) for i in range(2)]
        scores_aps = [psb("at_sco%d" % i, [128, S], F32) for i in range(2)]
        probs_ap = psb("at_pr", [128, 4 * S], BF16)
        pT_aps = [psb("at_pT%d" % i, [128, 256], BF16) for i in range(2)]
        cx_aps = [psb("at_cx%d" % i, [128, 256], BF16) for i in range(2)]
        lrow = psb("at_lrow", [1, 2 * 256], F32)
        lcol = psb("at_lcol", [128, 4], F32)
        linvb_aps = [psb("at_linvb%d" % i, [128, 256], F32) for i in range(2)]
        mcol = psb("at_mcol", [128, 2], F32)
        ncol = psb("at_ncol", [128, 2], F32)
        scol = psb("at_scol", [128, 2], F32)
        kvqsem = B.new_sem("at_skvq")
        Ksem = B.new_sem("at_sK")
        Qsem = B.new_sem("at_sQ")
        vfsems = [B.new_sem("at_svf%d" % i) for i in range(2)]
        cxsems = [B.new_sem("at_scx%d" % i) for i in range(2)]

        g3 = goutA.rearrange("(r f) t -> r f t", r=4)
        g4 = goutA.rearrange("(r f) t -> f r t", r=4)

        def body(ctx):
            ctx.barrier(ctx.entry)
            cc_tok = (CC, CC.n)
            ctx.wait("sy", cc_tok)
            fpool = Pool([Bank(a) for a in FBaps])
            gpool = Pool([Bank(a) for a in GBaps])
            regs = {}

            def setup_regs(e):
                for idx in range(4):
                    regs[idx] = e.value_load(meta_sb[0:1, idx:idx + 1])
            ctx.run("sy", setup_regs)
            vf = [Slot(a, sm) for a, sm in zip(vf_aps, vfsems)]
            pT = [Slot(a) for a in pT_aps]
            cx = [Slot(a, sm) for a, sm in zip(cx_aps, cxsems)]
            K = Slot(K_ap, Ksem)
            Q = Slot(Q_ap, Qsem)
            Vtm = Slot(Vtm_ap)
            st_probs_free = [[[], []], [[], []]]
            st_lrow_free = [[], []]
            st_lcol_free = [[], []]
            last_cs = None

            for i in range(4):
                if i > 0:
                    ctx.barrier([(x, x.n) for x in B.sems])
                ctx.wait("sy", K.ready)
                ctx.wait("sy", Q.ready)
                for sl in vf:
                    ctx.wait("sy", sl.ready)
                hq_tok = ctx.dma("sy", lambda i=i: dict(
                    out=kvqH,
                    in_=g3[:, bass.ds(regs[i], 384), :]), sem=kvqsem)
                ctx.wait("sy", hq_tok)
                K.ready = ctx.dma("sy", lambda i=i: dict(
                    out=K_ap.rearrange("p (r t) -> p r t", r=4),
                    in_=kvqH.rearrange("r f t -> f r t")[128:256, :, :]), sem=K.sem)
                Q.ready = ctx.dma("sy", lambda i=i: dict(
                    out=Q_ap.rearrange("p (r t) -> p r t", r=4),
                    in_=kvqH.rearrange("r f t -> f r t")[0:128, :, :]), sem=Q.sem)
                vt_tok = None
                for c in range(8):
                    sl = vf[c % 2]
                    ctx.waits("sy", sl.free)
                    sl.free = []
                    sl.ready = ctx.dma("sy", lambda c=c, i=i, sl=sl: dict(
                        out=sl.ap[:],
                        in_=kvqH[c // 2, 256:384,
                                 (c % 2) * 1024:(c % 2 + 1) * 1024]), sem=sl.sem)
                    gb = gpool.acquire(ctx)
                    ctx.wait("te", sl.ready)
                    tv = None
                    for j in range(8):
                        tv = ctx.op("te", lambda e, gb=gb, sl=sl, j=j: e.transpose(
                            gb.ap[:, j * 128:(j + 1) * 128],
                            sl.ap[:, j * 128:(j + 1) * 128], identbf_sb),
                            TS if j == 7 else None)
                    sl.free = [tv]
                    ctx.wait("ve", tv)
                    vt_tok = ctx.op("ve", lambda e, gb=gb, c=c: e.tensor_copy(
                        Vtm_ap[:, c * 1024:(c + 1) * 1024], gb.ap[:, :]), VS)
                    gb.free = [vt_tok]
                Vtm.ready = vt_tok

                exp_toks = [[None, None], [None, None]]
                vB_toks = [None, None]
                probs_free = st_probs_free
                lrow_free = st_lrow_free
                lcol_free = st_lcol_free

                def stage_scores(qb):
                    par = qb % 2
                    scores_ap = scores_aps[par]
                    vL = None
                    for a in range(2):
                        qa = qb * 256 + a * 128
                        nk = qa + 128
                        ev_tok = None
                        nkb = (nk + 511) // 512
                        for kb in range(nkb):
                            n = min(512, nk - kb * 512)
                            bsc = fpool.acquire(ctx)
                            ctx.wait("te", K.ready)
                            ctx.wait("te", Q.ready)
                            tsc = ctx.op("te", lambda e, bsc=bsc, qa=qa, kb=kb, n=n: e.matmul(
                                bsc.ap[:, 0:n], Q_ap[:, qa:qa + 128],
                                K_ap[:, kb * 512:kb * 512 + n],
                                start=True, stop=True), TS)
                            ctx.wait("sc", tsc)
                            ev_tok = ctx.op("sc", lambda e, bsc=bsc, kb=kb, n=n,
                                            scores_ap=scores_ap: e.copy(
                                scores_ap[0:128, kb * 512:kb * 512 + n],
                                bsc.ap[:, 0:n]), SS)
                            bsc.free = [ev_tok]
                        ctx.wait("ve", ev_tok)
                        ctx.vop(lambda e, qa=qa, scores_ap=scores_ap: e.tensor_add(
                            scores_ap[0:128, qa:qa + 128],
                            scores_ap[0:128, qa:qa + 128], cmask_sb))
                        mx_tok = ctx.vop(lambda e, nk=nk, par=par, scores_ap=scores_ap:
                                         e.reduce_max(
                            mcol[:, par:par + 1], scores_ap[0:128, 0:nk],
                            axis=mybir.AxisListType.X))
                        ctx.wait("sc", mx_tok)
                        ctx.waits("sc", probs_free[par][a])
                        probs_free[par][a] = []
                        nstok = ctx.op("sc", lambda e, par=par: e.mul(
                            ncol[:, par:par + 1], mcol[:, par:par + 1], -SCALE), SS)
                        ctx.wait("sc", nstok)
                        ex_tok = ctx.op("sc", lambda e, a=a, nk=nk, par=par,
                                        scores_ap=scores_ap: e.activation(
                            probs_ap[:, (par * 2 + a) * S:(par * 2 + a) * S + nk],
                            scores_ap[0:128, 0:nk],
                            AF.Exp, bias=ncol[:, par:par + 1], scale=SCALE,
                            accum_out=scol[:, par:par + 1]), SS)
                        exp_toks[par][a] = ex_tok
                        ctx.wait("ve", ex_tok)
                        ctx.waits("ve", lcol_free[par])
                        lcol_free[par] = []
                        rc_tok = ctx.vop(lambda e, a=a, par=par: e.reciprocal(
                            lcol[:, par * 2 + a:par * 2 + a + 1],
                            scol[:, par:par + 1]))
                        bL = fpool.acquire(ctx)
                        ctx.wait("te", rc_tok)
                        tL = ctx.op("te", lambda e, bL=bL, a=a, par=par: e.transpose(
                            bL.ap[0:1, a * 128:(a + 1) * 128],
                            lcol[:, par * 2 + a:par * 2 + a + 1],
                            identf_sb), TS)
                        lcol_free[par].append(tL)
                        ctx.waits("ve", lrow_free[par])
                        lrow_free[par] = []
                        ctx.wait("ve", tL)
                        vL = ctx.vop(lambda e, bL=bL, a=a, par=par: e.tensor_copy(
                            lrow[0:1, par * 256 + a * 128:par * 256 + (a + 1) * 128],
                            bL.ap[0:1, a * 128:(a + 1) * 128]))
                        bL.free = [vL]
                    bB = fpool.acquire(ctx)
                    ctx.wait("te", vL)
                    tB = ctx.op("te", lambda e, bB=bB, par=par: e.matmul(
                        bB.ap[:, 0:256], onesrow_sb,
                        lrow[0:1, par * 256:par * 256 + 256],
                        start=True, stop=True), TS)
                    lrow_free[par] = [tB]
                    ctx.wait("ve", tB)
                    vB = ctx.op("ve", lambda e, bB=bB, par=par: e.tensor_copy(
                        linvb_aps[par], bB.ap[:, 0:256]), VS)
                    bB.free = [vB]
                    vB_toks[par] = vB

                def stage_pv(qb):
                    par = qb % 2
                    bctx = fpool.acquire(ctx)
                    nkt = 2 * qb + 2
                    nkA = qb * 256 + 128
                    pv_tok = None
                    for kt in range(nkt):
                        bothv = kt * 128 < nkA
                        gb = gpool.acquire(ctx)
                        ctx.wait("te", exp_toks[par][1])
                        if bothv:
                            ctx.wait("te", exp_toks[par][0])
                            ctx.op("te", lambda e, gb=gb, kt=kt, par=par: e.transpose(
                                gb.ap[:, 0:128],
                                probs_ap[:, (par * 2) * S + kt * 128:(par * 2) * S + (kt + 1) * 128],
                                identbf_sb))
                        tT = ctx.op("te", lambda e, gb=gb, kt=kt, par=par: e.transpose(
                            gb.ap[:, 128:256],
                            probs_ap[:, (par * 2 + 1) * S + kt * 128:(par * 2 + 1) * S + (kt + 1) * 128],
                            identbf_sb), TS)
                        p = pT[kt % 2]
                        ctx.waits("ve", p.free)
                        p.free = []
                        ctx.wait("ve", tT)
                        lo = 0 if bothv else 128
                        vT = ctx.op("ve", lambda e, p=p, gb=gb, lo=lo: e.tensor_copy(
                            p.ap[:, lo:256], gb.ap[:, lo:256]), VS)
                        gb.free = [vT]
                        ctx.wait("te", vT)
                        ctx.wait("te", Vtm.ready)
                        pv_tok = ctx.op("te", lambda e, p=p, kt=kt, lo=lo, bctx=bctx,
                                        nkt=nkt: e.matmul(
                            bctx.ap[:, lo:256], Vtm_ap[:, kt * 128:(kt + 1) * 128],
                            p.ap[:, lo:256],
                            start=(kt == 0), stop=(kt == nkt - 1)), TS)
                        p.free = [pv_tok]
                    probs_free[par][0] = [pv_tok]
                    probs_free[par][1] = [pv_tok]
                    c = cx[qb % 2]
                    ctx.waits("ve", c.free)
                    c.free = []
                    ctx.wait("ve", pv_tok)
                    ctx.wait("ve", vB_toks[par])
                    vC = ctx.op("ve", lambda e, c=c, bctx=bctx, par=par: e.tensor_mul(
                        c.ap[:], bctx.ap[:, 0:256], linvb_aps[par]), VS)
                    bctx.free = [vC]
                    ctx.wait("sy", vC)
                    cs_tok = ctx.dma("sy", lambda c=c, i=i, qb=qb: dict(
                        out=ginB[i * 128:(i + 1) * 128, qb * 256:(qb + 1) * 256],
                        in_=c.ap[:]), sem=c.sem)
                    c.free = [cs_tok]
                    return cs_tok, pv_tok

                for qb in range(33):
                    if qb < 32:
                        stage_scores(qb)
                    if qb >= 1:
                        cs_tok, pv_tok = stage_pv(qb - 1)
                        last_cs = {**(last_cs or {}), (qb - 1) % 2: cs_tok}
            for t in last_cs.values():
                ctx.wait("gp", t)
            ctx.op("gp", lambda e: e.collective_compute(
                "AllGather", ALU.bypass,
                replica_groups=[[0, 1, 2, 3], [4, 5, 6, 7]],
                ins=[ginB.opt()], outs=[goutB.opt()]), CC)

        B.phase(blk, body)
        pes.close()

    def proj_phase(blk):
        pes = ExitStack()

        def psb(name, shape, dtype):
            return pes.enter_context(nc.sbuf_tensor(name, shape, dtype)).ap()

        ctxf_ap = psb("pj_ctx", [128, NDD * 2048], BF16)
        waps = [psb("pj_w%d" % i, [128, NDD * 128], BF16) for i in range(2)]
        po_aps = [psb("pj_po%d" % i, [128, 512], F32) for i in range(2)]
        yaps = [psb("pj_y%d" % i, [128, 512], F32) for i in range(4)]
        y2aps = [psb("pj_y2%d" % i, [128, 512], F32) for i in range(2)]
        pctxsem = B.new_sem("pj_sctxH")
        pcfsem = B.new_sem("pj_sctxf")
        pwsems = [B.new_sem("pj_sw%d" % i) for i in range(2)]
        pysems = [B.new_sem("pj_sy%d" % i) for i in range(4)]
        py2sems = [B.new_sem("pj_sy2%d" % i) for i in range(2)]

        def body(ctx):
            ctx.barrier(ctx.entry)
            cc_tok = (CC, CC.n)
            ctx.wait("sy", cc_tok)
            fpool = Pool([Bank(a) for a in FBaps])
            regs = {}

            def setup_regs(e):
                regs[0] = e.value_load(meta_sb[0:1, 12:13])
            ctx.run("sy", setup_regs)
            cpy_tok = ctx.dma("sy", lambda: dict(
                out=ctxH, in_=goutB[:, bass.ds(regs[0], 2048)]), sem=pctxsem)
            ctx.wait("sy", cpy_tok)
            ctxf_tok = None
            for hc in range(NDD):
                ctxf_tok = ctx.dma("sy", lambda hc=hc: dict(
                    out=ctxf_ap[:, hc * 2048:(hc + 1) * 2048],
                    in_=ctxH[hc * 128:(hc + 1) * 128, :]), sem=pcfsem)
            W = [Slot(a, sm) for a, sm in zip(waps, pwsems)]
            po = [Slot(a) for a in po_aps]
            ysl = [Slot(a, sm) for a, sm in zip(yaps, pysems)]
            y2 = [Slot(a, sm) for a, sm in zip(y2aps, py2sems)]
            for dd in range(NDD):
                s = W[dd % 2]
                ctx.waits("gp", s.free)
                s.free = []
                s.ready = ctx.dma("gp", lambda s=s, dd=dd: dict(
                    out=s.ap, in_=wprojT[dd]), sem=s.sem)
                banks = [fpool.acquire(ctx) for _ in range(4)]
                ctx.wait("te", s.ready)
                ctx.wait("te", ctxf_tok)
                tp = None
                for hc in range(NDD):
                    for ti in range(4):
                        tp = ctx.op("te", lambda e, hc=hc, ti=ti, s=s: e.matmul(
                            banks[ti].ap[:, :], s.ap[:, hc * 128:(hc + 1) * 128],
                            ctxf_ap[:, hc * 2048 + ti * 512:hc * 2048 + (ti + 1) * 512],
                            start=(hc == 0), stop=(hc == NDD - 1)),
                            TS if (hc == NDD - 1 and ti == 3) else None)
                s.free = [tp]
                for ti in range(4):
                    o = po[ti % 2]
                    ctx.waits("sc", o.free)
                    o.free = []
                    ctx.wait("sc", tp)
                    sp = ctx.op("sc", lambda e, o=o, ti=ti, dd=dd: e.activation(
                        o.ap[:], banks[ti].ap[:, :], AF.Identity,
                        bias=bproj_sb[:, dd:dd + 1]), SS)
                    banks[ti].free = [sp]
                    sl = ysl[ti]
                    ctx.waits("sy", sl.free)
                    sl.free = []
                    sl.ready = ctx.dma("sy", lambda sl=sl, dd=dd, ti=ti: dict(
                        out=sl.ap[:],
                        in_=y1H[dd * 128:(dd + 1) * 128, ti * 512:(ti + 1) * 512]),
                        sem=sl.sem)
                    o2 = y2[ti % 2]
                    ctx.waits("ve", o2.free)
                    o2.free = []
                    ctx.wait("ve", sp)
                    ctx.wait("ve", sl.ready)
                    v2 = ctx.op("ve", lambda e, o2=o2, o=o, sl=sl: e.tensor_add(
                        o2.ap[:], o.ap[:], sl.ap[:]), VS)
                    o.free = [v2]
                    sl.free = [v2]
                    ctx.wait("sy", v2)
                    s2 = ctx.dma("sy", lambda o2=o2, dd=dd, ti=ti: dict(
                        out=y2H[dd * 128:(dd + 1) * 128, ti * 512:(ti + 1) * 512],
                        in_=o2.ap[:]), sem=o2.sem)
                    o2.free = [s2]

        B.phase(blk, body)
        pes.close()

    with nc.Block() as blk:
        def init_body(ctx):
            for dst, src in [
                (identbf_sb, identbf), (identf_sb, identf),
                (onescol_sb, onescol), (onesrow_sb, onesrow),
                (cmask_sb, cmaskD), (battn_sb, battnC),
                (bproj_sb, bprojC), (meta_sb, meta),
            ]:
                ctx.dma("sy", lambda dst=dst, src=src: dict(out=dst, in_=src))
            ctx.dma("sy", lambda: dict(
                out=gsg_sb.rearrange("p (dd c) -> p dd c", dd=NDD),
                in_=gate_sg.rearrange("(dd p) c -> p dd c", p=128)))
            ctx.op("ve", lambda e: e.memset(eps_sb, EPS), VS)

        B.phase(blk, init_body)
        ffn_phase(blk, xT, y1H, "store")
        qkv_phase(blk)
        attn_phase(blk)
        proj_phase(blk)
        ffn_phase(blk, y2H, None, "out")

        def fin_body(ctx):
            ctx.barrier(ctx.entry)
            for dap, sap in dump_specs:
                ctx.dma("sy", lambda dap=dap, sap=sap: dict(out=dap, in_=sap))
            ctx.wait("sy", (B.SY, B.SY.n))

        B.phase(blk, fin_body)

    es.close()
    return nc


def _host_inputs(L0_attn_x, L0_attn_w_attn, L0_attn_b_attn, L0_attn_w_proj,
                 L0_attn_b_proj, L0_ffn_gate_w, L0_ffn_w1, L0_ffn_w3, L0_ffn_w2,
                 L0_ffn_s_w1, L0_ffn_s_w3, L0_ffn_s_w2, L0_ffn_sg_w):
    f = np.asarray
    x = f(L0_attn_x, dtype=np.float32)
    def slab_kd(w):
        K, M = w.shape
        return np.ascontiguousarray(
            w.reshape(K // 128, 128, M // 128, 128).transpose(2, 1, 0, 3)
            .reshape(M // 128, 128, K))

    w1f = f(L0_ffn_w1, dtype=np.float32)
    w3f = f(L0_ffn_w3, dtype=np.float32)
    w2f = f(L0_ffn_w2, dtype=np.float32)
    consts = dict(
        wattnT=slab_kd(f(L0_attn_w_attn, dtype=np.float32).T),
        battnC=np.ascontiguousarray(f(L0_attn_b_attn, dtype=np.float32).reshape(NEC, 128).T),
        wprojT=slab_kd(f(L0_attn_w_proj, dtype=np.float32).T),
        bprojC=np.ascontiguousarray(f(L0_attn_b_proj, dtype=np.float32).reshape(NDD, 128).T),
        gate_sg=np.ascontiguousarray(np.concatenate(
            [f(L0_ffn_gate_w, dtype=np.float32), f(L0_ffn_sg_w, dtype=np.float32)],
            axis=1)),
        w1=np.stack([slab_kd(w1f[e]) for e in range(NE)]),
        w3=np.stack([slab_kd(w3f[e]) for e in range(NE)]),
        w2=np.stack([slab_kd(w2f[e]) for e in range(NE)]),
        sw1=slab_kd(f(L0_ffn_s_w1, dtype=np.float32)),
        sw3=slab_kd(f(L0_ffn_s_w3, dtype=np.float32)),
        sw2=np.stack([slab_kd(f(L0_ffn_s_w2, dtype=np.float32)[q * IH:(q + 1) * IH, :])
                      for q in range(4)]),
        identbf=np.eye(128, dtype=BF),
        identf=np.eye(128, dtype=np.float32),
        onescol=np.ones((128, 1), np.float32),
        onesrow=np.ones((1, 128), np.float32),
        cmaskD=np.where(np.arange(128)[:, None] >= np.arange(128)[None, :],
                        np.float32(0.0), np.float32(BIGNEG)).astype(np.float32),
    )
    in_maps = []
    for core in range(8):
        b, q = core // 4, core % 4
        m = np.zeros((1, 16), np.int32)
        hb = q * 4 * 384
        for i in range(4):
            m[0, 0 + i] = hb + i * 384
        m[0, 12] = q * TC
        d = dict(consts)
        d["xT"] = np.ascontiguousarray(x[b, q * TC:(q + 1) * TC, :].T)
        d["meta"] = m
        in_maps.append(d)
    return in_maps


def kernel(**inputs):
    in_maps = _host_inputs(**inputs)
    nc = build_program()
    res = run_bass_kernel_spmd(nc, in_maps, core_ids=list(range(8)))
    y = np.empty((2, S, D), np.float32)
    for core in range(8):
        b, q = core // 4, core % 4
        y[b, q * TC:(q + 1) * TC, :] = res.results[core]["out"]
    return y


# version 11
# speedup vs baseline: 1.0184x; 1.0171x over previous
import numpy as np
import ml_dtypes
from contextlib import ExitStack

import concourse.bass as bass
import concourse.mybir as mybir
from concourse.bass_utils import run_bass_kernel_spmd

F32 = mybir.dt.float32
BF16 = mybir.dt.bfloat16
I32 = mybir.dt.int32
AF = mybir.ActivationFunctionType
ALU = mybir.AluOpType
BF = ml_dtypes.bfloat16

D = 2048
TC = 2048
ST = 1024
NDD = D // 128
NE = 16
IH = 1408
NIC = IH // 128
SHID = 5632
NJC = SHID // 128
QKVF = 3 * D
NEC = QKVF // 128
S = 8192
HD = 128
SCALE = 1.0 / float(np.sqrt(HD))
BIGNEG = -1.0e9
EPS = 1e-5

ENGS = ("sy", "gp", "te", "ve", "sc")
import os
ONLY_SC_EVAC = bool(os.environ.get("ONLY_SC_EVAC"))


class Sem:
    __slots__ = ("h", "n", "name")

    def __init__(self, es, nc, name):
        self.h = es.enter_context(nc.semaphore(name))
        self.n = 0
        self.name = name


class Eng:
    def __init__(self, name):
        self.name = name
        self.wm = {}
        self.e = None

    def wait(self, tok):
        if tok is None:
            return
        sem, n = tok
        if n <= 0:
            return
        if self.wm.get(sem.name, 0) < n:
            self.e.wait_ge(sem.h, n)
            self.wm[sem.name] = n


class Slot:
    __slots__ = ("ap", "free", "ready", "sem")

    def __init__(self, ap, sem=None):
        self.ap = ap
        self.free = []
        self.ready = None
        self.sem = sem


class Bank(Slot):
    pass


class Pool:
    def __init__(self, banks):
        self.banks = banks
        self.i = 0

    def acquire(self, ctx):
        b = self.banks[self.i % len(self.banks)]
        self.i += 1
        ctx.waits("te", b.free)
        b.free = []
        return b


class Ctx:
    def __init__(self, B, mode):
        self.B = B
        self.mode = mode
        self.entry = None

    def wait(self, eng, tok):
        if self.mode == eng:
            self.B.E[eng].wait(tok)

    def waits(self, eng, toks):
        for t in toks:
            self.wait(eng, t)

    def dma(self, q, thunk, sem=None):
        if sem is None:
            sem = self.B.SY if q == "sy" else self.B.SW
        if self.mode == q:
            kw = thunk()
            self.B.E[q].e.dma_start(**kw).then_inc(sem.h, 16)
        sem.n += 16
        return (sem, sem.n)

    def op(self, eng, fn, sem=None):
        instr = None
        if self.mode == eng:
            instr = fn(self.B.E[eng].e)
        if sem is not None:
            if instr is not None:
                instr.then_inc(sem.h, 1)
            sem.n += 1
            return (sem, sem.n)
        return None

    def run(self, eng, fn):
        if self.mode == eng:
            return fn(self.B.E[eng].e)
        return None

    def vop(self, fn):
        tok = self.op("ve", fn, self.B.VS)
        self.wait("ve", tok)
        return tok

    def barrier(self, toks):
        for t in toks:
            self.B.E[self.mode].wait(t)


class Builder:
    def __init__(self, nc, es):
        self.nc = nc
        self.es = es
        self.SY = Sem(es, nc, "SY")
        self.SW = Sem(es, nc, "SW")
        self.CC = Sem(es, nc, "CC")
        self.TS = Sem(es, nc, "TS")
        self.VS = Sem(es, nc, "VS")
        self.SS = Sem(es, nc, "SS")
        self.sems = [self.SY, self.SW, self.CC, self.TS, self.VS, self.SS]
        self.E = {n: Eng(n) for n in ENGS}

    def new_sem(self, name):
        s = Sem(self.es, self.nc, name)
        self.sems.append(s)
        return s

    def toks(self):
        return [(s, s.n) for s in self.sems]

    def phase(self, blk, body):
        entry = self.toks()
        snap = [s.n for s in self.sems]
        final = None
        for mode in ENGS:
            for s, n0 in zip(self.sems, snap):
                s.n = n0
            ctx = Ctx(self, mode)
            ctx.entry = entry

            def run_body(e, mode=mode, ctx=ctx):
                self.E[mode].e = e
                body(ctx)

            getattr_map = {"sy": blk.sync, "gp": blk.gpsimd, "te": blk.tensor,
                           "ve": blk.vector, "sc": blk.scalar}
            getattr_map[mode](run_body)
            end = [s.n for s in self.sems]
            if final is None:
                final = end
            else:
                assert final == end, ("phase replay divergence", final, end)
        for s, n in zip(self.sems, final):
            s.n = n


def build_program(dumps=()):
    nc = bass.Bass("TRN2", target_bir_lowering=False, debug=False)

    xT = nc.dram_tensor("xT", [D, TC], F32, kind="ExternalInput").ap()
    wattnT = nc.dram_tensor("wattnT", [NEC, 128, D], F32, kind="ExternalInput").ap()
    battnC = nc.dram_tensor("battnC", [128, NEC], F32, kind="ExternalInput").ap()
    wprojT = nc.dram_tensor("wprojT", [NDD, 128, D], F32, kind="ExternalInput").ap()
    bprojC = nc.dram_tensor("bprojC", [128, NDD], F32, kind="ExternalInput").ap()
    gate_sg = nc.dram_tensor("gate_sg", [D, 17], F32, kind="ExternalInput").ap()
    w1 = nc.dram_tensor("w1", [NE, NIC, 128, D], F32, kind="ExternalInput").ap()
    w3 = nc.dram_tensor("w3", [NE, NIC, 128, D], F32, kind="ExternalInput").ap()
    w2 = nc.dram_tensor("w2", [NE, NDD, 128, IH], F32, kind="ExternalInput").ap()
    sw1 = nc.dram_tensor("sw1", [NJC, 128, D], F32, kind="ExternalInput").ap()
    sw3 = nc.dram_tensor("sw3", [NJC, 128, D], F32, kind="ExternalInput").ap()
    sw2 = nc.dram_tensor("sw2", [4, NDD, 128, IH], F32, kind="ExternalInput").ap()
    identbf = nc.dram_tensor("identbf", [128, 128], BF16, kind="ExternalInput").ap()
    identf = nc.dram_tensor("identf", [128, 128], F32, kind="ExternalInput").ap()
    onescol = nc.dram_tensor("onescol", [128, 1], F32, kind="ExternalInput").ap()
    onesrow = nc.dram_tensor("onesrow", [1, 128], F32, kind="ExternalInput").ap()
    cmaskD = nc.dram_tensor("cmaskD", [128, 128], F32, kind="ExternalInput").ap()
    meta = nc.dram_tensor("meta", [1, 16], I32, kind="ExternalInput").ap()

    out = nc.dram_tensor("out", [TC, D], F32, kind="ExternalOutput").ap()

    y1H = nc.dram_tensor("y1H", [D, TC], F32, kind="Internal").ap()
    y2H = nc.dram_tensor("y2H", [D, TC], F32, kind="Internal").ap()
    ginA = nc.dram_tensor("ginA", [QKVF, TC], BF16, kind="Internal").ap()
    goutA = nc.dram_tensor("goutA", [4 * QKVF, TC], BF16, kind="Internal").ap()
    ginB = nc.dram_tensor("ginB", [4 * HD, S], BF16, kind="Internal").ap()
    goutB = nc.dram_tensor("goutB", [D, S], BF16, kind="Internal").ap()
    kvqH = nc.dram_tensor("kvqH", [2, 4, 384, TC], BF16, kind="Internal").ap()
    ctxH = nc.dram_tensor("ctxH", [D, TC], BF16, kind="Internal").ap()

    dump_specs = []
    srcs = {"y1H": y1H, "y2H": y2H, "ginA": ginA, "goutA": goutA,
            "ginB": ginB, "goutB": goutB, "kvqH": kvqH, "ctxH": ctxH}
    for name in dumps:
        sap = srcs[name]
        dap = nc.dram_tensor("dump_" + name, list(sap.tensor.shape),
                             sap.tensor.dtype, kind="ExternalOutput").ap()
        dump_specs.append((dap, sap))

    es = ExitStack()
    B = Builder(nc, es)
    SY, SW, CC, TS, VS, SS = B.SY, B.SW, B.CC, B.TS, B.VS, B.SS

    def sbuf(name, shape, dtype):
        return es.enter_context(nc.sbuf_tensor(name, shape, dtype)).ap()

    identbf_sb = sbuf("c_identbf", [128, 128], BF16)
    identf_sb = sbuf("c_identf", [128, 128], F32)
    onescol_sb = sbuf("c_onescol", [128, 1], F32)
    onesrow_sb = sbuf("c_onesrow", [1, 128], F32)
    cmask_sb = sbuf("c_cmask", [128, 128], F32)
    battn_sb = sbuf("c_battn", [128, NEC], F32)
    bproj_sb = sbuf("c_bproj", [128, NDD], F32)
    gsg_sb = sbuf("c_gsg", [128, NDD * 17], F32)
    meta_sb = sbuf("c_meta", [1, 16], I32)
    eps_sb = sbuf("c_eps", [1, 1], F32)

    FBaps = [es.enter_context(nc.psum_tensor("psf%d" % i, [128, 512], F32)).ap()
             for i in range(6)]
    GBaps = [es.enter_context(nc.psum_tensor("psg%d" % i, [128, 1024], BF16)).ap()
             for i in range(2)]

    def emit_ln(ctx, st, fpool, ysrc, tok0, ntok, zt_ap, zt_stride, zt_off, gate):
        y_sl = st["y_sl"]
        sc2 = st["sc2"]
        z32 = st["z32"]
        ztmp = st["ztmp"]
        row_m, row_r, row_t = st["row_m"], st["row_r"], st["row_t"]
        mb_sb, rb_sb = st["mb"], st["rb"]
        rf = st.setdefault("row_free", {"m": [], "r": [], "t": []})
        zt_free = st.get("zt_free", [])
        zt_done = None
        ntt = ntok // 512
        for tt in range(ntt):
            tabs = tok0 + tt * 512
            tloc = zt_off + tt * 512
            b_sum = fpool.acquire(ctx)
            b_sq = fpool.acquire(ctx)
            sum_tok = sumsq_tok = None
            for dd in range(NDD):
                sl = y_sl[dd % 4]
                ctx.waits("sy", sl.free)
                sl.free = []
                sl.ready = ctx.dma("sy", lambda dd=dd, tabs=tabs, sl=sl: dict(
                    out=sl.ap[:], in_=ysrc[dd * 128:(dd + 1) * 128, tabs:tabs + 512]),
                    sem=sl.sem)
                sq = sc2[dd % 2]
                ctx.wait("sc", sl.ready)
                ctx.waits("sc", sq.free)
                sq.free = []
                sq.ready = ctx.op("sc", lambda e, sq=sq, sl=sl: e.square(sq.ap[:], sl.ap[:]), SS)
                ctx.wait("te", sl.ready)
                t1 = ctx.op("te", lambda e, sl=sl, dd=dd, b=b_sum: e.matmul(
                    b.ap[0:1, :], onescol_sb, sl.ap[:],
                    start=(dd == 0), stop=(dd == NDD - 1)), TS)
                ctx.wait("te", sq.ready)
                t2 = ctx.op("te", lambda e, sq=sq, dd=dd, b=b_sq: e.matmul(
                    b.ap[0:1, :], onescol_sb, sq.ap[:],
                    start=(dd == 0), stop=(dd == NDD - 1)), TS)
                sl.free = [t1, sq.ready]
                sq.free = [t2]
                sum_tok, sumsq_tok = t1, t2
            ctx.wait("sc", sum_tok)
            ctx.waits("sc", rf["m"])
            rf["m"] = []
            mtok = ctx.op("sc", lambda e: e.mul(row_m[0:1, :], b_sum.ap[0:1, :], 1.0 / D), SS)
            ctx.wait("sc", sumsq_tok)
            ctx.waits("sc", rf["r"])
            rf["r"] = []
            qtok = ctx.op("sc", lambda e: e.mul(row_r[0:1, :], b_sq.ap[0:1, :], 1.0 / D), SS)
            b_sum.free = [mtok]
            b_sq.free = [qtok]
            ctx.wait("ve", qtok)
            ctx.waits("ve", rf["t"])
            rf["t"] = []
            ctx.vop(lambda e: e.tensor_mul(row_t[0:1, :], row_m[0:1, :], row_m[0:1, :]))
            vtok = ctx.vop(lambda e: e.tensor_sub(row_r[0:1, :], row_r[0:1, :],
                                                  row_t[0:1, :]))
            ctx.wait("sc", vtok)
            lntok = ctx.op("sc", lambda e: e.activation(row_t[0:1, :], row_r[0:1, :], AF.Ln,
                                                        bias=eps_sb[0:1, 0:1]), SS)
            ctx.wait("sc", lntok)
            rtok = ctx.op("sc", lambda e: e.activation(row_r[0:1, :], row_t[0:1, :], AF.Exp,
                                                       scale=-0.5), SS)
            b_mb = fpool.acquire(ctx)
            b_rb = fpool.acquire(ctx)
            ctx.wait("te", mtok)
            tmb = ctx.op("te", lambda e: e.matmul(b_mb.ap[:, :], onesrow_sb, row_m[0:1, :],
                                                  start=True, stop=True), TS)
            ctx.wait("te", rtok)
            trb = ctx.op("te", lambda e: e.matmul(b_rb.ap[:, :], onesrow_sb, row_r[0:1, :],
                                                  start=True, stop=True), TS)
            ctx.wait("ve", tmb)
            ctx.op("ve", lambda e: e.tensor_copy(mb_sb, b_mb.ap[:, :]))
            ctx.wait("ve", trb)
            mrtok = ctx.op("ve", lambda e: e.tensor_copy(rb_sb, b_rb.ap[:, :]), VS)
            b_mb.free = [mrtok]
            b_rb.free = [mrtok]
            rf["m"] = [vtok, tmb]
            rf["r"] = [trb]
            rf["t"] = [rtok]
            if gate:
                gbank = [fpool.acquire(ctx) for _ in range(4)]
            gtok = None
            for dd in range(NDD):
                sl = y_sl[dd % 4]
                ctx.waits("sy", sl.free)
                sl.free = []
                sl.ready = ctx.dma("sy", lambda dd=dd, tabs=tabs, sl=sl: dict(
                    out=sl.ap[:], in_=ysrc[dd * 128:(dd + 1) * 128, tabs:tabs + 512]),
                    sem=sl.sem)
                z = z32[dd % 2]
                ctx.wait("ve", sl.ready)
                ctx.wait("ve", mrtok)
                ctx.waits("ve", z.free)
                z.free = []
                ctx.waits("ve", zt_free)
                ctx.op("ve", lambda e, sl=sl: e.tensor_sub(ztmp, sl.ap[:], mb_sb))
                ctx.op("ve", lambda e, z=z: e.tensor_mul(z.ap[:], ztmp, rb_sb))
                ztok = ctx.op("ve", lambda e, z=z, dd=dd, tloc=tloc: e.tensor_copy(
                    zt_ap[:, dd * zt_stride + tloc:dd * zt_stride + tloc + 512],
                    z.ap[:]), VS)
                sl.free = [ztok]
                z.free = [ztok]
                zt_done = ztok
                if gate:
                    ctx.wait("te", ztok)
                    for g in range(4):
                        gtok = ctx.op("te", lambda e, g=g, dd=dd, z=z: e.matmul(
                            gbank[g].ap[:, 0:17], z.ap[:, g * 128:(g + 1) * 128],
                            gsg_sb[:, dd * 17:(dd + 1) * 17],
                            start=(dd == 0), stop=(dd == NDD - 1)),
                            TS if g == 3 else None)
                    z.free.append(gtok)
            if gate:
                emit_gate_finalize(ctx, st, fpool, gbank, gtok, tt * 512)
        st["zt_free"] = []
        return zt_done

    def emit_gate_finalize(ctx, st, fpool, gbank, gdone, tloc):
        gl, pp, pn, cwv, msk = st["gl"], st["pp"], st["pn"], st["cw"], st["msk"]
        col, col2 = st["col"], st["col2"]
        sg_cols, cwT = st["sg"], st["cwT"]
        vt3 = None
        for g in range(4):
            gb = gbank[g]
            ctx.wait("ve", gdone)
            ctx.vop(lambda e, gb=gb: e.tensor_copy(gl, gb.ap[:, 0:17]))
            ctx.vop(lambda e, g=g: e.tensor_copy(
                sg_cols[:, (tloc // 128) + g:(tloc // 128) + g + 1], gl[:, 16:17]))
            vt = ctx.vop(lambda e: e.reduce_max(col, gl[:, 0:16],
                                                axis=mybir.AxisListType.X))
            gb.free = [vt]
            ctx.wait("sc", vt)
            ntok = ctx.op("sc", lambda e: e.mul(col2, col, -1.0), SS)
            ctx.wait("sc", ntok)
            stok = ctx.op("sc", lambda e: e.activation(pp, gl[:, 0:16], AF.Exp,
                                                       bias=col2, accum_out=col), SS)
            ctx.wait("ve", stok)
            ctx.vop(lambda e: e.reciprocal(col2, col))
            ctx.vop(lambda e: e.tensor_scalar(pn, pp, col2, None, ALU.mult))
            cur = pn
            for rnd in range(3):
                ctx.vop(lambda e, cur=cur: e.reduce_max(col, cur,
                                                        axis=mybir.AxisListType.X))
                ctx.vop(lambda e, cur=cur: e.tensor_scalar(
                    msk, cur, col, None, ALU.is_ge))
                dst = pp if rnd % 2 == 0 else cwv
                ctx.vop(lambda e, cur=cur, dst=dst: e.scalar_tensor_tensor(
                    dst, msk, BIGNEG, cur, ALU.mult, ALU.add))
                cur = dst
            ctx.vop(lambda e, cur=cur: e.reduce_max(col, cur,
                                                    axis=mybir.AxisListType.X))
            ctx.vop(lambda e: e.tensor_scalar(msk, pn, col, None, ALU.is_ge))
            vt2 = ctx.vop(lambda e: e.tensor_mul(cwv, msk, pn))
            bT = fpool.acquire(ctx)
            ctx.wait("te", vt2)
            ttk = ctx.op("te", lambda e, bT=bT: e.transpose(bT.ap[0:16, 0:128], cwv,
                                                            identf_sb), TS)
            ctx.wait("ve", ttk)
            vt3 = ctx.op("ve", lambda e, g=g, tloc=tloc, bT=bT: e.tensor_copy(
                cwT[0:16, tloc + g * 128:tloc + (g + 1) * 128],
                bT.ap[0:16, 0:128]), VS)
            bT.free = [vt3]
        st["cwT_done"] = vt3

    def emit_sig(ctx, st, fpool):
        sg_cols, sig, sigrow, sigb = st["sg"], st["sig"], st["sigrow"], st["sigb"]
        ctx.wait("sc", st["cwT_done"])
        s1 = ctx.op("sc", lambda e: e.activation(sig, sg_cols, AF.Sigmoid), SS)
        rowtoks = []
        for h in range(2):
            bT = fpool.acquire(ctx)
            ctx.wait("te", s1)
            tts = None
            for g in range(4):
                gg = h * 4 + g
                tts = ctx.op("te", lambda e, bT=bT, g=g, gg=gg: e.transpose(
                    bT.ap[0:1, g * 128:(g + 1) * 128], sig[:, gg:gg + 1], identf_sb),
                    TS if g == 3 else None)
            ctx.wait("ve", tts)
            vt = ctx.op("ve", lambda e, bT=bT, h=h: e.tensor_copy(
                sigrow[0:1, h * 512:(h + 1) * 512], bT.ap[0:1, 0:512]), VS)
            bT.free = [vt]
            rowtoks.append(vt)
        sigb_tok = None
        for h in range(2):
            bb = fpool.acquire(ctx)
            ctx.wait("te", rowtoks[h])
            tb = ctx.op("te", lambda e, bb=bb, h=h: e.matmul(
                bb.ap[:, :], onesrow_sb, sigrow[0:1, h * 512:(h + 1) * 512],
                start=True, stop=True), TS)
            ctx.wait("ve", tb)
            sigb_tok = ctx.op("ve", lambda e, bb=bb, h=h: e.tensor_copy(
                sigb[:, h * 512:(h + 1) * 512], bb.ap[:, :]), VS)
            bb.free = [sigb_tok]
        st["sigb_done"] = sigb_tok

    def emit_unit(ctx, st, fpool, zt_ap, zt_done, w1src, w3src, w2src,
                  scale_ap, scale_tok, acc_init, he_state):
        W1, W3, W2S = st["W1"], st["W3"], st["W2S"]
        silu = st["sc2"]
        prod = st["ztmp"]
        he_ap = st["he"]
        acc_ap = st["acc"]
        ctx.waits("ve", he_state["free"])
        v_tok = None
        t_h3 = None
        for jc in range(NIC):
            s1 = W1[jc % 2]
            s3 = W3[jc % 2]
            ctx.waits("gp", s1.free)
            s1.free = []
            s1.ready = ctx.dma("gp", lambda s1=s1, jc=jc: dict(
                out=s1.ap, in_=w1src(jc)), sem=s1.sem)
            ctx.waits("gp", s3.free)
            s3.free = []
            s3.ready = ctx.dma("gp", lambda s3=s3, jc=jc: dict(
                out=s3.ap, in_=w3src(jc)), sem=s3.sem)
            bh1 = [fpool.acquire(ctx), fpool.acquire(ctx)]
            bh3 = [fpool.acquire(ctx), fpool.acquire(ctx)]
            ctx.wait("te", s1.ready)
            ctx.wait("te", s3.ready)
            ctx.wait("te", zt_done)
            t_h1 = None
            for dd in range(NDD):
                for th in range(2):
                    t_h1 = ctx.op("te", lambda e, dd=dd, th=th, s1=s1: e.matmul(
                        bh1[th].ap[:, :], s1.ap[:, dd * 128:(dd + 1) * 128],
                        zt_ap[:, dd * 1024 + th * 512:dd * 1024 + (th + 1) * 512],
                        start=(dd == 0), stop=(dd == NDD - 1)),
                        TS if (dd == NDD - 1 and th == 1) else None)
                for th in range(2):
                    t_h3 = ctx.op("te", lambda e, dd=dd, th=th, s3=s3: e.matmul(
                        bh3[th].ap[:, :], s3.ap[:, dd * 128:(dd + 1) * 128],
                        zt_ap[:, dd * 1024 + th * 512:dd * 1024 + (th + 1) * 512],
                        start=(dd == 0), stop=(dd == NDD - 1)),
                        TS if (dd == NDD - 1 and th == 1) else None)
            s1.free = [t_h1]
            s3.free = [t_h3]
            for th in range(2):
                sl = silu[th]
                ctx.wait("sc", t_h1)
                ctx.waits("sc", sl.free)
                sl.free = []
                s_tok = ctx.op("sc", lambda e, sl=sl, th=th: e.activation(
                    sl.ap[:], bh1[th].ap[:, :], AF.Silu), SS)
                bh1[th].free = [s_tok]
                ctx.wait("ve", s_tok)
                ctx.wait("ve", t_h3)
                ctx.wait("ve", scale_tok)
                ctx.op("ve", lambda e, sl=sl, th=th: e.tensor_mul(
                    prod, sl.ap[:], bh3[th].ap[:, :]))
                v_tok = ctx.op("ve", lambda e, jc=jc, th=th: e.tensor_mul(
                    he_ap[:, jc * 1024 + th * 512:jc * 1024 + (th + 1) * 512],
                    prod, scale_ap[:, th * 512:(th + 1) * 512]), VS)
                sl.free = [v_tok]
                bh3[th].free = [v_tok]
        he_ready = v_tok
        st["last_hidden"] = t_h3
        t_w2 = None
        a_tok = None
        for dd in range(NDD):
            s2 = W2S[dd % 2]
            ctx.waits("gp", s2.free)
            s2.free = []
            s2.ready = ctx.dma("gp", lambda s2=s2, dd=dd: dict(
                out=s2.ap, in_=w2src(dd)), sem=s2.sem)
            b0 = fpool.acquire(ctx)
            b1 = fpool.acquire(ctx)
            ctx.wait("te", s2.ready)
            ctx.wait("te", he_ready)
            for ic in range(NIC):
                for th, bb in ((0, b0), (1, b1)):
                    t_w2 = ctx.op("te", lambda e, ic=ic, th=th, bb=bb, s2=s2: e.matmul(
                        bb.ap[:, :], s2.ap[:, ic * 128:(ic + 1) * 128],
                        he_ap[:, ic * 1024 + th * 512:ic * 1024 + (th + 1) * 512],
                        start=(ic == 0), stop=(ic == NIC - 1)),
                        TS if (ic == NIC - 1 and th == 1) else None)
            s2.free = [t_w2]
            ctx.wait("ve", t_w2)
            for th, bb in ((0, b0), (1, b1)):
                dst = acc_ap[:, dd * 1024 + th * 512:dd * 1024 + (th + 1) * 512]
                if acc_init:
                    a_tok = ctx.op("ve", lambda e, dst=dst, bb=bb: e.tensor_copy(
                        dst, bb.ap[:, :]), VS)
                else:
                    a_tok = ctx.op("ve", lambda e, dst=dst, bb=bb: e.tensor_add(
                        dst, dst, bb.ap[:, :]), VS)
                bb.free = [a_tok]
        he_state["free"] = [t_w2]
        return a_tok

    def emit_cw_bcast(ctx, st, fpool, e_idx, slot):
        cwT = st["cwT"]
        cwrow = st["cwrow"]
        ctx.waits("sy", cwrow.free)
        cwrow.free = []
        ctx.wait("sy", st["cwT_done"])
        rtok = ctx.dma("sy", lambda e_idx=e_idx: dict(
            out=cwrow.ap[0:1, :], in_=cwT[e_idx:e_idx + 1, :]), sem=cwrow.sem)
        toks = []
        for h in range(2):
            bb = fpool.acquire(ctx)
            ctx.wait("te", rtok)
            tb = ctx.op("te", lambda e, bb=bb, h=h: e.matmul(
                bb.ap[:, :], onesrow_sb, cwrow.ap[0:1, h * 512:(h + 1) * 512],
                start=True, stop=True), TS)
            ctx.waits("ve", slot.free)
            slot.free = []
            ctx.wait("ve", tb)
            vt = ctx.op("ve", lambda e, bb=bb, h=h, slot=slot: e.tensor_copy(
                slot.ap[:, h * 512:(h + 1) * 512], bb.ap[:, :]), VS)
            bb.free = [vt]
            toks.append(vt)
        cwrow.free = [toks[1]]
        slot.ready = toks[1]

    def ffn_phase(blk, ysrc, ydst, final_mode):
        pes = ExitStack()

        def psb(name, shape, dtype):
            return pes.enter_context(nc.sbuf_tensor(name, shape, dtype)).ap()

        tag = "f1" if final_mode == "store" else "f2"
        zt_ap = psb(tag + "_zt", [128, NDD * 1024], BF16)
        acc_ap = psb(tag + "_acc", [128, NDD * 1024], F32)
        he_ap = psb(tag + "_he", [128, NIC * 1024], BF16)
        w1aps = [psb(tag + "_w1%d" % i, [128, NDD * 128], BF16) for i in range(2)]
        w3aps = [psb(tag + "_w3%d" % i, [128, NDD * 128], BF16) for i in range(2)]
        w2aps = [psb(tag + "_w2%d" % i, [128, NIC * 128], BF16) for i in range(2)]
        yaps = [psb(tag + "_y%d" % i, [128, 512], F32) for i in range(4)]
        sc2aps = [psb(tag + "_sc2%d" % i, [128, 512], F32) for i in range(2)]
        z32aps = [psb(tag + "_z32%d" % i, [128, 512], F32) for i in range(2)]
        ztmp = psb(tag + "_ztmp", [128, 512], F32)
        row_m = psb(tag + "_rowm", [1, 512], F32)
        row_r = psb(tag + "_rowr", [1, 512], F32)
        row_t = psb(tag + "_rowt", [1, 512], F32)
        mb = psb(tag + "_mb", [128, 512], F32)
        rb = psb(tag + "_rb", [128, 512], F32)
        gl = psb(tag + "_gl", [128, 17], F32)
        pp = psb(tag + "_pp", [128, 16], F32)
        pn = psb(tag + "_pn", [128, 16], F32)
        cwv = psb(tag + "_cwv", [128, 16], F32)
        msk = psb(tag + "_msk", [128, 16], F32)
        col = psb(tag + "_col", [128, 1], F32)
        col2 = psb(tag + "_col2", [128, 1], F32)
        sg = psb(tag + "_sg", [128, 8], F32)
        sig = psb(tag + "_sig", [128, 8], F32)
        sigrow = psb(tag + "_sigrow", [1, 1024], F32)
        sigb = psb(tag + "_sigb", [128, 1024], F32)
        cwT = psb(tag + "_cwT", [16, 1024], F32)
        cwrow_ap = psb(tag + "_cwrow", [1, 1024], F32)
        cwbaps = [psb(tag + "_cwb%d" % i, [128, 1024], F32) for i in range(2)]
        yo_aps = [psb(tag + "_yo%d" % i, [128, 512], F32) for i in range(2)]
        pc_aps = [psb(tag + "_pc%d" % i, [128, 128], F32) for i in range(2)]
        ysems = [B.new_sem(tag + "_sy%d" % i) for i in range(4)]
        w1sems = [B.new_sem(tag + "_sw1%d" % i) for i in range(2)]
        w3sems = [B.new_sem(tag + "_sw3%d" % i) for i in range(2)]
        w2sems = [B.new_sem(tag + "_sw2%d" % i) for i in range(2)]
        cwrsem = B.new_sem(tag + "_scwr")
        yosems = [B.new_sem(tag + "_syo%d" % i) for i in range(2)]
        pcsems = [B.new_sem(tag + "_spc%d" % i) for i in range(2)]

        def body(ctx):
            if ctx.mode == "gp":
                ctx.barrier([t for t in ctx.entry if t[0] is not CC])
            else:
                ctx.barrier(ctx.entry)
            fpool = Pool([Bank(a) for a in FBaps])
            st = {
                "y_sl": [Slot(a, sm) for a, sm in zip(yaps, ysems)],
                "sc2": [Slot(a) for a in sc2aps],
                "z32": [Slot(a) for a in z32aps],
                "ztmp": ztmp, "row_m": row_m, "row_r": row_r, "row_t": row_t,
                "mb": mb, "rb": rb, "gl": gl, "pp": pp, "pn": pn, "cw": cwv,
                "msk": msk, "col": col, "col2": col2, "sg": sg, "sig": sig,
                "sigrow": sigrow, "sigb": sigb, "cwT": cwT,
                "cwrow": Slot(cwrow_ap, cwrsem),
                "W1": [Slot(a, sm) for a, sm in zip(w1aps, w1sems)],
                "W3": [Slot(a, sm) for a, sm in zip(w3aps, w3sems)],
                "W2S": [Slot(a, sm) for a, sm in zip(w2aps, w2sems)],
                "he": he_ap, "acc": acc_ap,
            }
            cwb = [Slot(a) for a in cwbaps]
            yo = [Slot(a, sm) for a, sm in zip(yo_aps, yosems)]
            pc = [Slot(a, sm) for a, sm in zip(pc_aps, pcsems)]
            he_state = {"free": []}
            for s in range(2):
                if s > 0:
                    ctx.barrier([(x, x.n) for x in B.sems])
                t0 = s * ST
                zt_done = emit_ln(ctx, st, fpool, ysrc, t0, ST, zt_ap, 1024, 0,
                                  gate=True)
                emit_sig(ctx, st, fpool)
                a_tok = None
                for q in range(4):
                    a_tok = emit_unit(
                        ctx, st, fpool, zt_ap, zt_done,
                        w1src=lambda jc, q=q: sw1[q * NIC + jc],
                        w3src=lambda jc, q=q: sw3[q * NIC + jc],
                        w2src=lambda dd, q=q: sw2[q, dd],
                        scale_ap=sigb, scale_tok=st["sigb_done"],
                        acc_init=(q == 0), he_state=he_state)
                for ei in range(NE):
                    slot = cwb[ei % 2]
                    emit_cw_bcast(ctx, st, fpool, ei, slot)
                    a_tok = emit_unit(
                        ctx, st, fpool, zt_ap, zt_done,
                        w1src=lambda jc, ei=ei: w1[ei, jc],
                        w3src=lambda jc, ei=ei: w3[ei, jc],
                        w2src=lambda dd, ei=ei: w2[ei, dd],
                        scale_ap=slot.ap, scale_tok=slot.ready,
                        acc_init=False, he_state=he_state)
                    slot.free = [a_tok]
                acc_done = a_tok
                st["zt_free"] = [st["last_hidden"]]
                for dd in range(NDD):
                    for th in range(2):
                        tabs = t0 + th * 512
                        sl = st["y_sl"][(dd * 2 + th) % 4]
                        ctx.waits("sy", sl.free)
                        sl.free = []
                        sl.ready = ctx.dma("sy", lambda dd=dd, tabs=tabs, sl=sl: dict(
                            out=sl.ap[:], in_=ysrc[dd * 128:(dd + 1) * 128, tabs:tabs + 512]),
                            sem=sl.sem)
                        o = yo[(dd * 2 + th) % 2]
                        ctx.wait("ve", sl.ready)
                        ctx.wait("ve", acc_done)
                        ctx.waits("ve", o.free)
                        o.free = []
                        votok = ctx.op("ve", lambda e, o=o, dd=dd, th=th, sl=sl:
                                       e.scalar_tensor_tensor(
                                           o.ap[:],
                                           acc_ap[:, dd * 1024 + th * 512:dd * 1024 + (th + 1) * 512],
                                           0.5, sl.ap[:], ALU.mult, ALU.add), VS)
                        sl.free = [votok]
                        if final_mode == "store":
                            ctx.wait("sy", votok)
                            stok = ctx.dma("sy", lambda o=o, dd=dd, tabs=tabs: dict(
                                out=ydst[dd * 128:(dd + 1) * 128, tabs:tabs + 512],
                                in_=o.ap[:]), sem=o.sem)
                            o.free = [stok]
                        else:
                            lastt = votok
                            for sub in range(4):
                                bT = fpool.acquire(ctx)
                                ctx.wait("te", votok)
                                ttok = ctx.op("te", lambda e, bT=bT, o=o, sub=sub: e.transpose(
                                    bT.ap[:, 0:128], o.ap[:, sub * 128:(sub + 1) * 128],
                                    identf_sb), TS)
                                p = pc[(dd * 8 + th * 4 + sub) % 2]
                                ctx.waits("ve", p.free)
                                p.free = []
                                ctx.wait("ve", ttok)
                                pvt = ctx.op("ve", lambda e, p=p, bT=bT: e.tensor_copy(
                                    p.ap[:], bT.ap[:, 0:128]), VS)
                                bT.free = [pvt]
                                ctx.wait("sy", pvt)
                                pstok = ctx.dma("sy", lambda p=p, tabs=tabs, sub=sub, dd=dd: dict(
                                    out=out[tabs + sub * 128:tabs + (sub + 1) * 128,
                                            dd * 128:(dd + 1) * 128],
                                    in_=p.ap[:]), sem=p.sem)
                                p.free = [pstok]
                                lastt = ttok
                            o.free = [lastt]

        B.phase(blk, body)
        pes.close()

    def qkv_phase(blk):
        pes = ExitStack()

        def psb(name, shape, dtype):
            return pes.enter_context(nc.sbuf_tensor(name, shape, dtype)).ap()

        zt_ap = psb("qk_zt", [128, NDD * 2048], BF16)
        waps = [psb("qk_w%d" % i, [128, NDD * 128], BF16) for i in range(2)]
        qaps = [psb("qk_q%d" % i, [128, 2048], BF16) for i in range(2)]
        yaps = [psb("qk_y%d" % i, [128, 512], F32) for i in range(4)]
        sc2aps = [psb("qk_sc2%d" % i, [128, 512], F32) for i in range(2)]
        z32aps = [psb("qk_z32%d" % i, [128, 512], F32) for i in range(2)]
        ztmp = psb("qk_ztmp", [128, 512], F32)
        row_m = psb("qk_rowm", [1, 512], F32)
        row_r = psb("qk_rowr", [1, 512], F32)
        row_t = psb("qk_rowt", [1, 512], F32)
        mb = psb("qk_mb", [128, 512], F32)
        rb = psb("qk_rb", [128, 512], F32)
        qysems = [B.new_sem("qk_sy%d" % i) for i in range(4)]
        qwsems = [B.new_sem("qk_sw%d" % i) for i in range(2)]
        qssems = [B.new_sem("qk_sq%d" % i) for i in range(2)]

        def body(ctx):
            ctx.barrier(ctx.entry)
            fpool = Pool([Bank(a) for a in FBaps])
            st = {
                "y_sl": [Slot(a, sm) for a, sm in zip(yaps, qysems)],
                "sc2": [Slot(a) for a in sc2aps],
                "z32": [Slot(a) for a in z32aps],
                "ztmp": ztmp, "row_m": row_m, "row_r": row_r, "row_t": row_t,
                "mb": mb, "rb": rb,
            }
            zt_done = None
            for half in range(2):
                zt_done = emit_ln(ctx, st, fpool, y1H, half * 1024, 1024, zt_ap,
                                  2048, half * 1024, gate=False)
            W = [Slot(a, sm) for a, sm in zip(waps, qwsems)]
            Q = [Slot(a, sm) for a, sm in zip(qaps, qssems)]
            last_stores = {}
            for ec in range(NEC):
                s = W[ec % 2]
                ctx.waits("gp", s.free)
                s.free = []
                s.ready = ctx.dma("gp", lambda s=s, ec=ec: dict(
                    out=s.ap, in_=wattnT[ec]), sem=s.sem)
                banks = [fpool.acquire(ctx) for _ in range(4)]
                ctx.wait("te", s.ready)
                ctx.wait("te", zt_done)
                tq = None
                for dd in range(NDD):
                    for ti in range(4):
                        tq = ctx.op("te", lambda e, dd=dd, ti=ti, s=s: e.matmul(
                            banks[ti].ap[:, :], s.ap[:, dd * 128:(dd + 1) * 128],
                            zt_ap[:, dd * 2048 + ti * 512:dd * 2048 + (ti + 1) * 512],
                            start=(dd == 0), stop=(dd == NDD - 1)),
                            TS if (dd == NDD - 1 and ti == 3) else None)
                s.free = [tq]
                qs = Q[ec % 2]
                ctx.waits("sc", qs.free)
                qs.free = []
                ctx.wait("sc", tq)
                sq = None
                for ti in range(4):
                    sq = ctx.op("sc", lambda e, ti=ti, ec=ec, qs=qs: e.activation(
                        qs.ap[:, ti * 512:(ti + 1) * 512], banks[ti].ap[:, :],
                        AF.Identity, bias=battn_sb[:, ec:ec + 1]),
                        SS if ti == 3 else None)
                for ti in range(4):
                    banks[ti].free = [sq]
                ctx.wait("sy", sq)
                stok = ctx.dma("sy", lambda qs=qs, ec=ec: dict(
                    out=ginA[ec * 128:(ec + 1) * 128, :], in_=qs.ap[:]), sem=qs.sem)
                qs.free = [stok]
                last_stores[ec % 2] = stok
            for t in last_stores.values():
                ctx.wait("gp", t)
            ctx.op("gp", lambda e: e.collective_compute(
                "AllGather", ALU.bypass,
                replica_groups=[[0, 1, 2, 3], [4, 5, 6, 7]],
                ins=[ginA.opt()], outs=[goutA.opt()]), CC)

        B.phase(blk, body)
        pes.close()

    def attn_phase(blk):
        pes = ExitStack()

        def psb(name, shape, dtype):
            return pes.enter_context(nc.sbuf_tensor(name, shape, dtype)).ap()

        K_ap = psb("at_K", [128, S], BF16)
        Q_ap = psb("at_Q", [128, S], BF16)
        Vtm_ap = psb("at_V", [128, S], BF16)
        vf_aps = [psb("at_vf%d" % i, [128, 1024], BF16) for i in range(2)]
        scores_ap = psb("at_sco", [128, S], F32)
        probs_ap = psb("at_pr", [128, 6 * S], BF16)
        pT_aps = [psb("at_pT%d" % i, [128, 256], BF16) for i in range(2)]
        cx_aps = [psb("at_cx%d" % i, [128, 256], BF16) for i in range(2)]
        lrow = psb("at_lrow", [1, 3 * 256], F32)
        lcol = psb("at_lcol", [128, 6], F32)
        linvb_aps = [psb("at_linvb%d" % i, [128, 256], F32) for i in range(3)]
        mcol = psb("at_mcol", [128, 3], F32)
        ncol = psb("at_ncol", [128, 3], F32)
        scol = psb("at_scol", [128, 3], F32)
        kvqsems = [B.new_sem("at_skvq%d" % i) for i in range(2)]
        Ksem = B.new_sem("at_sK")
        Qsem = B.new_sem("at_sQ")
        vfsems = [B.new_sem("at_svf%d" % i) for i in range(2)]
        cxsems = [B.new_sem("at_scx%d" % i) for i in range(2)]

        g3 = goutA.rearrange("(r f) t -> r f t", r=4)
        g4 = goutA.rearrange("(r f) t -> f r t", r=4)

        def body(ctx):
            ctx.barrier(ctx.entry)
            cc_tok = (CC, CC.n)
            ctx.wait("sy", cc_tok)
            fpool = Pool([Bank(a) for a in FBaps])
            gpool = Pool([Bank(a) for a in GBaps])
            regs = {}

            def setup_regs(e):
                for idx in range(4):
                    regs[idx] = e.value_load(meta_sb[0:1, idx:idx + 1])
            ctx.run("sy", setup_regs)
            vf = [Slot(a, sm) for a, sm in zip(vf_aps, vfsems)]
            pT = [Slot(a) for a in pT_aps]
            cx = [Slot(a, sm) for a, sm in zip(cx_aps, cxsems)]
            K = Slot(K_ap, Ksem)
            Q = Slot(Q_ap, Qsem)
            Vtm = Slot(Vtm_ap)
            st_probs_free = [[[], []], [[], []], [[], []]]
            st_lrow_free = [[], [], []]
            st_lcol_free = [[], [], []]
            st_scores_free = [[]]
            last_cs = None

            hq_toks = [None, None]
            hq_toks[0] = ctx.dma("sy", lambda: dict(
                out=kvqH[0],
                in_=g3[:, bass.ds(regs[0], 384), :]), sem=kvqsems[0])
            for i in range(4):
                if i < 3:
                    ctx.wait("sy", K.ready)
                    ctx.wait("sy", Q.ready)
                    for sl in vf:
                        ctx.wait("sy", sl.ready)
                    hq_toks[(i + 1) % 2] = ctx.dma("sy", lambda i=i: dict(
                        out=kvqH[(i + 1) % 2],
                        in_=g3[:, bass.ds(regs[i + 1], 384), :]), sem=kvqsems[(i + 1) % 2])
                ctx.wait("sy", hq_toks[i % 2])
                ctx.waits("sy", K.free)
                K.free = []
                K.ready = ctx.dma("sy", lambda i=i: dict(
                    out=K_ap.rearrange("p (r t) -> p r t", r=4),
                    in_=kvqH[i % 2].rearrange("r f t -> f r t")[128:256, :, :]), sem=K.sem)
                ctx.waits("sy", Q.free)
                Q.free = []
                Q.ready = ctx.dma("sy", lambda i=i: dict(
                    out=Q_ap.rearrange("p (r t) -> p r t", r=4),
                    in_=kvqH[i % 2].rearrange("r f t -> f r t")[0:128, :, :]), sem=Q.sem)
                vt_tok = None
                ctx.waits("ve", Vtm.free)
                Vtm.free = []
                for c in range(8):
                    sl = vf[c % 2]
                    ctx.waits("sy", sl.free)
                    sl.free = []
                    sl.ready = ctx.dma("sy", lambda c=c, i=i, sl=sl: dict(
                        out=sl.ap[:],
                        in_=kvqH[i % 2, c // 2, 256:384,
                                 (c % 2) * 1024:(c % 2 + 1) * 1024]), sem=sl.sem)
                    gb = gpool.acquire(ctx)
                    ctx.wait("te", sl.ready)
                    tv = None
                    for j in range(8):
                        tv = ctx.op("te", lambda e, gb=gb, sl=sl, j=j: e.transpose(
                            gb.ap[:, j * 128:(j + 1) * 128],
                            sl.ap[:, j * 128:(j + 1) * 128], identbf_sb),
                            TS if j == 7 else None)
                    sl.free = [tv]
                    ctx.wait("ve", tv)
                    vt_tok = ctx.op("ve", lambda e, gb=gb, c=c: e.tensor_copy(
                        Vtm_ap[:, c * 1024:(c + 1) * 1024], gb.ap[:, :]), VS)
                    gb.free = [vt_tok]
                Vtm.ready = vt_tok

                exp_toks = [[None, None], [None, None], [None, None]]
                vB_toks = [None, None, None]
                probs_free = st_probs_free
                lrow_free = st_lrow_free
                lcol_free = st_lcol_free
                rc_toks = [[None, None], [None, None], [None, None]]

                def stage_scores(qb):
                    par = qb % 3
                    tsc = None
                    for a in range(2):
                        qa = qb * 256 + a * 128
                        nk = qa + 128
                        ev_tok = None
                        nkb = (nk + 511) // 512
                        ev_toks = []
                        for kb in range(nkb):
                            n = min(512, nk - kb * 512)
                            bsc = fpool.acquire(ctx)
                            ctx.wait("te", K.ready)
                            ctx.wait("te", Q.ready)
                            tsc = ctx.op("te", lambda e, bsc=bsc, qa=qa, kb=kb, n=n: e.matmul(
                                bsc.ap[:, 0:n], Q_ap[:, qa:qa + 128],
                                K_ap[:, kb * 512:kb * 512 + n],
                                start=True, stop=True), TS)
                            ev_eng = "sc" if (kb % 2 == 0 or ONLY_SC_EVAC) else "ve"
                            ev_sem = SS if (kb % 2 == 0 or ONLY_SC_EVAC) else VS
                            ctx.wait(ev_eng, tsc)
                            ctx.waits(ev_eng, st_scores_free[0])
                            def ev_fn(e, bsc=bsc, kb=kb, n=n):
                                if hasattr(e, "copy"):
                                    return e.copy(scores_ap[0:128, kb * 512:kb * 512 + n],
                                                  bsc.ap[:, 0:n])
                                return e.tensor_copy(scores_ap[0:128, kb * 512:kb * 512 + n],
                                                     bsc.ap[:, 0:n])
                            ev_tok = ctx.op(ev_eng, ev_fn, ev_sem)
                            bsc.free = [ev_tok]
                            ev_toks.append(ev_tok)
                        ctx.waits("ve", ev_toks)
                        ctx.vop(lambda e, qa=qa: e.tensor_add(
                            scores_ap[0:128, qa:qa + 128],
                            scores_ap[0:128, qa:qa + 128], cmask_sb))
                        mx_tok = ctx.vop(lambda e, nk=nk, par=par: e.reduce_max(
                            mcol[:, par:par + 1], scores_ap[0:128, 0:nk],
                            axis=mybir.AxisListType.X))
                        st_scores_free[0] = [mx_tok]
                        ctx.waits("sc", ev_toks)
                        ctx.wait("sc", mx_tok)
                        ctx.waits("sc", probs_free[par][a])
                        probs_free[par][a] = []
                        nstok = ctx.op("sc", lambda e, par=par: e.mul(
                            ncol[:, par:par + 1], mcol[:, par:par + 1], -SCALE), SS)
                        ctx.wait("sc", nstok)
                        ex_tok = ctx.op("sc", lambda e, a=a, nk=nk, par=par: e.activation(
                            probs_ap[:, (par * 2 + a) * S:(par * 2 + a) * S + nk],
                            scores_ap[0:128, 0:nk],
                            AF.Exp, bias=ncol[:, par:par + 1], scale=SCALE,
                            accum_out=scol[:, par:par + 1]), SS)
                        exp_toks[par][a] = ex_tok
                        st_scores_free[0] = [mx_tok, ex_tok]
                        ctx.wait("ve", ex_tok)
                        ctx.waits("ve", lcol_free[par])
                        lcol_free[par] = []
                        rc_toks[par][a] = ctx.vop(lambda e, a=a, par=par: e.reciprocal(
                            lcol[:, par * 2 + a:par * 2 + a + 1],
                            scol[:, par:par + 1]))
                    return tsc

                def stage_linv(qb):
                    par = qb % 3
                    vL = None
                    for a in range(2):
                        bL = fpool.acquire(ctx)
                        ctx.wait("te", rc_toks[par][a])
                        tL = ctx.op("te", lambda e, bL=bL, a=a, par=par: e.transpose(
                            bL.ap[0:1, a * 128:(a + 1) * 128],
                            lcol[:, par * 2 + a:par * 2 + a + 1],
                            identf_sb), TS)
                        lcol_free[par].append(tL)
                        ctx.waits("ve", lrow_free[par])
                        lrow_free[par] = []
                        ctx.wait("ve", tL)
                        vL = ctx.vop(lambda e, bL=bL, a=a, par=par: e.tensor_copy(
                            lrow[0:1, par * 256 + a * 128:par * 256 + (a + 1) * 128],
                            bL.ap[0:1, a * 128:(a + 1) * 128]))
                        bL.free = [vL]
                    bB = fpool.acquire(ctx)
                    ctx.wait("te", vL)
                    tB = ctx.op("te", lambda e, bB=bB, par=par: e.matmul(
                        bB.ap[:, 0:256], onesrow_sb,
                        lrow[0:1, par * 256:par * 256 + 256],
                        start=True, stop=True), TS)
                    lrow_free[par] = [tB]
                    ctx.wait("ve", tB)
                    vB = ctx.op("ve", lambda e, bB=bB, par=par: e.tensor_copy(
                        linvb_aps[par], bB.ap[:, 0:256]), VS)
                    bB.free = [vB]
                    vB_toks[par] = vB

                def stage_pv(qb):
                    par = qb % 3
                    bctx = fpool.acquire(ctx)
                    nkt = 2 * qb + 2
                    nkA = qb * 256 + 128
                    pv_tok = None
                    for kt in range(nkt):
                        bothv = kt * 128 < nkA
                        gb = gpool.acquire(ctx)
                        ctx.wait("te", exp_toks[par][1])
                        if bothv:
                            ctx.wait("te", exp_toks[par][0])
                            ctx.op("te", lambda e, gb=gb, kt=kt, par=par: e.transpose(
                                gb.ap[:, 0:128],
                                probs_ap[:, (par * 2) * S + kt * 128:(par * 2) * S + (kt + 1) * 128],
                                identbf_sb))
                        tT = ctx.op("te", lambda e, gb=gb, kt=kt, par=par: e.transpose(
                            gb.ap[:, 128:256],
                            probs_ap[:, (par * 2 + 1) * S + kt * 128:(par * 2 + 1) * S + (kt + 1) * 128],
                            identbf_sb), TS)
                        p = pT[kt % 2]
                        ctx.waits("ve", p.free)
                        p.free = []
                        ctx.wait("ve", tT)
                        lo = 0 if bothv else 128
                        vT = ctx.op("ve", lambda e, p=p, gb=gb, lo=lo: e.tensor_copy(
                            p.ap[:, lo:256], gb.ap[:, lo:256]), VS)
                        gb.free = [vT]
                        ctx.wait("te", vT)
                        ctx.wait("te", Vtm.ready)
                        pv_tok = ctx.op("te", lambda e, p=p, kt=kt, lo=lo, bctx=bctx,
                                        nkt=nkt: e.matmul(
                            bctx.ap[:, lo:256], Vtm_ap[:, kt * 128:(kt + 1) * 128],
                            p.ap[:, lo:256],
                            start=(kt == 0), stop=(kt == nkt - 1)), TS)
                        p.free = [pv_tok]
                    probs_free[par][0] = [pv_tok]
                    probs_free[par][1] = [pv_tok]
                    c = cx[qb % 2]
                    ctx.waits("ve", c.free)
                    c.free = []
                    ctx.wait("ve", pv_tok)
                    ctx.wait("ve", vB_toks[par])
                    vC = ctx.op("ve", lambda e, c=c, bctx=bctx, par=par: e.tensor_mul(
                        c.ap[:], bctx.ap[:, 0:256], linvb_aps[par]), VS)
                    bctx.free = [vC]
                    ctx.wait("sy", vC)
                    cs_tok = ctx.dma("sy", lambda c=c, i=i, qb=qb: dict(
                        out=ginB[i * 128:(i + 1) * 128, qb * 256:(qb + 1) * 256],
                        in_=c.ap[:]), sem=c.sem)
                    c.free = [cs_tok]
                    return cs_tok, pv_tok

                last_sc_tok = None
                for it in range(34):
                    if it < 32:
                        last_sc_tok = stage_scores(it)
                    if it >= 2:
                        stage_linv(it - 2)
                        cs_tok, pv_tok = stage_pv(it - 2)
                        last_cs = {**(last_cs or {}), (it - 2) % 2: cs_tok}
                K.free = [last_sc_tok]
                Q.free = [last_sc_tok]
                Vtm.free = [pv_tok]
            for t in last_cs.values():
                ctx.wait("gp", t)
            ctx.op("gp", lambda e: e.collective_compute(
                "AllGather", ALU.bypass,
                replica_groups=[[0, 1, 2, 3], [4, 5, 6, 7]],
                ins=[ginB.opt()], outs=[goutB.opt()]), CC)

        B.phase(blk, body)
        pes.close()

    def proj_phase(blk):
        pes = ExitStack()

        def psb(name, shape, dtype):
            return pes.enter_context(nc.sbuf_tensor(name, shape, dtype)).ap()

        ctxf_ap = psb("pj_ctx", [128, NDD * 2048], BF16)
        waps = [psb("pj_w%d" % i, [128, NDD * 128], BF16) for i in range(2)]
        po_aps = [psb("pj_po%d" % i, [128, 512], F32) for i in range(2)]
        yaps = [psb("pj_y%d" % i, [128, 512], F32) for i in range(4)]
        y2aps = [psb("pj_y2%d" % i, [128, 512], F32) for i in range(2)]
        pctxsem = B.new_sem("pj_sctxH")
        pcfsem = B.new_sem("pj_sctxf")
        pwsems = [B.new_sem("pj_sw%d" % i) for i in range(2)]
        pysems = [B.new_sem("pj_sy%d" % i) for i in range(4)]
        py2sems = [B.new_sem("pj_sy2%d" % i) for i in range(2)]

        def body(ctx):
            if ctx.mode == "gp":
                ctx.barrier([t for t in ctx.entry if t[0] is not CC])
            else:
                ctx.barrier(ctx.entry)
            cc_tok = (CC, CC.n)
            ctx.wait("sy", cc_tok)
            fpool = Pool([Bank(a) for a in FBaps])
            regs = {}

            def setup_regs(e):
                regs[0] = e.value_load(meta_sb[0:1, 12:13])
            ctx.run("sy", setup_regs)
            cpy_tok = ctx.dma("sy", lambda: dict(
                out=ctxH, in_=goutB[:, bass.ds(regs[0], 2048)]), sem=pctxsem)
            ctx.wait("sy", cpy_tok)
            ctxf_tok = None
            for hc in range(NDD):
                ctxf_tok = ctx.dma("sy", lambda hc=hc: dict(
                    out=ctxf_ap[:, hc * 2048:(hc + 1) * 2048],
                    in_=ctxH[hc * 128:(hc + 1) * 128, :]), sem=pcfsem)
            W = [Slot(a, sm) for a, sm in zip(waps, pwsems)]
            po = [Slot(a) for a in po_aps]
            ysl = [Slot(a, sm) for a, sm in zip(yaps, pysems)]
            y2 = [Slot(a, sm) for a, sm in zip(y2aps, py2sems)]
            for dd in range(NDD):
                s = W[dd % 2]
                ctx.waits("gp", s.free)
                s.free = []
                s.ready = ctx.dma("gp", lambda s=s, dd=dd: dict(
                    out=s.ap, in_=wprojT[dd]), sem=s.sem)
                banks = [fpool.acquire(ctx) for _ in range(4)]
                ctx.wait("te", s.ready)
                ctx.wait("te", ctxf_tok)
                tp = None
                for hc in range(NDD):
                    for ti in range(4):
                        tp = ctx.op("te", lambda e, hc=hc, ti=ti, s=s: e.matmul(
                            banks[ti].ap[:, :], s.ap[:, hc * 128:(hc + 1) * 128],
                            ctxf_ap[:, hc * 2048 + ti * 512:hc * 2048 + (ti + 1) * 512],
                            start=(hc == 0), stop=(hc == NDD - 1)),
                            TS if (hc == NDD - 1 and ti == 3) else None)
                s.free = [tp]
                for ti in range(4):
                    o = po[ti % 2]
                    ctx.waits("sc", o.free)
                    o.free = []
                    ctx.wait("sc", tp)
                    sp = ctx.op("sc", lambda e, o=o, ti=ti, dd=dd: e.activation(
                        o.ap[:], banks[ti].ap[:, :], AF.Identity,
                        bias=bproj_sb[:, dd:dd + 1]), SS)
                    banks[ti].free = [sp]
                    sl = ysl[ti]
                    ctx.waits("sy", sl.free)
                    sl.free = []
                    sl.ready = ctx.dma("sy", lambda sl=sl, dd=dd, ti=ti: dict(
                        out=sl.ap[:],
                        in_=y1H[dd * 128:(dd + 1) * 128, ti * 512:(ti + 1) * 512]),
                        sem=sl.sem)
                    o2 = y2[ti % 2]
                    ctx.waits("ve", o2.free)
                    o2.free = []
                    ctx.wait("ve", sp)
                    ctx.wait("ve", sl.ready)
                    v2 = ctx.op("ve", lambda e, o2=o2, o=o, sl=sl: e.tensor_add(
                        o2.ap[:], o.ap[:], sl.ap[:]), VS)
                    o.free = [v2]
                    sl.free = [v2]
                    ctx.wait("sy", v2)
                    s2 = ctx.dma("sy", lambda o2=o2, dd=dd, ti=ti: dict(
                        out=y2H[dd * 128:(dd + 1) * 128, ti * 512:(ti + 1) * 512],
                        in_=o2.ap[:]), sem=o2.sem)
                    o2.free = [s2]

        B.phase(blk, body)
        pes.close()

    with nc.Block() as blk:
        def init_body(ctx):
            for dst, src in [
                (identbf_sb, identbf), (identf_sb, identf),
                (onescol_sb, onescol), (onesrow_sb, onesrow),
                (cmask_sb, cmaskD), (battn_sb, battnC),
                (bproj_sb, bprojC), (meta_sb, meta),
            ]:
                ctx.dma("sy", lambda dst=dst, src=src: dict(out=dst, in_=src))
            ctx.dma("sy", lambda: dict(
                out=gsg_sb.rearrange("p (dd c) -> p dd c", dd=NDD),
                in_=gate_sg.rearrange("(dd p) c -> p dd c", p=128)))
            ctx.op("ve", lambda e: e.memset(eps_sb, EPS), VS)

        B.phase(blk, init_body)
        ffn_phase(blk, xT, y1H, "store")
        qkv_phase(blk)
        attn_phase(blk)
        proj_phase(blk)
        ffn_phase(blk, y2H, None, "out")

        def fin_body(ctx):
            ctx.barrier(ctx.entry)
            for dap, sap in dump_specs:
                ctx.dma("sy", lambda dap=dap, sap=sap: dict(out=dap, in_=sap))
            ctx.wait("sy", (B.SY, B.SY.n))

        B.phase(blk, fin_body)

    es.close()
    return nc


def _host_inputs(L0_attn_x, L0_attn_w_attn, L0_attn_b_attn, L0_attn_w_proj,
                 L0_attn_b_proj, L0_ffn_gate_w, L0_ffn_w1, L0_ffn_w3, L0_ffn_w2,
                 L0_ffn_s_w1, L0_ffn_s_w3, L0_ffn_s_w2, L0_ffn_sg_w):
    f = np.asarray
    x = f(L0_attn_x, dtype=np.float32)
    def slab_kd(w):
        K, M = w.shape
        return np.ascontiguousarray(
            w.reshape(K // 128, 128, M // 128, 128).transpose(2, 1, 0, 3)
            .reshape(M // 128, 128, K))

    w1f = f(L0_ffn_w1, dtype=np.float32)
    w3f = f(L0_ffn_w3, dtype=np.float32)
    w2f = f(L0_ffn_w2, dtype=np.float32)
    consts = dict(
        wattnT=slab_kd(f(L0_attn_w_attn, dtype=np.float32).T),
        battnC=np.ascontiguousarray(f(L0_attn_b_attn, dtype=np.float32).reshape(NEC, 128).T),
        wprojT=slab_kd(f(L0_attn_w_proj, dtype=np.float32).T),
        bprojC=np.ascontiguousarray(f(L0_attn_b_proj, dtype=np.float32).reshape(NDD, 128).T),
        gate_sg=np.ascontiguousarray(np.concatenate(
            [f(L0_ffn_gate_w, dtype=np.float32), f(L0_ffn_sg_w, dtype=np.float32)],
            axis=1)),
        w1=np.stack([slab_kd(w1f[e]) for e in range(NE)]),
        w3=np.stack([slab_kd(w3f[e]) for e in range(NE)]),
        w2=np.stack([slab_kd(w2f[e]) for e in range(NE)]),
        sw1=slab_kd(f(L0_ffn_s_w1, dtype=np.float32)),
        sw3=slab_kd(f(L0_ffn_s_w3, dtype=np.float32)),
        sw2=np.stack([slab_kd(f(L0_ffn_s_w2, dtype=np.float32)[q * IH:(q + 1) * IH, :])
                      for q in range(4)]),
        identbf=np.eye(128, dtype=BF),
        identf=np.eye(128, dtype=np.float32),
        onescol=np.ones((128, 1), np.float32),
        onesrow=np.ones((1, 128), np.float32),
        cmaskD=np.where(np.arange(128)[:, None] >= np.arange(128)[None, :],
                        np.float32(0.0), np.float32(BIGNEG)).astype(np.float32),
    )
    in_maps = []
    for core in range(8):
        b, q = core // 4, core % 4
        m = np.zeros((1, 16), np.int32)
        hb = q * 4 * 384
        for i in range(4):
            m[0, 0 + i] = hb + i * 384
        m[0, 12] = q * TC
        d = dict(consts)
        d["xT"] = np.ascontiguousarray(x[b, q * TC:(q + 1) * TC, :].T)
        d["meta"] = m
        in_maps.append(d)
    return in_maps


def kernel(**inputs):
    in_maps = _host_inputs(**inputs)
    nc = build_program()
    res = run_bass_kernel_spmd(nc, in_maps, core_ids=list(range(8)))
    y = np.empty((2, S, D), np.float32)
    for core in range(8):
        b, q = core // 4, core % 4
        y[b, q * TC:(q + 1) * TC, :] = res.results[core]["out"]
    return y


# version 12 (final)
# speedup vs baseline: 1.0686x; 1.0492x over previous
import numpy as np
import ml_dtypes
from contextlib import ExitStack

import concourse.bass as bass
import concourse.mybir as mybir
from concourse.bass_utils import run_bass_kernel_spmd

F32 = mybir.dt.float32
BF16 = mybir.dt.bfloat16
I32 = mybir.dt.int32
AF = mybir.ActivationFunctionType
ALU = mybir.AluOpType
BF = ml_dtypes.bfloat16

D = 2048
TC = 2048
ST = 1024
NDD = D // 128
NE = 16
IH = 1408
NIC = IH // 128
SHID = 5632
NJC = SHID // 128
QKVF = 3 * D
NEC = QKVF // 128
S = 8192
HD = 128
SCALE = 1.0 / float(np.sqrt(HD))
BIGNEG = -1.0e9
EPS = 1e-5

ENGS = ("sy", "gp", "te", "ve", "sc")
import os
ONLY_SC_EVAC = bool(os.environ.get("ONLY_SC_EVAC"))


class Sem:
    __slots__ = ("h", "n", "name")

    def __init__(self, es, nc, name):
        self.h = es.enter_context(nc.semaphore(name))
        self.n = 0
        self.name = name


class Eng:
    def __init__(self, name):
        self.name = name
        self.wm = {}
        self.e = None

    def wait(self, tok):
        if tok is None:
            return
        sem, n = tok
        if n <= 0:
            return
        if self.wm.get(sem.name, 0) < n:
            self.e.wait_ge(sem.h, n)
            self.wm[sem.name] = n


class Slot:
    __slots__ = ("ap", "free", "ready", "sem")

    def __init__(self, ap, sem=None):
        self.ap = ap
        self.free = []
        self.ready = None
        self.sem = sem


class Bank(Slot):
    pass


class Pool:
    def __init__(self, banks):
        self.banks = banks
        self.i = 0

    def acquire(self, ctx):
        b = self.banks[self.i % len(self.banks)]
        self.i += 1
        ctx.waits("te", b.free)
        b.free = []
        return b


class Ctx:
    def __init__(self, B, mode):
        self.B = B
        self.mode = mode
        self.entry = None

    def wait(self, eng, tok):
        if self.mode == eng:
            self.B.E[eng].wait(tok)

    def waits(self, eng, toks):
        for t in toks:
            self.wait(eng, t)

    def dma(self, q, thunk, sem=None):
        if sem is None:
            sem = self.B.SY if q == "sy" else self.B.SW
        if self.mode == q:
            kw = thunk()
            self.B.E[q].e.dma_start(**kw).then_inc(sem.h, 16)
        sem.n += 16
        return (sem, sem.n)

    def op(self, eng, fn, sem=None):
        instr = None
        if self.mode == eng:
            instr = fn(self.B.E[eng].e)
        if sem is not None:
            if instr is not None:
                instr.then_inc(sem.h, 1)
            sem.n += 1
            return (sem, sem.n)
        return None

    def run(self, eng, fn):
        if self.mode == eng:
            return fn(self.B.E[eng].e)
        return None

    def vop(self, fn):
        tok = self.op("ve", fn, self.B.VS)
        self.wait("ve", tok)
        return tok

    def barrier(self, toks):
        for t in toks:
            self.B.E[self.mode].wait(t)


class Builder:
    def __init__(self, nc, es):
        self.nc = nc
        self.es = es
        self.SY = Sem(es, nc, "SY")
        self.SW = Sem(es, nc, "SW")
        self.CC = Sem(es, nc, "CC")
        self.TS = Sem(es, nc, "TS")
        self.VS = Sem(es, nc, "VS")
        self.SS = Sem(es, nc, "SS")
        self.sems = [self.SY, self.SW, self.CC, self.TS, self.VS, self.SS]
        self.E = {n: Eng(n) for n in ENGS}

    def new_sem(self, name):
        s = Sem(self.es, self.nc, name)
        self.sems.append(s)
        return s

    def toks(self):
        return [(s, s.n) for s in self.sems]

    def phase(self, blk, body):
        entry = self.toks()
        snap = [s.n for s in self.sems]
        final = None
        for mode in ENGS:
            for s, n0 in zip(self.sems, snap):
                s.n = n0
            ctx = Ctx(self, mode)
            ctx.entry = entry

            def run_body(e, mode=mode, ctx=ctx):
                self.E[mode].e = e
                body(ctx)

            getattr_map = {"sy": blk.sync, "gp": blk.gpsimd, "te": blk.tensor,
                           "ve": blk.vector, "sc": blk.scalar}
            getattr_map[mode](run_body)
            end = [s.n for s in self.sems]
            if final is None:
                final = end
            else:
                assert final == end, ("phase replay divergence", final, end)
        for s, n in zip(self.sems, final):
            s.n = n


def build_program(dumps=()):
    nc = bass.Bass("TRN2", target_bir_lowering=False, debug=False)

    xT = nc.dram_tensor("xT", [D, TC], F32, kind="ExternalInput").ap()
    wattnT = nc.dram_tensor("wattnT", [NEC, 128, D], F32, kind="ExternalInput").ap()
    battnC = nc.dram_tensor("battnC", [128, NEC], F32, kind="ExternalInput").ap()
    wprojT = nc.dram_tensor("wprojT", [NDD, 128, D], F32, kind="ExternalInput").ap()
    bprojC = nc.dram_tensor("bprojC", [128, NDD], F32, kind="ExternalInput").ap()
    gate_sg = nc.dram_tensor("gate_sg", [D, 17], F32, kind="ExternalInput").ap()
    w1 = nc.dram_tensor("w1", [NE, NIC, 128, D], F32, kind="ExternalInput").ap()
    w3 = nc.dram_tensor("w3", [NE, NIC, 128, D], F32, kind="ExternalInput").ap()
    w2 = nc.dram_tensor("w2", [NE, NDD, 128, IH], F32, kind="ExternalInput").ap()
    sw1 = nc.dram_tensor("sw1", [NJC, 128, D], F32, kind="ExternalInput").ap()
    sw3 = nc.dram_tensor("sw3", [NJC, 128, D], F32, kind="ExternalInput").ap()
    sw2 = nc.dram_tensor("sw2", [4, NDD, 128, IH], F32, kind="ExternalInput").ap()
    identbf = nc.dram_tensor("identbf", [128, 128], BF16, kind="ExternalInput").ap()
    identf = nc.dram_tensor("identf", [128, 128], F32, kind="ExternalInput").ap()
    onescol = nc.dram_tensor("onescol", [128, 1], F32, kind="ExternalInput").ap()
    onesrow = nc.dram_tensor("onesrow", [1, 128], F32, kind="ExternalInput").ap()
    cmaskD = nc.dram_tensor("cmaskD", [128, 128], F32, kind="ExternalInput").ap()
    meta = nc.dram_tensor("meta", [1, 16], I32, kind="ExternalInput").ap()

    out = nc.dram_tensor("out", [TC, D], F32, kind="ExternalOutput").ap()

    y1H = nc.dram_tensor("y1H", [D, TC], F32, kind="Internal").ap()
    y2H = nc.dram_tensor("y2H", [D, TC], F32, kind="Internal").ap()
    ginA = nc.dram_tensor("ginA", [QKVF, TC], BF16, kind="Internal").ap()
    goutA = nc.dram_tensor("goutA", [4 * QKVF, TC], BF16, kind="Internal").ap()
    ginB = nc.dram_tensor("ginB", [4 * HD, S], BF16, kind="Internal").ap()
    goutB = nc.dram_tensor("goutB", [D, S], BF16, kind="Internal").ap()
    kvqH = nc.dram_tensor("kvqH", [2, 4, 384, TC], BF16, kind="Internal").ap()
    ctxH = nc.dram_tensor("ctxH", [D, TC], BF16, kind="Internal").ap()

    dump_specs = []
    srcs = {"y1H": y1H, "y2H": y2H, "ginA": ginA, "goutA": goutA,
            "ginB": ginB, "goutB": goutB, "kvqH": kvqH, "ctxH": ctxH}
    for name in dumps:
        sap = srcs[name]
        dap = nc.dram_tensor("dump_" + name, list(sap.tensor.shape),
                             sap.tensor.dtype, kind="ExternalOutput").ap()
        dump_specs.append((dap, sap))

    es = ExitStack()
    B = Builder(nc, es)
    SY, SW, CC, TS, VS, SS = B.SY, B.SW, B.CC, B.TS, B.VS, B.SS

    def sbuf(name, shape, dtype):
        return es.enter_context(nc.sbuf_tensor(name, shape, dtype)).ap()

    identbf_sb = sbuf("c_identbf", [128, 128], BF16)
    identf_sb = sbuf("c_identf", [128, 128], F32)
    onescol_sb = sbuf("c_onescol", [128, 1], F32)
    onesrow_sb = sbuf("c_onesrow", [1, 128], F32)
    cmask_sb = sbuf("c_cmask", [128, 128], F32)
    battn_sb = sbuf("c_battn", [128, NEC], F32)
    bproj_sb = sbuf("c_bproj", [128, NDD], F32)
    gsg_sb = sbuf("c_gsg", [128, NDD * 17], F32)
    meta_sb = sbuf("c_meta", [1, 16], I32)
    eps_sb = sbuf("c_eps", [1, 1], F32)

    FBaps = [es.enter_context(nc.psum_tensor("psf%d" % i, [128, 512], F32)).ap()
             for i in range(6)]
    GBaps = [es.enter_context(nc.psum_tensor("psg%d" % i, [128, 1024], BF16)).ap()
             for i in range(2)]

    def emit_ln(ctx, st, fpool, ysrc, tok0, ntok, zt_ap, zt_stride, zt_off, gate):
        y_sl = st["y_sl"]
        sc2 = st["sc2"]
        z32 = st["z32"]
        ztmp = st["ztmp"]
        row_m, row_r, row_t = st["row_m"], st["row_r"], st["row_t"]
        mb_sb, rb_sb = st["mb"], st["rb"]
        rf = st.setdefault("row_free", {"m": [], "r": [], "t": []})
        zt_free = st.get("zt_free", [])
        zt_done = None
        ntt = ntok // 512
        for tt in range(ntt):
            tabs = tok0 + tt * 512
            tloc = zt_off + tt * 512
            b_sum = fpool.acquire(ctx)
            b_sq = fpool.acquire(ctx)
            sum_tok = sumsq_tok = None
            for dd in range(NDD):
                sl = y_sl[dd % 4]
                ctx.waits("sy", sl.free)
                sl.free = []
                sl.ready = ctx.dma("sy", lambda dd=dd, tabs=tabs, sl=sl: dict(
                    out=sl.ap[:], in_=ysrc[dd * 128:(dd + 1) * 128, tabs:tabs + 512]),
                    sem=sl.sem)
                sq = sc2[dd % 2]
                ctx.wait("sc", sl.ready)
                ctx.waits("sc", sq.free)
                sq.free = []
                sq.ready = ctx.op("sc", lambda e, sq=sq, sl=sl: e.square(sq.ap[:], sl.ap[:]), SS)
                ctx.wait("te", sl.ready)
                t1 = ctx.op("te", lambda e, sl=sl, dd=dd, b=b_sum: e.matmul(
                    b.ap[0:1, :], onescol_sb, sl.ap[:],
                    start=(dd == 0), stop=(dd == NDD - 1)), TS)
                ctx.wait("te", sq.ready)
                t2 = ctx.op("te", lambda e, sq=sq, dd=dd, b=b_sq: e.matmul(
                    b.ap[0:1, :], onescol_sb, sq.ap[:],
                    start=(dd == 0), stop=(dd == NDD - 1)), TS)
                sl.free = [t1, sq.ready]
                sq.free = [t2]
                sum_tok, sumsq_tok = t1, t2
            ctx.wait("sc", sum_tok)
            ctx.waits("sc", rf["m"])
            rf["m"] = []
            mtok = ctx.op("sc", lambda e: e.mul(row_m[0:1, :], b_sum.ap[0:1, :], 1.0 / D), SS)
            ctx.wait("sc", sumsq_tok)
            ctx.waits("sc", rf["r"])
            rf["r"] = []
            qtok = ctx.op("sc", lambda e: e.mul(row_r[0:1, :], b_sq.ap[0:1, :], 1.0 / D), SS)
            b_sum.free = [mtok]
            b_sq.free = [qtok]
            ctx.wait("ve", qtok)
            ctx.waits("ve", rf["t"])
            rf["t"] = []
            ctx.vop(lambda e: e.tensor_mul(row_t[0:1, :], row_m[0:1, :], row_m[0:1, :]))
            vtok = ctx.vop(lambda e: e.tensor_sub(row_r[0:1, :], row_r[0:1, :],
                                                  row_t[0:1, :]))
            ctx.wait("sc", vtok)
            lntok = ctx.op("sc", lambda e: e.activation(row_t[0:1, :], row_r[0:1, :], AF.Ln,
                                                        bias=eps_sb[0:1, 0:1]), SS)
            ctx.wait("sc", lntok)
            rtok = ctx.op("sc", lambda e: e.activation(row_r[0:1, :], row_t[0:1, :], AF.Exp,
                                                       scale=-0.5), SS)
            b_mb = fpool.acquire(ctx)
            b_rb = fpool.acquire(ctx)
            ctx.wait("te", mtok)
            tmb = ctx.op("te", lambda e: e.matmul(b_mb.ap[:, :], onesrow_sb, row_m[0:1, :],
                                                  start=True, stop=True), TS)
            ctx.wait("te", rtok)
            trb = ctx.op("te", lambda e: e.matmul(b_rb.ap[:, :], onesrow_sb, row_r[0:1, :],
                                                  start=True, stop=True), TS)
            ctx.wait("ve", tmb)
            ctx.op("ve", lambda e: e.tensor_copy(mb_sb, b_mb.ap[:, :]))
            ctx.wait("ve", trb)
            mrtok = ctx.op("ve", lambda e: e.tensor_copy(rb_sb, b_rb.ap[:, :]), VS)
            b_mb.free = [mrtok]
            b_rb.free = [mrtok]
            rf["m"] = [vtok, tmb]
            rf["r"] = [trb]
            rf["t"] = [rtok]
            if gate:
                gbank = [fpool.acquire(ctx) for _ in range(4)]
            gtok = None
            for dd in range(NDD):
                sl = y_sl[dd % 4]
                ctx.waits("sy", sl.free)
                sl.free = []
                sl.ready = ctx.dma("sy", lambda dd=dd, tabs=tabs, sl=sl: dict(
                    out=sl.ap[:], in_=ysrc[dd * 128:(dd + 1) * 128, tabs:tabs + 512]),
                    sem=sl.sem)
                z = z32[dd % 2]
                ctx.wait("ve", sl.ready)
                ctx.wait("ve", mrtok)
                ctx.waits("ve", z.free)
                z.free = []
                ctx.waits("ve", zt_free)
                ctx.op("ve", lambda e, sl=sl: e.tensor_sub(ztmp, sl.ap[:], mb_sb))
                ctx.op("ve", lambda e, z=z: e.tensor_mul(z.ap[:], ztmp, rb_sb))
                ztok = ctx.op("ve", lambda e, z=z, dd=dd, tloc=tloc: e.tensor_copy(
                    zt_ap[:, dd * zt_stride + tloc:dd * zt_stride + tloc + 512],
                    z.ap[:]), VS)
                sl.free = [ztok]
                z.free = [ztok]
                zt_done = ztok
                if gate:
                    ctx.wait("te", ztok)
                    for g in range(4):
                        gtok = ctx.op("te", lambda e, g=g, dd=dd, z=z: e.matmul(
                            gbank[g].ap[:, 0:17], z.ap[:, g * 128:(g + 1) * 128],
                            gsg_sb[:, dd * 17:(dd + 1) * 17],
                            start=(dd == 0), stop=(dd == NDD - 1)),
                            TS if g == 3 else None)
                    z.free.append(gtok)
            if gate:
                emit_gate_finalize(ctx, st, fpool, gbank, gtok, tt * 512)
        st["zt_free"] = []
        return zt_done

    def emit_gate_finalize(ctx, st, fpool, gbank, gdone, tloc):
        gl, pp, pn, cwv, msk = st["gl"], st["pp"], st["pn"], st["cw"], st["msk"]
        col, col2 = st["col"], st["col2"]
        sg_cols, cwT = st["sg"], st["cwT"]
        vt3 = None
        for g in range(4):
            gb = gbank[g]
            ctx.wait("ve", gdone)
            ctx.vop(lambda e, gb=gb: e.tensor_copy(gl, gb.ap[:, 0:17]))
            ctx.vop(lambda e, g=g: e.tensor_copy(
                sg_cols[:, (tloc // 128) + g:(tloc // 128) + g + 1], gl[:, 16:17]))
            vt = ctx.vop(lambda e: e.reduce_max(col, gl[:, 0:16],
                                                axis=mybir.AxisListType.X))
            gb.free = [vt]
            ctx.wait("sc", vt)
            ntok = ctx.op("sc", lambda e: e.mul(col2, col, -1.0), SS)
            ctx.wait("sc", ntok)
            stok = ctx.op("sc", lambda e: e.activation(pp, gl[:, 0:16], AF.Exp,
                                                       bias=col2, accum_out=col), SS)
            ctx.wait("ve", stok)
            ctx.vop(lambda e: e.reciprocal(col2, col))
            ctx.vop(lambda e: e.tensor_scalar(pn, pp, col2, None, ALU.mult))
            cur = pn
            for rnd in range(3):
                ctx.vop(lambda e, cur=cur: e.reduce_max(col, cur,
                                                        axis=mybir.AxisListType.X))
                ctx.vop(lambda e, cur=cur: e.tensor_scalar(
                    msk, cur, col, None, ALU.is_ge))
                dst = pp if rnd % 2 == 0 else cwv
                ctx.vop(lambda e, cur=cur, dst=dst: e.scalar_tensor_tensor(
                    dst, msk, BIGNEG, cur, ALU.mult, ALU.add))
                cur = dst
            ctx.vop(lambda e, cur=cur: e.reduce_max(col, cur,
                                                    axis=mybir.AxisListType.X))
            ctx.vop(lambda e: e.tensor_scalar(msk, pn, col, None, ALU.is_ge))
            vt2 = ctx.vop(lambda e: e.tensor_mul(cwv, msk, pn))
            bT = fpool.acquire(ctx)
            ctx.wait("te", vt2)
            ttk = ctx.op("te", lambda e, bT=bT: e.transpose(bT.ap[0:16, 0:128], cwv,
                                                            identf_sb), TS)
            ctx.wait("ve", ttk)
            vt3 = ctx.op("ve", lambda e, g=g, tloc=tloc, bT=bT: e.tensor_copy(
                cwT[0:16, tloc + g * 128:tloc + (g + 1) * 128],
                bT.ap[0:16, 0:128]), VS)
            bT.free = [vt3]
        st["cwT_done"] = vt3

    def emit_sig(ctx, st, fpool):
        sg_cols, sig, sigrow, sigb = st["sg"], st["sig"], st["sigrow"], st["sigb"]
        ctx.wait("sc", st["cwT_done"])
        s1 = ctx.op("sc", lambda e: e.activation(sig, sg_cols, AF.Sigmoid), SS)
        rowtoks = []
        for h in range(2):
            bT = fpool.acquire(ctx)
            ctx.wait("te", s1)
            tts = None
            for g in range(4):
                gg = h * 4 + g
                tts = ctx.op("te", lambda e, bT=bT, g=g, gg=gg: e.transpose(
                    bT.ap[0:1, g * 128:(g + 1) * 128], sig[:, gg:gg + 1], identf_sb),
                    TS if g == 3 else None)
            ctx.wait("ve", tts)
            vt = ctx.op("ve", lambda e, bT=bT, h=h: e.tensor_copy(
                sigrow[0:1, h * 512:(h + 1) * 512], bT.ap[0:1, 0:512]), VS)
            bT.free = [vt]
            rowtoks.append(vt)
        sigb_tok = None
        for h in range(2):
            bb = fpool.acquire(ctx)
            ctx.wait("te", rowtoks[h])
            tb = ctx.op("te", lambda e, bb=bb, h=h: e.matmul(
                bb.ap[:, :], onesrow_sb, sigrow[0:1, h * 512:(h + 1) * 512],
                start=True, stop=True), TS)
            ctx.wait("ve", tb)
            sigb_tok = ctx.op("ve", lambda e, bb=bb, h=h: e.tensor_copy(
                sigb[:, h * 512:(h + 1) * 512], bb.ap[:, :]), VS)
            bb.free = [sigb_tok]
        st["sigb_done"] = sigb_tok

    def emit_unit(ctx, st, fpool, zt_ap, zt_done, w1src, w3src, w2src,
                  scale_ap, scale_tok, acc_init, he_state):
        W1, W3, W2S = st["W1"], st["W3"], st["W2S"]
        silu = st["sc2"]
        prod = st["ztmp"]
        he_ap = st["he"]
        acc_ap = st["acc"]
        ctx.waits("ve", he_state["free"])
        v_tok = None
        t_h3 = None
        for jc in range(NIC):
            s1 = W1[jc % 2]
            s3 = W3[jc % 2]
            ctx.waits("gp", s1.free)
            s1.free = []
            s1.ready = ctx.dma("gp", lambda s1=s1, jc=jc: dict(
                out=s1.ap, in_=w1src(jc)), sem=s1.sem)
            ctx.waits("gp", s3.free)
            s3.free = []
            s3.ready = ctx.dma("gp", lambda s3=s3, jc=jc: dict(
                out=s3.ap, in_=w3src(jc)), sem=s3.sem)
            bh1 = [fpool.acquire(ctx), fpool.acquire(ctx)]
            bh3 = [fpool.acquire(ctx), fpool.acquire(ctx)]
            ctx.wait("te", s1.ready)
            ctx.wait("te", s3.ready)
            ctx.wait("te", zt_done)
            t_h1 = None
            for dd in range(NDD):
                for th in range(2):
                    t_h1 = ctx.op("te", lambda e, dd=dd, th=th, s1=s1: e.matmul(
                        bh1[th].ap[:, :], s1.ap[:, dd * 128:(dd + 1) * 128],
                        zt_ap[:, dd * 1024 + th * 512:dd * 1024 + (th + 1) * 512],
                        start=(dd == 0), stop=(dd == NDD - 1)),
                        TS if (dd == NDD - 1 and th == 1) else None)
                for th in range(2):
                    t_h3 = ctx.op("te", lambda e, dd=dd, th=th, s3=s3: e.matmul(
                        bh3[th].ap[:, :], s3.ap[:, dd * 128:(dd + 1) * 128],
                        zt_ap[:, dd * 1024 + th * 512:dd * 1024 + (th + 1) * 512],
                        start=(dd == 0), stop=(dd == NDD - 1)),
                        TS if (dd == NDD - 1 and th == 1) else None)
            s1.free = [t_h1]
            s3.free = [t_h3]
            for th in range(2):
                sl = silu[th]
                ctx.wait("sc", t_h1)
                ctx.waits("sc", sl.free)
                sl.free = []
                s_tok = ctx.op("sc", lambda e, sl=sl, th=th: e.activation(
                    sl.ap[:], bh1[th].ap[:, :], AF.Silu), SS)
                bh1[th].free = [s_tok]
                ctx.wait("ve", s_tok)
                ctx.wait("ve", t_h3)
                ctx.wait("ve", scale_tok)
                ctx.op("ve", lambda e, sl=sl, th=th: e.tensor_mul(
                    prod, sl.ap[:], bh3[th].ap[:, :]))
                v_tok = ctx.op("ve", lambda e, jc=jc, th=th: e.tensor_mul(
                    he_ap[:, jc * 1024 + th * 512:jc * 1024 + (th + 1) * 512],
                    prod, scale_ap[:, th * 512:(th + 1) * 512]), VS)
                sl.free = [v_tok]
                bh3[th].free = [v_tok]
        he_ready = v_tok
        st["last_hidden"] = t_h3
        t_w2 = None
        a_tok = None
        for dd in range(NDD):
            s2 = W2S[dd % 2]
            ctx.waits("gp", s2.free)
            s2.free = []
            s2.ready = ctx.dma("gp", lambda s2=s2, dd=dd: dict(
                out=s2.ap, in_=w2src(dd)), sem=s2.sem)
            b0 = fpool.acquire(ctx)
            b1 = fpool.acquire(ctx)
            ctx.wait("te", s2.ready)
            ctx.wait("te", he_ready)
            for ic in range(NIC):
                for th, bb in ((0, b0), (1, b1)):
                    t_w2 = ctx.op("te", lambda e, ic=ic, th=th, bb=bb, s2=s2: e.matmul(
                        bb.ap[:, :], s2.ap[:, ic * 128:(ic + 1) * 128],
                        he_ap[:, ic * 1024 + th * 512:ic * 1024 + (th + 1) * 512],
                        start=(ic == 0), stop=(ic == NIC - 1)),
                        TS if (ic == NIC - 1 and th == 1) else None)
            s2.free = [t_w2]
            ctx.wait("ve", t_w2)
            for th, bb in ((0, b0), (1, b1)):
                dst = acc_ap[:, dd * 1024 + th * 512:dd * 1024 + (th + 1) * 512]
                if acc_init:
                    a_tok = ctx.op("ve", lambda e, dst=dst, bb=bb: e.tensor_copy(
                        dst, bb.ap[:, :]), VS)
                else:
                    a_tok = ctx.op("ve", lambda e, dst=dst, bb=bb: e.tensor_add(
                        dst, dst, bb.ap[:, :]), VS)
                bb.free = [a_tok]
        he_state["free"] = [t_w2]
        return a_tok

    def emit_cw_bcast(ctx, st, fpool, e_idx, slot):
        cwT = st["cwT"]
        cwrow = st["cwrow"]
        ctx.waits("sy", cwrow.free)
        cwrow.free = []
        ctx.wait("sy", st["cwT_done"])
        rtok = ctx.dma("sy", lambda e_idx=e_idx: dict(
            out=cwrow.ap[0:1, :], in_=cwT[e_idx:e_idx + 1, :]), sem=cwrow.sem)
        toks = []
        for h in range(2):
            bb = fpool.acquire(ctx)
            ctx.wait("te", rtok)
            tb = ctx.op("te", lambda e, bb=bb, h=h: e.matmul(
                bb.ap[:, :], onesrow_sb, cwrow.ap[0:1, h * 512:(h + 1) * 512],
                start=True, stop=True), TS)
            ctx.waits("ve", slot.free)
            slot.free = []
            ctx.wait("ve", tb)
            vt = ctx.op("ve", lambda e, bb=bb, h=h, slot=slot: e.tensor_copy(
                slot.ap[:, h * 512:(h + 1) * 512], bb.ap[:, :]), VS)
            bb.free = [vt]
            toks.append(vt)
        cwrow.free = [toks[1]]
        slot.ready = toks[1]

    def ffn_phase(blk, ysrc, ydst, final_mode):
        pes = ExitStack()

        def psb(name, shape, dtype):
            return pes.enter_context(nc.sbuf_tensor(name, shape, dtype)).ap()

        tag = "f1" if final_mode == "store" else "f2"
        zt_ap = psb(tag + "_zt", [128, NDD * 1024], BF16)
        acc_ap = psb(tag + "_acc", [128, NDD * 1024], F32)
        he_ap = psb(tag + "_he", [128, NIC * 1024], BF16)
        w1aps = [psb(tag + "_w1%d" % i, [128, NDD * 128], BF16) for i in range(2)]
        w3aps = [psb(tag + "_w3%d" % i, [128, NDD * 128], BF16) for i in range(2)]
        w2aps = [psb(tag + "_w2%d" % i, [128, NIC * 128], BF16) for i in range(2)]
        yaps = [psb(tag + "_y%d" % i, [128, 512], F32) for i in range(4)]
        sc2aps = [psb(tag + "_sc2%d" % i, [128, 512], F32) for i in range(2)]
        z32aps = [psb(tag + "_z32%d" % i, [128, 512], F32) for i in range(2)]
        ztmp = psb(tag + "_ztmp", [128, 512], F32)
        row_m = psb(tag + "_rowm", [1, 512], F32)
        row_r = psb(tag + "_rowr", [1, 512], F32)
        row_t = psb(tag + "_rowt", [1, 512], F32)
        mb = psb(tag + "_mb", [128, 512], F32)
        rb = psb(tag + "_rb", [128, 512], F32)
        gl = psb(tag + "_gl", [128, 17], F32)
        pp = psb(tag + "_pp", [128, 16], F32)
        pn = psb(tag + "_pn", [128, 16], F32)
        cwv = psb(tag + "_cwv", [128, 16], F32)
        msk = psb(tag + "_msk", [128, 16], F32)
        col = psb(tag + "_col", [128, 1], F32)
        col2 = psb(tag + "_col2", [128, 1], F32)
        sg = psb(tag + "_sg", [128, 8], F32)
        sig = psb(tag + "_sig", [128, 8], F32)
        sigrow = psb(tag + "_sigrow", [1, 1024], F32)
        sigb = psb(tag + "_sigb", [128, 1024], F32)
        cwT = psb(tag + "_cwT", [16, 1024], F32)
        cwrow_ap = psb(tag + "_cwrow", [1, 1024], F32)
        cwbaps = [psb(tag + "_cwb%d" % i, [128, 1024], F32) for i in range(2)]
        yo_aps = [psb(tag + "_yo%d" % i, [128, 512], F32) for i in range(2)]
        pc_aps = [psb(tag + "_pc%d" % i, [128, 128], F32) for i in range(2)]
        ysems = [B.new_sem(tag + "_sy%d" % i) for i in range(4)]
        w1sems = [B.new_sem(tag + "_sw1%d" % i) for i in range(2)]
        w3sems = [B.new_sem(tag + "_sw3%d" % i) for i in range(2)]
        w2sems = [B.new_sem(tag + "_sw2%d" % i) for i in range(2)]
        cwrsem = B.new_sem(tag + "_scwr")
        yosems = [B.new_sem(tag + "_syo%d" % i) for i in range(2)]
        pcsems = [B.new_sem(tag + "_spc%d" % i) for i in range(2)]

        def body(ctx):
            if ctx.mode == "gp":
                ctx.barrier([t for t in ctx.entry if t[0] is not CC])
            else:
                ctx.barrier(ctx.entry)
            fpool = Pool([Bank(a) for a in FBaps])
            st = {
                "y_sl": [Slot(a, sm) for a, sm in zip(yaps, ysems)],
                "sc2": [Slot(a) for a in sc2aps],
                "z32": [Slot(a) for a in z32aps],
                "ztmp": ztmp, "row_m": row_m, "row_r": row_r, "row_t": row_t,
                "mb": mb, "rb": rb, "gl": gl, "pp": pp, "pn": pn, "cw": cwv,
                "msk": msk, "col": col, "col2": col2, "sg": sg, "sig": sig,
                "sigrow": sigrow, "sigb": sigb, "cwT": cwT,
                "cwrow": Slot(cwrow_ap, cwrsem),
                "W1": [Slot(a, sm) for a, sm in zip(w1aps, w1sems)],
                "W3": [Slot(a, sm) for a, sm in zip(w3aps, w3sems)],
                "W2S": [Slot(a, sm) for a, sm in zip(w2aps, w2sems)],
                "he": he_ap, "acc": acc_ap,
            }
            cwb = [Slot(a) for a in cwbaps]
            yo = [Slot(a, sm) for a, sm in zip(yo_aps, yosems)]
            pc = [Slot(a, sm) for a, sm in zip(pc_aps, pcsems)]
            he_state = {"free": []}
            for s in range(2):
                if s > 0:
                    ctx.barrier([(x, x.n) for x in B.sems])
                t0 = s * ST
                zt_done = emit_ln(ctx, st, fpool, ysrc, t0, ST, zt_ap, 1024, 0,
                                  gate=True)
                emit_sig(ctx, st, fpool)
                a_tok = None
                for q in range(4):
                    a_tok = emit_unit(
                        ctx, st, fpool, zt_ap, zt_done,
                        w1src=lambda jc, q=q: sw1[q * NIC + jc],
                        w3src=lambda jc, q=q: sw3[q * NIC + jc],
                        w2src=lambda dd, q=q: sw2[q, dd],
                        scale_ap=sigb, scale_tok=st["sigb_done"],
                        acc_init=(q == 0), he_state=he_state)
                for ei in range(NE):
                    slot = cwb[ei % 2]
                    emit_cw_bcast(ctx, st, fpool, ei, slot)
                    a_tok = emit_unit(
                        ctx, st, fpool, zt_ap, zt_done,
                        w1src=lambda jc, ei=ei: w1[ei, jc],
                        w3src=lambda jc, ei=ei: w3[ei, jc],
                        w2src=lambda dd, ei=ei: w2[ei, dd],
                        scale_ap=slot.ap, scale_tok=slot.ready,
                        acc_init=False, he_state=he_state)
                    slot.free = [a_tok]
                acc_done = a_tok
                st["zt_free"] = [st["last_hidden"]]
                for dd in range(NDD):
                    for th in range(2):
                        tabs = t0 + th * 512
                        sl = st["y_sl"][(dd * 2 + th) % 4]
                        ctx.waits("sy", sl.free)
                        sl.free = []
                        sl.ready = ctx.dma("sy", lambda dd=dd, tabs=tabs, sl=sl: dict(
                            out=sl.ap[:], in_=ysrc[dd * 128:(dd + 1) * 128, tabs:tabs + 512]),
                            sem=sl.sem)
                        o = yo[(dd * 2 + th) % 2]
                        ctx.wait("ve", sl.ready)
                        ctx.wait("ve", acc_done)
                        ctx.waits("ve", o.free)
                        o.free = []
                        votok = ctx.op("ve", lambda e, o=o, dd=dd, th=th, sl=sl:
                                       e.scalar_tensor_tensor(
                                           o.ap[:],
                                           acc_ap[:, dd * 1024 + th * 512:dd * 1024 + (th + 1) * 512],
                                           0.5, sl.ap[:], ALU.mult, ALU.add), VS)
                        sl.free = [votok]
                        if final_mode == "store":
                            ctx.wait("sy", votok)
                            stok = ctx.dma("sy", lambda o=o, dd=dd, tabs=tabs: dict(
                                out=ydst[dd * 128:(dd + 1) * 128, tabs:tabs + 512],
                                in_=o.ap[:]), sem=o.sem)
                            o.free = [stok]
                        else:
                            lastt = votok
                            for sub in range(4):
                                bT = fpool.acquire(ctx)
                                ctx.wait("te", votok)
                                ttok = ctx.op("te", lambda e, bT=bT, o=o, sub=sub: e.transpose(
                                    bT.ap[:, 0:128], o.ap[:, sub * 128:(sub + 1) * 128],
                                    identf_sb), TS)
                                p = pc[(dd * 8 + th * 4 + sub) % 2]
                                ctx.waits("ve", p.free)
                                p.free = []
                                ctx.wait("ve", ttok)
                                pvt = ctx.op("ve", lambda e, p=p, bT=bT: e.tensor_copy(
                                    p.ap[:], bT.ap[:, 0:128]), VS)
                                bT.free = [pvt]
                                ctx.wait("sy", pvt)
                                pstok = ctx.dma("sy", lambda p=p, tabs=tabs, sub=sub, dd=dd: dict(
                                    out=out[tabs + sub * 128:tabs + (sub + 1) * 128,
                                            dd * 128:(dd + 1) * 128],
                                    in_=p.ap[:]), sem=p.sem)
                                p.free = [pstok]
                                lastt = ttok
                            o.free = [lastt]

        B.phase(blk, body)
        pes.close()

    def qkv_phase(blk):
        pes = ExitStack()

        def psb(name, shape, dtype):
            return pes.enter_context(nc.sbuf_tensor(name, shape, dtype)).ap()

        zt_ap = psb("qk_zt", [128, NDD * 2048], BF16)
        waps = [psb("qk_w%d" % i, [128, NDD * 128], BF16) for i in range(2)]
        qaps = [psb("qk_q%d" % i, [128, 2048], BF16) for i in range(2)]
        yaps = [psb("qk_y%d" % i, [128, 512], F32) for i in range(4)]
        sc2aps = [psb("qk_sc2%d" % i, [128, 512], F32) for i in range(2)]
        z32aps = [psb("qk_z32%d" % i, [128, 512], F32) for i in range(2)]
        ztmp = psb("qk_ztmp", [128, 512], F32)
        row_m = psb("qk_rowm", [1, 512], F32)
        row_r = psb("qk_rowr", [1, 512], F32)
        row_t = psb("qk_rowt", [1, 512], F32)
        mb = psb("qk_mb", [128, 512], F32)
        rb = psb("qk_rb", [128, 512], F32)
        qysems = [B.new_sem("qk_sy%d" % i) for i in range(4)]
        qwsems = [B.new_sem("qk_sw%d" % i) for i in range(2)]
        qssems = [B.new_sem("qk_sq%d" % i) for i in range(2)]

        def body(ctx):
            ctx.barrier(ctx.entry)
            fpool = Pool([Bank(a) for a in FBaps])
            st = {
                "y_sl": [Slot(a, sm) for a, sm in zip(yaps, qysems)],
                "sc2": [Slot(a) for a in sc2aps],
                "z32": [Slot(a) for a in z32aps],
                "ztmp": ztmp, "row_m": row_m, "row_r": row_r, "row_t": row_t,
                "mb": mb, "rb": rb,
            }
            zt_done = None
            for half in range(2):
                zt_done = emit_ln(ctx, st, fpool, y1H, half * 1024, 1024, zt_ap,
                                  2048, half * 1024, gate=False)
            W = [Slot(a, sm) for a, sm in zip(waps, qwsems)]
            Q = [Slot(a, sm) for a, sm in zip(qaps, qssems)]
            last_stores = {}
            for ec in range(NEC):
                s = W[ec % 2]
                ctx.waits("gp", s.free)
                s.free = []
                s.ready = ctx.dma("gp", lambda s=s, ec=ec: dict(
                    out=s.ap, in_=wattnT[ec]), sem=s.sem)
                banks = [fpool.acquire(ctx) for _ in range(4)]
                ctx.wait("te", s.ready)
                ctx.wait("te", zt_done)
                tq = None
                for dd in range(NDD):
                    for ti in range(4):
                        tq = ctx.op("te", lambda e, dd=dd, ti=ti, s=s: e.matmul(
                            banks[ti].ap[:, :], s.ap[:, dd * 128:(dd + 1) * 128],
                            zt_ap[:, dd * 2048 + ti * 512:dd * 2048 + (ti + 1) * 512],
                            start=(dd == 0), stop=(dd == NDD - 1)),
                            TS if (dd == NDD - 1 and ti == 3) else None)
                s.free = [tq]
                qs = Q[ec % 2]
                ctx.waits("sc", qs.free)
                qs.free = []
                ctx.wait("sc", tq)
                sq = None
                for ti in range(4):
                    sq = ctx.op("sc", lambda e, ti=ti, ec=ec, qs=qs: e.activation(
                        qs.ap[:, ti * 512:(ti + 1) * 512], banks[ti].ap[:, :],
                        AF.Identity, bias=battn_sb[:, ec:ec + 1]),
                        SS if ti == 3 else None)
                for ti in range(4):
                    banks[ti].free = [sq]
                ctx.wait("sy", sq)
                stok = ctx.dma("sy", lambda qs=qs, ec=ec: dict(
                    out=ginA[ec * 128:(ec + 1) * 128, :], in_=qs.ap[:]), sem=qs.sem)
                qs.free = [stok]
                last_stores[ec % 2] = stok
            for t in last_stores.values():
                ctx.wait("gp", t)
            ctx.op("gp", lambda e: e.collective_compute(
                "AllGather", ALU.bypass,
                replica_groups=[[0, 1, 2, 3], [4, 5, 6, 7]],
                ins=[ginA.opt()], outs=[goutA.opt()]), CC)

        B.phase(blk, body)
        pes.close()

    def attn_phase(blk):
        pes = ExitStack()

        def psb(name, shape, dtype):
            return pes.enter_context(nc.sbuf_tensor(name, shape, dtype)).ap()

        K_ap = psb("at_K", [128, S], BF16)
        Q_ap = psb("at_Q", [128, S], BF16)
        Vtm_ap = psb("at_V", [128, S], BF16)
        vf_aps = [psb("at_vf%d" % i, [128, 1024], BF16) for i in range(2)]
        scores_ap = psb("at_sco", [128, S], F32)
        probs_ap = psb("at_pr", [128, 6 * S], BF16)
        pT_aps = [psb("at_pT%d" % i, [128, 1024], BF16) for i in range(2)]
        cx_aps = [psb("at_cx%d" % i, [128, 256], BF16) for i in range(2)]
        lrow = psb("at_lrow", [1, 3 * 256], F32)
        lcol = psb("at_lcol", [128, 6], F32)
        linvb_aps = [psb("at_linvb%d" % i, [128, 256], F32) for i in range(3)]
        mcol = psb("at_mcol", [128, 3], F32)
        ncol = psb("at_ncol", [128, 3], F32)
        scol = psb("at_scol", [128, 3], F32)
        kvqsems = [B.new_sem("at_skvq%d" % i) for i in range(2)]
        Ksem = B.new_sem("at_sK")
        Qsem = B.new_sem("at_sQ")
        vfsems = [B.new_sem("at_svf%d" % i) for i in range(2)]
        cxsems = [B.new_sem("at_scx%d" % i) for i in range(2)]

        g3 = goutA.rearrange("(r f) t -> r f t", r=4)
        g4 = goutA.rearrange("(r f) t -> f r t", r=4)

        def body(ctx):
            ctx.barrier(ctx.entry)
            cc_tok = (CC, CC.n)
            ctx.wait("sy", cc_tok)
            fpool = Pool([Bank(a) for a in FBaps])
            gpool = Pool([Bank(a) for a in GBaps])
            regs = {}

            def setup_regs(e):
                for idx in range(4):
                    regs[idx] = e.value_load(meta_sb[0:1, idx:idx + 1])
            ctx.run("sy", setup_regs)
            vf = [Slot(a, sm) for a, sm in zip(vf_aps, vfsems)]
            pT = [Slot(a) for a in pT_aps]
            cx = [Slot(a, sm) for a, sm in zip(cx_aps, cxsems)]
            K = Slot(K_ap, Ksem)
            Q = Slot(Q_ap, Qsem)
            Vtm = Slot(Vtm_ap)
            st_probs_free = [[[], []], [[], []], [[], []]]
            st_lrow_free = [[], [], []]
            st_lcol_free = [[], [], []]
            st_scores_free = [[]]
            last_cs = None

            hq_toks = [None, None]
            hq_toks[0] = ctx.dma("sy", lambda: dict(
                out=kvqH[0],
                in_=g3[:, bass.ds(regs[0], 384), :]), sem=kvqsems[0])
            for i in range(4):
                if i < 3:
                    ctx.wait("sy", K.ready)
                    ctx.wait("sy", Q.ready)
                    for sl in vf:
                        ctx.wait("sy", sl.ready)
                    hq_toks[(i + 1) % 2] = ctx.dma("sy", lambda i=i: dict(
                        out=kvqH[(i + 1) % 2],
                        in_=g3[:, bass.ds(regs[i + 1], 384), :]), sem=kvqsems[(i + 1) % 2])
                ctx.wait("sy", hq_toks[i % 2])
                ctx.waits("sy", K.free)
                K.free = []
                K.ready = ctx.dma("sy", lambda i=i: dict(
                    out=K_ap.rearrange("p (r t) -> p r t", r=4),
                    in_=kvqH[i % 2].rearrange("r f t -> f r t")[128:256, :, :]), sem=K.sem)
                ctx.waits("sy", Q.free)
                Q.free = []
                Q.ready = ctx.dma("sy", lambda i=i: dict(
                    out=Q_ap.rearrange("p (r t) -> p r t", r=4),
                    in_=kvqH[i % 2].rearrange("r f t -> f r t")[0:128, :, :]), sem=Q.sem)
                vt_tok = None
                ctx.waits("ve", Vtm.free)
                Vtm.free = []
                for c in range(8):
                    sl = vf[c % 2]
                    ctx.waits("sy", sl.free)
                    sl.free = []
                    sl.ready = ctx.dma("sy", lambda c=c, i=i, sl=sl: dict(
                        out=sl.ap[:],
                        in_=kvqH[i % 2, c // 2, 256:384,
                                 (c % 2) * 1024:(c % 2 + 1) * 1024]), sem=sl.sem)
                    gb = gpool.acquire(ctx)
                    ctx.wait("te", sl.ready)
                    tv = None
                    for j in range(8):
                        tv = ctx.op("te", lambda e, gb=gb, sl=sl, j=j: e.transpose(
                            gb.ap[:, j * 128:(j + 1) * 128],
                            sl.ap[:, j * 128:(j + 1) * 128], identbf_sb),
                            TS if j == 7 else None)
                    sl.free = [tv]
                    ctx.wait("ve", tv)
                    vt_tok = ctx.op("ve", lambda e, gb=gb, c=c: e.tensor_copy(
                        Vtm_ap[:, c * 1024:(c + 1) * 1024], gb.ap[:, :]), VS)
                    gb.free = [vt_tok]
                Vtm.ready = vt_tok

                exp_toks = [[None, None], [None, None], [None, None]]
                vB_toks = [None, None, None]
                probs_free = st_probs_free
                lrow_free = st_lrow_free
                lcol_free = st_lcol_free
                rc_toks = [[None, None], [None, None], [None, None]]

                def stage_scores(qb):
                    par = qb % 3
                    tsc = None
                    for a in range(2):
                        qa = qb * 256 + a * 128
                        nk = qa + 128
                        ev_tok = None
                        nkb = (nk + 511) // 512
                        ev_toks = []
                        for kb in range(nkb):
                            n = min(512, nk - kb * 512)
                            bsc = fpool.acquire(ctx)
                            ctx.wait("te", K.ready)
                            ctx.wait("te", Q.ready)
                            tsc = ctx.op("te", lambda e, bsc=bsc, qa=qa, kb=kb, n=n: e.matmul(
                                bsc.ap[:, 0:n], Q_ap[:, qa:qa + 128],
                                K_ap[:, kb * 512:kb * 512 + n],
                                start=True, stop=True), TS)
                            ev_eng = "sc" if (kb % 2 == 0 or ONLY_SC_EVAC) else "ve"
                            ev_sem = SS if (kb % 2 == 0 or ONLY_SC_EVAC) else VS
                            ctx.wait(ev_eng, tsc)
                            ctx.waits(ev_eng, st_scores_free[0])
                            def ev_fn(e, bsc=bsc, kb=kb, n=n):
                                if hasattr(e, "copy"):
                                    return e.copy(scores_ap[0:128, kb * 512:kb * 512 + n],
                                                  bsc.ap[:, 0:n])
                                return e.tensor_copy(scores_ap[0:128, kb * 512:kb * 512 + n],
                                                     bsc.ap[:, 0:n])
                            ev_tok = ctx.op(ev_eng, ev_fn, ev_sem)
                            bsc.free = [ev_tok]
                            ev_toks.append(ev_tok)
                        ctx.waits("ve", ev_toks)
                        ctx.vop(lambda e, qa=qa: e.tensor_add(
                            scores_ap[0:128, qa:qa + 128],
                            scores_ap[0:128, qa:qa + 128], cmask_sb))
                        mx_tok = ctx.vop(lambda e, nk=nk, par=par: e.reduce_max(
                            mcol[:, par:par + 1], scores_ap[0:128, 0:nk],
                            axis=mybir.AxisListType.X))
                        st_scores_free[0] = [mx_tok]
                        ctx.waits("sc", ev_toks)
                        ctx.wait("sc", mx_tok)
                        ctx.waits("sc", probs_free[par][a])
                        probs_free[par][a] = []
                        nstok = ctx.op("sc", lambda e, par=par: e.mul(
                            ncol[:, par:par + 1], mcol[:, par:par + 1], -SCALE), SS)
                        ctx.wait("sc", nstok)
                        ex_tok = ctx.op("sc", lambda e, a=a, nk=nk, par=par: e.activation(
                            probs_ap[:, (par * 2 + a) * S:(par * 2 + a) * S + nk],
                            scores_ap[0:128, 0:nk],
                            AF.Exp, bias=ncol[:, par:par + 1], scale=SCALE,
                            accum_out=scol[:, par:par + 1]), SS)
                        exp_toks[par][a] = ex_tok
                        st_scores_free[0] = [mx_tok, ex_tok]
                        ctx.wait("ve", ex_tok)
                        ctx.waits("ve", lcol_free[par])
                        lcol_free[par] = []
                        rc_toks[par][a] = ctx.vop(lambda e, a=a, par=par: e.reciprocal(
                            lcol[:, par * 2 + a:par * 2 + a + 1],
                            scol[:, par:par + 1]))
                    return tsc

                def stage_linv(qb):
                    par = qb % 3
                    vL = None
                    for a in range(2):
                        bL = fpool.acquire(ctx)
                        ctx.wait("te", rc_toks[par][a])
                        tL = ctx.op("te", lambda e, bL=bL, a=a, par=par: e.transpose(
                            bL.ap[0:1, a * 128:(a + 1) * 128],
                            lcol[:, par * 2 + a:par * 2 + a + 1],
                            identf_sb), TS)
                        lcol_free[par].append(tL)
                        ctx.waits("ve", lrow_free[par])
                        lrow_free[par] = []
                        ctx.wait("ve", tL)
                        vL = ctx.vop(lambda e, bL=bL, a=a, par=par: e.tensor_copy(
                            lrow[0:1, par * 256 + a * 128:par * 256 + (a + 1) * 128],
                            bL.ap[0:1, a * 128:(a + 1) * 128]))
                        bL.free = [vL]
                    bB = fpool.acquire(ctx)
                    ctx.wait("te", vL)
                    tB = ctx.op("te", lambda e, bB=bB, par=par: e.matmul(
                        bB.ap[:, 0:256], onesrow_sb,
                        lrow[0:1, par * 256:par * 256 + 256],
                        start=True, stop=True), TS)
                    lrow_free[par] = [tB]
                    ctx.wait("ve", tB)
                    vB = ctx.op("ve", lambda e, bB=bB, par=par: e.tensor_copy(
                        linvb_aps[par], bB.ap[:, 0:256]), VS)
                    bB.free = [vB]
                    vB_toks[par] = vB

                def stage_pv(qb):
                    par = qb % 3
                    bctx = fpool.acquire(ctx)
                    nkt = 2 * qb + 2
                    nkA = qb * 256 + 128
                    pv_tok = None
                    for ktg in range(0, nkt, 4):
                        kts = list(range(ktg, min(ktg + 4, nkt)))
                        gb = gpool.acquire(ctx)
                        ctx.wait("te", exp_toks[par][1])
                        ctx.wait("te", exp_toks[par][0])
                        tT = None
                        for kt in kts:
                            co = (kt - ktg) * 256
                            if kt * 128 < nkA:
                                ctx.op("te", lambda e, gb=gb, kt=kt, par=par, co=co: e.transpose(
                                    gb.ap[:, co:co + 128],
                                    probs_ap[:, (par * 2) * S + kt * 128:(par * 2) * S + (kt + 1) * 128],
                                    identbf_sb))
                            tT = ctx.op("te", lambda e, gb=gb, kt=kt, par=par, co=co: e.transpose(
                                gb.ap[:, co + 128:co + 256],
                                probs_ap[:, (par * 2 + 1) * S + kt * 128:(par * 2 + 1) * S + (kt + 1) * 128],
                                identbf_sb), TS)
                        p = pT[(ktg // 4) % 2]
                        ctx.waits("ve", p.free)
                        p.free = []
                        ctx.wait("ve", tT)
                        vT = ctx.op("ve", lambda e, p=p, gb=gb: e.tensor_copy(
                            p.ap[:], gb.ap[:, :]), VS)
                        gb.free = [vT]
                        ctx.wait("te", vT)
                        ctx.wait("te", Vtm.ready)
                        for kt in kts:
                            co = (kt - ktg) * 256
                            lo = 0 if kt * 128 < nkA else 128
                            pv_tok = ctx.op("te", lambda e, p=p, kt=kt, lo=lo, co=co,
                                            bctx=bctx, nkt=nkt: e.matmul(
                                bctx.ap[:, lo:256], Vtm_ap[:, kt * 128:(kt + 1) * 128],
                                p.ap[:, co + lo:co + 256],
                                start=(kt == 0), stop=(kt == nkt - 1)), TS)
                        p.free = [pv_tok]
                    probs_free[par][0] = [pv_tok]
                    probs_free[par][1] = [pv_tok]
                    c = cx[qb % 2]
                    ctx.waits("ve", c.free)
                    c.free = []
                    ctx.wait("ve", pv_tok)
                    ctx.wait("ve", vB_toks[par])
                    vC = ctx.op("ve", lambda e, c=c, bctx=bctx, par=par: e.tensor_mul(
                        c.ap[:], bctx.ap[:, 0:256], linvb_aps[par]), VS)
                    bctx.free = [vC]
                    ctx.wait("sy", vC)
                    cs_tok = ctx.dma("sy", lambda c=c, i=i, qb=qb: dict(
                        out=ginB[i * 128:(i + 1) * 128, qb * 256:(qb + 1) * 256],
                        in_=c.ap[:]), sem=c.sem)
                    c.free = [cs_tok]
                    return cs_tok, pv_tok

                last_sc_tok = None
                for it in range(34):
                    if it < 32:
                        last_sc_tok = stage_scores(it)
                    if it >= 2:
                        stage_linv(it - 2)
                        cs_tok, pv_tok = stage_pv(it - 2)
                        last_cs = {**(last_cs or {}), (it - 2) % 2: cs_tok}
                K.free = [last_sc_tok]
                Q.free = [last_sc_tok]
                Vtm.free = [pv_tok]
            for t in last_cs.values():
                ctx.wait("gp", t)
            ctx.op("gp", lambda e: e.collective_compute(
                "AllGather", ALU.bypass,
                replica_groups=[[0, 1, 2, 3], [4, 5, 6, 7]],
                ins=[ginB.opt()], outs=[goutB.opt()]), CC)

        B.phase(blk, body)
        pes.close()

    def proj_phase(blk):
        pes = ExitStack()

        def psb(name, shape, dtype):
            return pes.enter_context(nc.sbuf_tensor(name, shape, dtype)).ap()

        ctxf_ap = psb("pj_ctx", [128, NDD * 2048], BF16)
        waps = [psb("pj_w%d" % i, [128, NDD * 128], BF16) for i in range(2)]
        po_aps = [psb("pj_po%d" % i, [128, 512], F32) for i in range(2)]
        yaps = [psb("pj_y%d" % i, [128, 512], F32) for i in range(4)]
        y2aps = [psb("pj_y2%d" % i, [128, 512], F32) for i in range(2)]
        pctxsem = B.new_sem("pj_sctxH")
        pcfsem = B.new_sem("pj_sctxf")
        pwsems = [B.new_sem("pj_sw%d" % i) for i in range(2)]
        pysems = [B.new_sem("pj_sy%d" % i) for i in range(4)]
        py2sems = [B.new_sem("pj_sy2%d" % i) for i in range(2)]

        def body(ctx):
            if ctx.mode == "gp":
                ctx.barrier([t for t in ctx.entry if t[0] is not CC])
            else:
                ctx.barrier(ctx.entry)
            cc_tok = (CC, CC.n)
            ctx.wait("sy", cc_tok)
            fpool = Pool([Bank(a) for a in FBaps])
            regs = {}

            def setup_regs(e):
                regs[0] = e.value_load(meta_sb[0:1, 12:13])
            ctx.run("sy", setup_regs)
            cpy_tok = ctx.dma("sy", lambda: dict(
                out=ctxH, in_=goutB[:, bass.ds(regs[0], 2048)]), sem=pctxsem)
            ctx.wait("sy", cpy_tok)
            ctxf_tok = None
            for hc in range(NDD):
                ctxf_tok = ctx.dma("sy", lambda hc=hc: dict(
                    out=ctxf_ap[:, hc * 2048:(hc + 1) * 2048],
                    in_=ctxH[hc * 128:(hc + 1) * 128, :]), sem=pcfsem)
            W = [Slot(a, sm) for a, sm in zip(waps, pwsems)]
            po = [Slot(a) for a in po_aps]
            ysl = [Slot(a, sm) for a, sm in zip(yaps, pysems)]
            y2 = [Slot(a, sm) for a, sm in zip(y2aps, py2sems)]
            for dd in range(NDD):
                s = W[dd % 2]
                ctx.waits("gp", s.free)
                s.free = []
                s.ready = ctx.dma("gp", lambda s=s, dd=dd: dict(
                    out=s.ap, in_=wprojT[dd]), sem=s.sem)
                banks = [fpool.acquire(ctx) for _ in range(4)]
                ctx.wait("te", s.ready)
                ctx.wait("te", ctxf_tok)
                tp = None
                for hc in range(NDD):
                    for ti in range(4):
                        tp = ctx.op("te", lambda e, hc=hc, ti=ti, s=s: e.matmul(
                            banks[ti].ap[:, :], s.ap[:, hc * 128:(hc + 1) * 128],
                            ctxf_ap[:, hc * 2048 + ti * 512:hc * 2048 + (ti + 1) * 512],
                            start=(hc == 0), stop=(hc == NDD - 1)),
                            TS if (hc == NDD - 1 and ti == 3) else None)
                s.free = [tp]
                for ti in range(4):
                    o = po[ti % 2]
                    ctx.waits("sc", o.free)
                    o.free = []
                    ctx.wait("sc", tp)
                    sp = ctx.op("sc", lambda e, o=o, ti=ti, dd=dd: e.activation(
                        o.ap[:], banks[ti].ap[:, :], AF.Identity,
                        bias=bproj_sb[:, dd:dd + 1]), SS)
                    banks[ti].free = [sp]
                    sl = ysl[ti]
                    ctx.waits("sy", sl.free)
                    sl.free = []
                    sl.ready = ctx.dma("sy", lambda sl=sl, dd=dd, ti=ti: dict(
                        out=sl.ap[:],
                        in_=y1H[dd * 128:(dd + 1) * 128, ti * 512:(ti + 1) * 512]),
                        sem=sl.sem)
                    o2 = y2[ti % 2]
                    ctx.waits("ve", o2.free)
                    o2.free = []
                    ctx.wait("ve", sp)
                    ctx.wait("ve", sl.ready)
                    v2 = ctx.op("ve", lambda e, o2=o2, o=o, sl=sl: e.tensor_add(
                        o2.ap[:], o.ap[:], sl.ap[:]), VS)
                    o.free = [v2]
                    sl.free = [v2]
                    ctx.wait("sy", v2)
                    s2 = ctx.dma("sy", lambda o2=o2, dd=dd, ti=ti: dict(
                        out=y2H[dd * 128:(dd + 1) * 128, ti * 512:(ti + 1) * 512],
                        in_=o2.ap[:]), sem=o2.sem)
                    o2.free = [s2]

        B.phase(blk, body)
        pes.close()

    with nc.Block() as blk:
        def init_body(ctx):
            for dst, src in [
                (identbf_sb, identbf), (identf_sb, identf),
                (onescol_sb, onescol), (onesrow_sb, onesrow),
                (cmask_sb, cmaskD), (battn_sb, battnC),
                (bproj_sb, bprojC), (meta_sb, meta),
            ]:
                ctx.dma("sy", lambda dst=dst, src=src: dict(out=dst, in_=src))
            ctx.dma("sy", lambda: dict(
                out=gsg_sb.rearrange("p (dd c) -> p dd c", dd=NDD),
                in_=gate_sg.rearrange("(dd p) c -> p dd c", p=128)))
            ctx.op("ve", lambda e: e.memset(eps_sb, EPS), VS)

        B.phase(blk, init_body)
        ffn_phase(blk, xT, y1H, "store")
        qkv_phase(blk)
        attn_phase(blk)
        proj_phase(blk)
        ffn_phase(blk, y2H, None, "out")

        def fin_body(ctx):
            ctx.barrier(ctx.entry)
            for dap, sap in dump_specs:
                ctx.dma("sy", lambda dap=dap, sap=sap: dict(out=dap, in_=sap))
            ctx.wait("sy", (B.SY, B.SY.n))

        B.phase(blk, fin_body)

    es.close()
    return nc


def _host_inputs(L0_attn_x, L0_attn_w_attn, L0_attn_b_attn, L0_attn_w_proj,
                 L0_attn_b_proj, L0_ffn_gate_w, L0_ffn_w1, L0_ffn_w3, L0_ffn_w2,
                 L0_ffn_s_w1, L0_ffn_s_w3, L0_ffn_s_w2, L0_ffn_sg_w):
    f = np.asarray
    x = f(L0_attn_x, dtype=np.float32)
    def slab_kd(w):
        K, M = w.shape
        return np.ascontiguousarray(
            w.reshape(K // 128, 128, M // 128, 128).transpose(2, 1, 0, 3)
            .reshape(M // 128, 128, K))

    w1f = f(L0_ffn_w1, dtype=np.float32)
    w3f = f(L0_ffn_w3, dtype=np.float32)
    w2f = f(L0_ffn_w2, dtype=np.float32)
    consts = dict(
        wattnT=slab_kd(f(L0_attn_w_attn, dtype=np.float32).T),
        battnC=np.ascontiguousarray(f(L0_attn_b_attn, dtype=np.float32).reshape(NEC, 128).T),
        wprojT=slab_kd(f(L0_attn_w_proj, dtype=np.float32).T),
        bprojC=np.ascontiguousarray(f(L0_attn_b_proj, dtype=np.float32).reshape(NDD, 128).T),
        gate_sg=np.ascontiguousarray(np.concatenate(
            [f(L0_ffn_gate_w, dtype=np.float32), f(L0_ffn_sg_w, dtype=np.float32)],
            axis=1)),
        w1=np.stack([slab_kd(w1f[e]) for e in range(NE)]),
        w3=np.stack([slab_kd(w3f[e]) for e in range(NE)]),
        w2=np.stack([slab_kd(w2f[e]) for e in range(NE)]),
        sw1=slab_kd(f(L0_ffn_s_w1, dtype=np.float32)),
        sw3=slab_kd(f(L0_ffn_s_w3, dtype=np.float32)),
        sw2=np.stack([slab_kd(f(L0_ffn_s_w2, dtype=np.float32)[q * IH:(q + 1) * IH, :])
                      for q in range(4)]),
        identbf=np.eye(128, dtype=BF),
        identf=np.eye(128, dtype=np.float32),
        onescol=np.ones((128, 1), np.float32),
        onesrow=np.ones((1, 128), np.float32),
        cmaskD=np.where(np.arange(128)[:, None] >= np.arange(128)[None, :],
                        np.float32(0.0), np.float32(BIGNEG)).astype(np.float32),
    )
    in_maps = []
    for core in range(8):
        b, q = core // 4, core % 4
        m = np.zeros((1, 16), np.int32)
        hb = q * 4 * 384
        for i in range(4):
            m[0, 0 + i] = hb + i * 384
        m[0, 12] = q * TC
        d = dict(consts)
        d["xT"] = np.ascontiguousarray(x[b, q * TC:(q + 1) * TC, :].T)
        d["meta"] = m
        in_maps.append(d)
    return in_maps


def kernel(**inputs):
    in_maps = _host_inputs(**inputs)
    nc = build_program()
    res = run_bass_kernel_spmd(nc, in_maps, core_ids=list(range(8)))
    y = np.empty((2, S, D), np.float32)
    for core in range(8):
        b, q = core // 4, core % 4
        y[b, q * TC:(q + 1) * TC, :] = res.results[core]["out"]
    return y
